# version 18
# baseline (speedup 1.0000x reference)
import contextlib
import numpy as np
import concourse.bass as bass
import concourse.mybir as mybir
from concourse.bass_utils import run_bass_kernel_spmd

F32 = mybir.dt.float32
BF16 = mybir.dt.bfloat16
I32 = mybir.dt.int32
U8 = mybir.dt.uint8
AF = mybir.ActivationFunctionType
ALU = mybir.AluOpType
AX = mybir.AxisListType

T = 2048
D = 1024
NT = T // 128
NF = D // 128
NH = 16
HD = 64
LN_EPS = 1e-5

ENGS = ("pe", "act", "dve", "sp", "gp")


class Op:
    __slots__ = ("idx", "eng", "fn", "deps", "is_dma", "inc", "val")

    def __init__(self, idx, eng, fn, deps, is_dma):
        self.idx = idx
        self.eng = eng
        self.fn = fn
        self.deps = deps
        self.is_dma = is_dma
        self.inc = is_dma
        self.val = 0


class Rgn:
    __slots__ = ("w", "rs")

    def __init__(self):
        self.w = None
        self.rs = []


def _pref(k):
    return k if isinstance(k, str) else k[0]


class Prog:

    def __init__(self, nc):
        self.nc = nc
        self.ops = []
        self.rgn = {}
        self.alias_deps = {}

    def R(self, key):
        if key not in self.rgn:
            self.rgn[key] = Rgn()
        return self.rgn[key]

    def keys_of(self, bufname):
        out = set()
        for k, r in self.rgn.items():
            if _pref(k) == bufname:
                if r.w is not None:
                    out.add(r.w)
                out.update(r.rs)
        return out

    def _emit(self, eng, fn, reads, writes, extra, is_dma):
        deps = set(extra)
        for k in writes:
            ad = self.alias_deps.get(_pref(k))
            if ad:
                deps.update(ad)
        for k in reads:
            r = self.R(k)
            if r.w is not None:
                deps.add(r.w)
        for k in writes:
            r = self.R(k)
            if r.w is not None:
                deps.add(r.w)
            deps.update(r.rs)
        op = Op(len(self.ops), eng, fn, sorted(deps), is_dma)
        self.ops.append(op)
        for k in reads:
            self.R(k).rs.append(op.idx)
        for k in writes:
            r = self.R(k)
            r.w = op.idx
            r.rs = []
        return op.idx

    def op(self, eng, fn, reads=(), writes=(), extra=()):
        return self._emit(eng, fn, reads, writes, extra, False)

    def dma(self, q, dst, src, reads=(), writes=(), extra=(), **kw):
        fn = (lambda e, d=dst, s=src, kw=kw: e.dma_start(out=d, in_=s, **kw))
        return self._emit(q, fn, reads, writes, extra, True)

    NDMALANES = 8

    def build(self, final_deps):
        ops = self.ops
        needed = [[] for _ in ops]
        for o in ops:
            for d in o.deps:
                src = ops[d]
                if (src.eng == o.eng and src.eng == "pe"
                        and not src.is_dma and not o.is_dma):
                    continue
                needed[o.idx].append(d)
                src.inc = True
        for d in final_deps:
            ops[d].inc = True
        cnt = {}
        dma_n = {}
        semkeys = {}
        prior_lane_val = {}
        for o in ops:
            if o.is_dma:
                i = dma_n.get(o.eng, 0)
                dma_n[o.eng] = i + 1
                key = f"d_{o.eng}{i % self.NDMALANES}"
                semkeys[o.idx] = key
                prior_lane_val[o.idx] = cnt.get(key, 0)
                cnt[key] = cnt.get(key, 0) + 16
                o.val = cnt[key]
            elif o.inc:
                key = "c_" + o.eng
                semkeys[o.idx] = key
                cnt[key] = cnt.get(key, 0) + 1
                o.val = cnt[key]
            else:
                semkeys[o.idx] = "c_" + o.eng
                o.val = None
        streams = {e: [] for e in ENGS}
        water = {e: {} for e in ENGS}
        sems = {}
        stack = contextlib.ExitStack()
        for key in sorted(set(semkeys.values()) | set(cnt)):
            sems[key] = stack.enter_context(self.nc.semaphore(key))

        def emit_wait(eng, sk, v):
            w = water[eng]
            if w.get(sk, 0) >= v:
                return
            w[sk] = v
            streams[eng].append(lambda e, s=sems[sk], vv=v: e.wait_ge(s, vv))

        for o in ops:
            eng = o.eng
            for d in needed[o.idx]:
                emit_wait(eng, semkeys[d], ops[d].val)
            if o.is_dma:
                pv = prior_lane_val[o.idx]
                if pv > 0:
                    emit_wait(eng, semkeys[o.idx], pv)
                streams[eng].append(
                    lambda e, f=o.fn, s=sems[semkeys[o.idx]]:
                    f(e).then_inc(s, 16))
            elif o.inc:
                streams[eng].append(
                    lambda e, f=o.fn, s=sems[semkeys[o.idx]]:
                    f(e).then_inc(s, 1))
            else:
                streams[eng].append(lambda e, f=o.fn: f(e))
        for d in final_deps:
            emit_wait("gp", semkeys[d], ops[d].val)
        return streams, stack


class Arena:

    def __init__(self, p, nc, words):
        self.p = p
        self.t = nc.alloc_sbuf_tensor("arena", [128, words], F32)
        self.tb = self.t.bitcast(BF16)
        self.words = words
        self.live = {}
        self.dead = {}

    def place(self, name, off, w, dt=F32, parts=128):
        assert off + w <= self.words, (name, off, w)
        for nm, (o2, w2) in self.live.items():
            assert off + w <= o2 or o2 + w2 <= off, \
                f"arena overlap: {name}@{off}+{w} vs live {nm}@{o2}+{w2}"
        deps = set()
        for nm, (o2, w2) in self.dead.items():
            if not (off + w <= o2 or o2 + w2 <= off):
                deps.update(self.p.keys_of(nm))
        if deps:
            self.p.alias_deps[name] = sorted(deps)
        self.live[name] = (off, w)
        self.dead.pop(name, None)
        if dt == F32:
            return self.t.ap()[0:parts, off:off + w]
        return self.tb.ap()[0:parts, 2 * off:2 * (off + w)]

    def free(self, *names):
        for name in names:
            self.dead[name] = self.live.pop(name)


def strided(view, elem_off, free_dims, nparts=128):
    pstep = view.ap[0][0]
    return bass.AP(view.tensor, view.offset + elem_off,
                   [[pstep, nparts]] + free_dims)



def build_nc(debug_stage=None, build_until=None):
    nc = bass.Bass("TRN2", target_bir_lowering=False, debug=False)

    def din(name, shape, dt=F32):
        return nc.dram_tensor(name, shape, dt, kind="ExternalInput").ap()

    x_in = din("x", [T, D])
    w_qkv = din("w_qkv", [D, 3 * D])
    w_out = din("w_out", [D, D])
    l0a_vecs = din("l0a_vecs", [128, 3 * D], BF16)
    identc = din("identc", [128, 128], BF16)
    l0f_w = din("l0f_w", [D, 1160])
    l0f_bt = din("l0f_bt", [128, D])
    l0f_b = din("l0f_b", [128, D], BF16)
    l1_maa_w1 = din("l1_maa_w1", [D, 160])
    l1_w2pk = din("l1_w2pk", [128, D])
    l1_w2p4 = din("l1_w2p4", [32, D])
    l1_dec_w1 = din("l1_dec_w1", [D, 64])
    l1_dec_w2 = din("l1_dec_w2", [64, D])
    l1_maav = din("l1_maav", [128, 56])
    l1_wr = din("l1_wr", [D, D])
    l1_wk = din("l1_wk", [D, D])
    l1_wv = din("l1_wv", [D, D])
    l1_wg = din("l1_wg", [D, D])
    l1_wo = din("l1_wo", [D, D])
    l1_lnw = din("l1_lnw", [128, D], BF16)
    l1_lnb = din("l1_lnb", [128, D], BF16)
    l1_masks = din("l1_masks", [128, 128], U8)
    l1f_w1t = din("l1f_w1t", [D, 4096])
    l1f_w2t = din("l1f_w2t", [4096, D])
    l1f_lnw = din("l1f_lnw", [128, D], BF16)
    l1f_lnb = din("l1f_lnb", [128, D], BF16)

    y_out = nc.dram_tensor("y", [T, D], F32, kind="ExternalOutput").ap()
    dbg = None
    if debug_stage is not None:
        dbg = nc.dram_tensor("dbg", [T, D], F32, kind="ExternalOutput").ap()

    def sbt(name, shape, dt):
        return nc.alloc_sbuf_tensor(name, shape, dt)

    muA = sbt("muA", [128, NT], F32).ap()
    rhoA = sbt("rhoA", [128, NT], F32).ap()
    muB = sbt("muB", [128, NT], F32).ap()
    rhoB = sbt("rhoB", [128, NT], F32).ap()
    stat6 = sbt("stat6", [128, 12 * NT], F32).ap()
    stat2 = sbt("stat2", [128, 2 * NT], F32).ap()
    ident = sbt("ident", [128, 128], BF16).ap()
    ones_row = sbt("ones_row", [1, 128], F32).ap()
    ones_mat = sbt("ones_mat", [128, 128], F32).ap()

    PBt = [nc.alloc_psum_tensor(f"pb{i}", [128, 512], F32) for i in range(6)]
    PB = [t.ap() for t in PBt]
    PBb = [t.bitcast(BF16).ap() for t in PBt]
    spill_x2 = nc.dram_tensor("spill_x2", [T, D], F32).ap()
    PTRt = nc.alloc_psum_tensor("ptr", [128, 1024], BF16)
    PTR = PTRt.ap()
    PTRf = PTRt.bitcast(F32).ap()
    PSM = nc.alloc_psum_tensor("psm", [128, 512], F32).ap()

    class TRot:

        def __init__(self, pairs):
            self.pairs = pairs
            self.i = 0

        def nxt(self):
            pr = self.pairs[self.i % len(self.pairs)]
            self.i += 1
            return pr

    p = Prog(nc)
    AR = Arena(p, nc, 52400)
    ROT_A = TRot([(PTR, ("PTR",))] + [(PBb[i], ("PB", i)) for i in range(4)])
    ROT_B = TRot([(PTR, ("PTR",)), (PBb[4], ("PB", 4)), (PBb[5], ("PB", 5))])
    ROT_C = TRot([(PTR, ("PTR",)), (PBb[2], ("PB", 2)), (PBb[4], ("PB", 4)),
                  (PBb[5], ("PB", 5))])

    stream = AR.place("stream", 0, 16384)
    xTb = AR.place("xTb", 16384, 8192, BF16)
    aoT = AR.place("aoT", 24576, 8192, BF16)
    tI_a = AR.place("tI_a", 32768, 1024)
    tIb_a = AR.place("tIb_a", 33792, 512, BF16)
    wout_s = AR.place("wout", 34304, 4096, BF16)
    vecs_s = AR.place("vecs", 38400, 1536, BF16)
    wq_g = [AR.place("wq0", 39936, 1536, BF16),
            AR.place("wq1", 41472, 1536, BF16)]
    qT_g = AR.place("qT", 43008, 1024, BF16)
    kT_g = AR.place("kT", 44032, 1024, BF16)
    vT_g = AR.place("vT", 45056, 1024, BF16)
    Vaug = [AR.place("vaug0", 46080, 1024, BF16),
            AR.place("vaug1", 47104, 1024, BF16)]
    attn_ch = AR.place("ach", 48128, 512, BF16)
    rb_bc = AR.place("rbc", 48640, 512)
    ach4 = AR.tb.ap()[0:128, 66560:68608]

    p.dma("sp", ident[:], identc[:], writes=["ident"])
    p.dma("sp", vecs_s[:], l0a_vecs[:], writes=["vecs"])
    for tt in range(NT):
        p.dma("sp", stream[:, tt * D:(tt + 1) * D],
              x_in[tt * 128:(tt + 1) * 128, :], writes=[("stream", tt)])
    for fc in range(NF):
        p.dma("gp", wout_s[:, fc * D:(fc + 1) * D],
              w_out[fc * 128:(fc + 1) * 128, :], writes=[("wout", fc)])
    p.op("dve", lambda v: v.memset(ones_row[:], 1.0), writes=["ones_row"])
    p.op("dve", lambda v: v.memset(ones_mat[:], 1.0), writes=["ones_mat"])
    ones_bf = sbt("ones_bf", [128, 1], BF16).ap()
    p.op("dve", lambda v: v.memset(ones_bf[:], 1.0), writes=["ones_bf"])
    for i in range(2):
        p.op("dve", lambda v, i=i: v.memset(Vaug[i][:], 0.0),
             writes=[("vaug_ones", i)])
        p.op("dve", lambda v, i=i: v.memset(
            strided(Vaug[i], 64, [[128, NT]]), 1.0),
            writes=[("vaug_ones", i)])

    def ln_stats(src_keys_fn, src_ap_fn, mu, rho, mukey, rhokey):
        for tt in range(NT):
            o6 = 12 * tt
            o2 = 2 * tt
            p.op("dve", lambda v, c=tt, o=o6: v.bn_stats(
                stat6[:, o:o + 6], src_ap_fn(c)[:, 0:512]),
                reads=src_keys_fn(tt), writes=[("stat6", tt, 0)])
            p.op("dve", lambda v, c=tt, o=o6: v.bn_stats(
                stat6[:, o + 6:o + 12], src_ap_fn(c)[:, 512:1024]),
                reads=src_keys_fn(tt), writes=[("stat6", tt, 1)])
            p.op("dve", lambda v, o=o6, o2_=o2: v.bn_aggr(
                stat2[:, o2_:o2_ + 2], stat6[:, o:o + 12]),
                reads=[("stat6", tt, 0), ("stat6", tt, 1)],
                writes=[("stat2", tt)])
            p.op("dve", lambda v, c=tt, m=mu, o2_=o2: v.tensor_copy(
                m[:, c:c + 1], stat2[:, o2_:o2_ + 1]),
                reads=[("stat2", tt)], writes=[(mukey, tt)])
            p.op("dve", lambda v, o2_=o2: v.tensor_scalar_add(
                stat2[:, o2_ + 1:o2_ + 2], stat2[:, o2_ + 1:o2_ + 2], LN_EPS),
                reads=[], writes=[("stat2", tt)])
            p.op("dve", lambda v, o2_=o2: v.reciprocal(
                stat2[:, o2_ + 1:o2_ + 2], stat2[:, o2_ + 1:o2_ + 2]),
                reads=[], writes=[("stat2", tt)])
            p.op("act", lambda s, c=tt, r=rho, o2_=o2: s.activation(
                r[:, c:c + 1], stat2[:, o2_ + 1:o2_ + 2], AF.Sqrt),
                reads=[("stat2", tt)], writes=[(rhokey, tt)])

    def tp128(dst_ap, src_ap, rds, wrs, rot, ncols=128, idbase=0):
        bank, key = rot.nxt()
        p.op("pe", lambda t, b=bank: t.transpose(
            b[:, 0:ncols], src_ap,
            ident[idbase:idbase + ncols, idbase:idbase + ncols]
            if ncols != 128 or idbase else ident[:]),
            reads=list(rds) + ["ident"], writes=[key])
        p.op("dve", lambda v, b=bank: v.tensor_copy(dst_ap, b[:, 0:ncols]),
             reads=[key], writes=list(wrs))

    ln_stats(lambda tt: [("stream", tt)],
             lambda tt: stream[:, tt * D:(tt + 1) * D], muA, rhoA, "muA", "rhoA")

    for tt in range(NT):
        p.op("dve", lambda v, c=tt: v.tensor_scalar(
            tI_a[:], stream[:, c * D:(c + 1) * D],
            muA[:, c:c + 1], rhoA[:, c:c + 1], ALU.subtract, ALU.mult),
            reads=[("stream", tt), ("muA", tt), ("rhoA", tt)],
            writes=["tI_a"])
        p.op("dve", lambda v: v.tensor_copy(tIb_a[:], tI_a[:]),
             reads=["tI_a"], writes=["tIb_a"])
        for fc in range(NF):
            tp128(xTb[:, fc * T + tt * 128: fc * T + (tt + 1) * 128],
                  tIb_a[:, fc * 128:(fc + 1) * 128],
                  ["tIb_a"], [("xTb", fc, tt)], ROT_A)
    AR.free("tI_a", "tIb_a")
    rowsum = AR.place("rowsum", 32768, 512, parts=1)
    tI_b = AR.place("tI_b", 33280, 1024)

    xT_keys = [("xTb", fc, tt) for fc in range(NF) for tt in range(NT)]

    for g in range(NH // 2 if build_until != "xT" else 0):
        wqb = wq_g[g % 2]
        wqn = "wq" + str(g % 2)
        for dc in range(NF):
            p.dma("gp", wqb[:, dc * 384:(dc + 1) * 384],
                  w_qkv[dc * 128:(dc + 1) * 128, 384 * g:384 * (g + 1)],
                  writes=[(wqn, dc)])

        for mt in range(3):
            for ns in range(4):
                bi = (mt * 4 + ns) % 6
                bank = PB[bi]
                bkey = ("PB", bi)
                for dc in range(NF):
                    rd = [(wqn, dc)]
                    if dc == 0:
                        rd = rd + xT_keys
                    p.op("pe", lambda t, b=bank, m=mt, n=ns, d=dc, w=wqb:
                         t.matmul(b[:, 0:512],
                                  w[:, d * 384 + m * 128:d * 384 + (m + 1) * 128],
                                  xTb[:, d * T + n * 512:d * T + (n + 1) * 512],
                                  start=(d == 0), stop=(d == NF - 1)),
                         reads=rd, writes=[bkey])
                cs = slice(ns * 512, (ns + 1) * 512)
                if mt == 0:
                    p.op("dve", lambda v, b=bank, c=cs: v.tensor_copy(
                        qT_g[0:64, c], b[0:64, 0:512]),
                        reads=[bkey], writes=[("qT", 0, ns)])
                    p.op("dve", lambda v, b=bank, c=cs: v.tensor_scalar_mul(
                        kT_g[0:64, c], b[64:128, 0:512], 0.125),
                        reads=[bkey], writes=[("kT", 0, ns)])
                elif mt == 1:
                    p.op("dve", lambda v, b=bank, c=cs: v.tensor_copy(
                        vT_g[0:64, c], b[0:64, 0:512]),
                        reads=[bkey], writes=[("vT", 0, ns)])
                    p.op("dve", lambda v, b=bank, c=cs: v.tensor_copy(
                        qT_g[64:128, c], b[64:128, 0:512]),
                        reads=[bkey], writes=[("qT", 1, ns)])
                else:
                    p.op("dve", lambda v, b=bank, c=cs: v.tensor_scalar_mul(
                        kT_g[64:128, c], b[0:64, 0:512], 0.125),
                        reads=[bkey], writes=[("kT", 1, ns)])
                    p.op("dve", lambda v, b=bank, c=cs: v.tensor_copy(
                        vT_g[64:128, c], b[64:128, 0:512]),
                        reads=[bkey], writes=[("vT", 1, ns)])

        qT_all = [("qT", h, n) for h in range(2) for n in range(4)]
        kT_all = [("kT", h, n) for h in range(2) for n in range(4)]
        vT_all = [("vT", h, n) for h in range(2) for n in range(4)]

        vaug_keys = [[], []]
        for hh in range(2):
            for tt in range(NT):
                k = ("vaug" + str(hh), tt)
                tp128(Vaug[hh][:, tt * 128:tt * 128 + 64],
                      vT_g[hh * 64:hh * 64 + 64, tt * 128:(tt + 1) * 128],
                      vT_all, [k, ("vaug_ones", hh)], ROT_B,
                      ncols=64, idbase=hh * 64)
                vaug_keys[hh].append(k)

        for hh in range(2):
            base = hh * 64
            h_abs = 2 * g + hh
            avb = [PB[2 * hh + 0], PB[2 * hh + 1]]
            avkeys = [("PB", 2 * hh + 0), ("PB", 2 * hh + 1)]
            if hh == 0:
                scb = [PSM, PB[4], PB[2], PB[3]]
                sck = [("PSM",), ("PB", 4), ("PB", 2), ("PB", 3)]
            else:
                scb = [PTRf, PB[5], PB[0], PB[1]]
                sck = [("PTR",), ("PB", 5), ("PB", 0), ("PB", 1)]
            fc = (h_abs * 64) // 128
            rhalf = (h_abs * 64) % 128
            pend = []

            def mk_div(q, n, h=h_abs, f=fc, r=rhalf, sb=scb[1], sk=sck[1],
                       ab=avb, abk=avkeys, g_=g):
                def _div(q=q, n=n, h=h, f=f, r=r, sb=sb, sk=sk,
                         ab=ab[n], abk=abk[n]):
                    p.op("dve", lambda v, a_=ab: v.tensor_scalar_add(
                        rowsum[:], a_[64:65, 0:512], 1e-6),
                        reads=[abk], writes=["rowsum"])
                    p.op("act", lambda s: s.activation(
                        rowsum[:], rowsum[:], AF.Ln),
                        reads=["rowsum"], writes=["rowsum"])
                    p.op("pe", lambda t, s_=sb: t.matmul(
                        s_[0:64, 0:512], ones_row[0:1, 0:64], rowsum[:],
                        start=True, stop=True),
                        reads=["rowsum", "ones_row"], writes=[sk])
                    p.op("act", lambda s, s_=sb: s.activation(
                        rb_bc[0:64, 0:512], s_[0:64, 0:512], AF.Exp,
                        scale=-1.0),
                        reads=[sk], writes=["rbc"])
                    p.op("dve", lambda v, a_=ab, q_=q, n_=n, f_=f, r_=r:
                         v.tensor_tensor(
                             aoT[r_:r_ + 64, f_ * T + q_ * 1024 + n_ * 512:
                                 f_ * T + q_ * 1024 + (n_ + 1) * 512],
                             a_[0:64, 0:512], rb_bc[0:64, 0:512], ALU.mult),
                         reads=[abk, "rbc"],
                         writes=[("aoT", f, h, q, n)])
                return _div

            for ci, (qh, n2) in enumerate([(q, n) for q in range(2)
                                           for n in range(2)]):
                for kt in range(NT):
                    sb = scb[kt % 4]
                    sk = sck[kt % 4]
                    abuf = attn_ch if kt % 4 < 2 else ach4
                    ach = abuf[:, (kt % 2) * 512:(kt % 2 + 1) * 512]
                    akey = ("ach", kt % 4)
                    p.op("pe", lambda t, b=base, k=kt, q=qh, n=n2, s_=sb:
                         t.matmul(s_[:, 0:512],
                                  kT_g[b:b + 64, k * 128:(k + 1) * 128],
                                  qT_g[b:b + 64,
                                       q * 1024 + n * 512:q * 1024 + (n + 1) * 512],
                                  start=True, stop=True,
                                  tile_position=(b, 0)),
                         reads=kT_all + qT_all, writes=[sk])
                    if kt % 2 == 0:
                        p.op("act", lambda s, d=ach, s_=sb: s.activation(
                            d, s_[:, 0:512], AF.Relu),
                            reads=[sk], writes=[akey])
                    else:
                        p.op("dve", lambda v, d=ach, s_=sb:
                             v.tensor_scalar_max(d, s_[:, 0:512], 0.0),
                             reads=[sk], writes=[akey])
                    if kt >= 1:
                        kp = kt - 1
                        pbuf = attn_ch if kp % 4 < 2 else ach4
                        p.op("pe", lambda t, k=kp, n=n2, h=hh,
                             d=pbuf[:, (kp % 2) * 512:(kp % 2 + 1) * 512],
                             ab=avb: t.matmul(
                                 ab[n][0:128, 0:512],
                                 Vaug[h][:, k * 128:(k + 1) * 128],
                                 d, start=(k == 0), stop=False),
                             reads=[("ach", kp % 4)] + vaug_keys[hh],
                             writes=[avkeys[n2]])
                    if kt == 1 and pend:
                        pend.pop()()
                kp = NT - 1
                pbuf = attn_ch if kp % 4 < 2 else ach4
                p.op("pe", lambda t, k=kp, n=n2, h=hh,
                     d=pbuf[:, (kp % 2) * 512:(kp % 2 + 1) * 512],
                     ab=avb: t.matmul(
                         ab[n][0:128, 0:512],
                         Vaug[h][:, k * 128:(k + 1) * 128],
                         d, start=False, stop=True),
                     reads=[("ach", kp % 4)] + vaug_keys[hh],
                     writes=[avkeys[n2]])
                pend.append(mk_div(qh, n2))
            while pend:
                pend.pop()()

    AR.free("wq0", "wq1", "qT", "kT", "vT", "vaug0", "vaug1", "rowsum", "xTb")
    aoT_all = [k for k in p.rgn if isinstance(k, tuple) and k and k[0] == "aoT"]

    ytmp = AR.place("ytmp", 16384, 8192, BF16)

    for tt in range(NT if build_until not in ("xT", "groups") else 0):
        for ns in range(2):
            bank = PB[ns]
            bkey = ("PB", ns)
            for fc in range(NF):
                rd = [("wout", fc)]
                if fc == 0:
                    rd = rd + aoT_all
                p.op("pe", lambda t, c=tt, n=ns, f=fc, b=bank:
                     t.matmul(b[:, 0:512],
                              aoT[:, f * T + c * 128:f * T + (c + 1) * 128],
                              wout_s[:, f * D + n * 512:f * D + (n + 1) * 512],
                              start=(f == 0), stop=(f == NF - 1)),
                     reads=rd, writes=[bkey])
            p.op("dve", lambda v, c=tt, n=ns, b=bank: v.tensor_tensor(
                ytmp[:, c * D + n * 512:c * D + (n + 1) * 512],
                b[:, 0:512], vecs_s[:, n * 512:(n + 1) * 512], ALU.add),
                reads=[bkey, "vecs"], writes=[("ytmp", tt, ns)])
            p.op("dve", lambda v, c=tt, n=ns: v.tensor_scalar(
                tI_b[:, n * 512:(n + 1) * 512],
                stream[:, c * D + n * 512:c * D + (n + 1) * 512],
                muA[:, c:c + 1], rhoA[:, c:c + 1], ALU.subtract, ALU.mult),
                reads=[("stream", tt), ("muA", tt), ("rhoA", tt)],
                writes=[("tI_b", ns), ("ach", 2), ("ach", 3)])
            p.op("dve", lambda v, c=tt, n=ns: v.tensor_tensor(
                ytmp[:, c * D + n * 512:c * D + (n + 1) * 512],
                ytmp[:, c * D + n * 512:c * D + (n + 1) * 512],
                tI_b[:, n * 512:(n + 1) * 512], ALU.add),
                reads=[("tI_b", ns)],
                writes=[("ytmp", tt, ns)])

    if build_until in ("xT", "groups"):
        NT_E = 0
    else:
        NT_E = NT
    if NT_E:
        ln_stats(lambda tt: [("ytmp", tt, 0), ("ytmp", tt, 1)],
                 lambda tt: ytmp[:, tt * D:(tt + 1) * D], muB, rhoB, "muB", "rhoB")
    for tt in range(NT_E):
        p.op("dve", lambda v, c=tt: v.tensor_scalar(
            ytmp[:, c * D:(c + 1) * D], ytmp[:, c * D:(c + 1) * D],
            muB[:, c:c + 1], rhoB[:, c:c + 1], ALU.subtract, ALU.mult),
            reads=[("muB", tt), ("rhoB", tt)],
            writes=[("ytmp", tt, 0), ("ytmp", tt, 1), ("y2", tt)])
        p.op("dve", lambda v, c=tt: v.tensor_tensor(
            ytmp[:, c * D:(c + 1) * D], ytmp[:, c * D:(c + 1) * D],
            vecs_s[:, D:2 * D], ALU.mult),
            reads=["vecs"], writes=[("y2", tt)])
        p.op("dve", lambda v, c=tt: v.tensor_tensor(
            ytmp[:, c * D:(c + 1) * D], ytmp[:, c * D:(c + 1) * D],
            vecs_s[:, 2 * D:3 * D], ALU.add),
            reads=["vecs"], writes=[("y2", tt)])
    if NT_E:
        ln_stats(lambda tt: [("y2", tt)],
                 lambda tt: ytmp[:, tt * D:(tt + 1) * D], muA, rhoA, "muA", "rhoA")
    for tt in range(NT_E):
        p.op("dve", lambda v, c=tt: v.tensor_scalar(
            ytmp[:, c * D:(c + 1) * D], ytmp[:, c * D:(c + 1) * D],
            muA[:, c:c + 1], rhoA[:, c:c + 1], ALU.subtract, ALU.mult),
            reads=[("muA", tt), ("rhoA", tt)],
            writes=[("y2", tt), ("y3", tt)])
        p.op("dve", lambda v, c=tt: v.tensor_tensor(
            stream[:, c * D:(c + 1) * D], stream[:, c * D:(c + 1) * D],
            ytmp[:, c * D:(c + 1) * D], ALU.add),
            reads=[("y3", tt)], writes=[("stream", tt)])

    DO_L0F = build_until not in ("xT", "groups", "l0a")
    AR.free("aoT", "wout", "vecs", "ach", "rbc")
    xT2 = AR.place("xT2", 24576, 8192, BF16)
    l0fw_s = AR.place("l0fw", 34304, 4640, BF16)
    bt_s = AR.place("l0fbt", 39936, 512, BF16)
    bfvec = AR.place("l0fb", 40448, 512, BF16)
    gsc = AR.place("gsc", 41100, 200)
    ghb = AR.place("ghb", 41300, 64, BF16)
    ghT = AR.place("ghT", 41364, 64, BF16)

    for dc in range(NF if DO_L0F else 0):
        p.dma("gp", l0fw_s[:, dc * 1160:(dc + 1) * 1160],
              l0f_w[dc * 128:(dc + 1) * 128, :], writes=[("l0fw", dc)])
    if DO_L0F:
        p.dma("gp", bt_s[:], l0f_bt[:], writes=["l0fbt"])
        p.dma("gp", bfvec[:], l0f_b[:], writes=["l0fb"])

    if DO_L0F:
        ln_stats(lambda tt: [("stream", tt)],
                 lambda tt: stream[:, tt * D:(tt + 1) * D], muA, rhoA, "muA", "rhoA")
    xcast = AR.place("xcast", 41428, 512, BF16)
    for tt in range(NT if DO_L0F else 0):
        p.op("dve", lambda v, c=tt: v.tensor_scalar(
            tI_b[:, 0:1024], stream[:, c * D:(c + 1) * D],
            muA[:, c:c + 1], rhoA[:, c:c + 1], ALU.subtract, ALU.mult),
            reads=[("stream", tt), ("muA", tt), ("rhoA", tt)], writes=["tI_bf"])
        p.op("dve", lambda v: v.tensor_copy(xcast[:], tI_b[:, 0:1024]),
             reads=["tI_bf"], writes=["xcast"])
        for fc in range(NF):
            tp128(xT2[:, fc * T + tt * 128: fc * T + (tt + 1) * 128],
                  xcast[:, fc * 128:(fc + 1) * 128],
                  ["xcast"], [("xT2", fc, tt)], ROT_A)
    xT2_keys = [("xT2", fc, tt) for fc in range(NF) for tt in range(NT)]

    for tt in range(NT if DO_L0F else 0):
        boff = 0 if tt % 2 == 0 else 3
        b0, b1, b2 = PB[boff], PB[boff + 1], PB[boff + 2]
        k0, k1, k2 = ("PB", boff), ("PB", boff + 1), ("PB", boff + 2)
        for dc in range(NF):
            rd = [("l0fw", dc), ("xT2", dc, tt)]
            p.op("pe", lambda t, c=tt, d=dc, b=b0: t.matmul(
                b[:, 0:512],
                xT2[:, d * T + c * 128:d * T + (c + 1) * 128],
                l0fw_s[:, d * 1160:d * 1160 + 512],
                start=(d == 0), stop=False),
                reads=rd, writes=[k0])
            p.op("pe", lambda t, c=tt, d=dc, b=b1: t.matmul(
                b[:, 0:512],
                xT2[:, d * T + c * 128:d * T + (c + 1) * 128],
                l0fw_s[:, d * 1160 + 512:d * 1160 + 1024],
                start=(d == 0), stop=False),
                reads=rd, writes=[k1])
            p.op("pe", lambda t, c=tt, d=dc, b=b2: t.matmul(
                b[:, 0:136],
                xT2[:, d * T + c * 128:d * T + (c + 1) * 128],
                l0fw_s[:, d * 1160 + 1024:d * 1160 + 1160],
                start=(d == 0), stop=(d == NF - 1)),
                reads=rd, writes=[k2])
        GA = gsc
        p.op("dve", lambda v, b=b2: v.tensor_reduce(
            GA[:, 24:25], b[:, 0:8], AX.X, ALU.max, negate=True),
            reads=[k2], writes=[("gsc", "negm1")])
        p.op("act", lambda s, b=b2: s.activation(
            GA[:, 0:8], b[:, 0:8], AF.Exp, bias=GA[:, 24:25], scale=1.0),
            reads=[k2, ("gsc", "negm1")], writes=[("gsc", "ge")])
        p.op("dve", lambda v: v.tensor_reduce(
            GA[:, 25:26], GA[:, 0:8], AX.X, ALU.add),
            reads=[("gsc", "ge")], writes=[("gsc", "s")])
        p.op("dve", lambda v: v.reciprocal(GA[:, 25:26], GA[:, 25:26]),
             reads=[("gsc", "s")], writes=[("gsc", "s")])
        p.op("dve", lambda v: v.tensor_scalar_mul(
            GA[:, 8:16], GA[:, 0:8], GA[:, 25:26]),
            reads=[("gsc", "ge"), ("gsc", "s")], writes=[("gsc", "gate")])
        p.op("dve", lambda v: v.tensor_reduce(
            GA[:, 26:27], GA[:, 8:16], AX.X, ALU.max),
            reads=[("gsc", "gate")], writes=[("gsc", "m1g")])
        p.op("dve", lambda v: v.tensor_scalar(
            GA[:, 16:24], GA[:, 8:16], GA[:, 26:27], None, ALU.is_equal),
            reads=[("gsc", "gate"), ("gsc", "m1g")], writes=[("gsc", "scr")])
        p.op("dve", lambda v: v.scalar_tensor_tensor(
            GA[:, 16:24], GA[:, 16:24], -1e30, GA[:, 8:16],
            ALU.mult, ALU.add),
            reads=[("gsc", "gate")], writes=[("gsc", "scr")])
        p.op("dve", lambda v: v.tensor_reduce(
            GA[:, 27:28], GA[:, 16:24], AX.X, ALU.max),
            reads=[("gsc", "scr")], writes=[("gsc", "m2")])
        p.op("dve", lambda v: v.tensor_scalar(
            GA[:, 16:24], GA[:, 8:16], GA[:, 27:28], None, ALU.is_ge),
            reads=[("gsc", "gate"), ("gsc", "m2")], writes=[("gsc", "scr")])
        p.op("dve", lambda v: v.tensor_tensor(
            GA[:, 8:16], GA[:, 8:16], GA[:, 16:24], ALU.mult),
            reads=[("gsc", "scr")], writes=[("gsc", "gate")])
        p.op("dve", lambda v: v.tensor_reduce(
            GA[:, 28:29], GA[:, 8:16], AX.X, ALU.add),
            reads=[("gsc", "gate")], writes=[("gsc", "s2")])
        p.op("dve", lambda v: v.tensor_scalar_add(
            GA[:, 28:29], GA[:, 28:29], 1e-6),
            reads=[], writes=[("gsc", "s2")])
        p.op("dve", lambda v: v.reciprocal(GA[:, 28:29], GA[:, 28:29]),
             reads=[], writes=[("gsc", "s2")])
        p.op("dve", lambda v: v.tensor_scalar_mul(
            GA[:, 8:16], GA[:, 8:16], GA[:, 28:29]),
            reads=[("gsc", "s2")], writes=[("gsc", "gate")])
        p.op("dve", lambda v, b=b2: v.tensor_tensor(
            ghb[:], b[:, 8:136],
            strided(GA, 8, [[1, 8], [0, 16]]), ALU.mult),
            reads=[k2, ("gsc", "gate")], writes=["ghb"])
        slot = tt % 8
        pkey = ("PTR",)
        p.op("pe", lambda t, s=slot: t.transpose(
            PTR[:, s * 128:s * 128 + 128], ghb[:], ident[:]),
            reads=["ghb", "ident"], writes=[pkey])
        p.op("dve", lambda v, s=slot: v.tensor_copy(
            ghT[:], PTR[:, s * 128:s * 128 + 128]),
            reads=[pkey], writes=["ghT"])
        p.op("pe", lambda t, b=b0: t.matmul(
            b[:, 0:512], ghT[:], bt_s[:, 0:512],
            start=False, stop=True),
            reads=["ghT", "l0fbt"], writes=[k0])
        p.op("pe", lambda t, b=b1: t.matmul(
            b[:, 0:512], ghT[:], bt_s[:, 512:1024],
            start=False, stop=True),
            reads=["ghT", "l0fbt"], writes=[k1])
        for ns, bb, kk in ((0, b0, k0), (1, b1, k1)):
            p.op("dve", lambda v, c=tt, n=ns, b=bb: v.tensor_tensor(
                ytmp[:, c * D + n * 512:c * D + (n + 1) * 512],
                b[:, 0:512], bfvec[:, n * 512:(n + 1) * 512], ALU.add),
                reads=[kk, "l0fb"], writes=[("ytmp", tt, ns)])

    if DO_L0F:
        ln_stats(lambda tt: [("ytmp", tt, 0), ("ytmp", tt, 1)],
                 lambda tt: ytmp[:, tt * D:(tt + 1) * D], muB, rhoB, "muB", "rhoB")
    for tt in range(NT if DO_L0F else 0):
        p.op("dve", lambda v, c=tt: v.tensor_scalar(
            ytmp[:, c * D:(c + 1) * D], ytmp[:, c * D:(c + 1) * D],
            muB[:, c:c + 1], rhoB[:, c:c + 1], ALU.subtract, ALU.mult),
            reads=[("muB", tt), ("rhoB", tt)],
            writes=[("ytmp", tt, 0), ("ytmp", tt, 1), ("yf3", tt)])
        p.op("dve", lambda v, c=tt: v.tensor_tensor(
            stream[:, c * D:(c + 1) * D], stream[:, c * D:(c + 1) * D],
            ytmp[:, c * D:(c + 1) * D], ALU.add),
            reads=[("yf3", tt)], writes=[("stream", tt)])

    DO_L1A = build_until not in ("xT", "groups", "l0a", "l0f")
    AR.free("ytmp", "xT2", "l0fw", "l0fbt", "l0fb", "gsc", "ghb", "ghT",
            "xcast", "tI_b")
    spill_x = nc.dram_tensor("spill_x", [T, D], F32).ap()
    spill_rkvg = {nm: nc.dram_tensor("spill_" + nm, [D, T], BF16).ap()
                  for nm in ("r", "k", "v", "g")}

    xTb1 = AR.place("xTb1", 16384, 8192, BF16)
    dxTb = AR.place("dxTb", 24576, 8192, BF16)
    tmpf = AR.place("tmpf", 32768, 1024)
    xbc = AR.place("xbc", 33792, 512, BF16)
    w1maa_s = AR.place("w1maa", 41984, 640, BF16)
    w2pk_s = AR.place("w2pk", 42624, 512, BF16)
    w2p4_s = AR.place("w2p4", 43136, 512, BF16)
    d1_s = AR.place("d1s", 43648, 256, BF16)
    d2_s = AR.place("d2s", 43904, 512, BF16)
    maav = AR.place("maav", 44416, 64)
    wjs = AR.place("wjs", 44480, 4096, BF16)
    xxa = AR.place("xxa", 48576, 1024, BF16)
    xxb = AR.place("xxb", 49600, 1024, BF16)
    t1s = AR.place("t1s", 50624, 1024, BF16)
    msc = AR.place("msc", 51648, 512)

    if DO_L1A:
        for dc in range(NF):
            p.dma("gp", w1maa_s[:, dc * 160:(dc + 1) * 160],
                  l1_maa_w1[dc * 128:(dc + 1) * 128, :], writes=[("w1maa", dc)])
            p.dma("gp", d1_s[:, dc * 64:(dc + 1) * 64],
                  l1_dec_w1[dc * 128:(dc + 1) * 128, :], writes=[("d1s", dc)])
        p.dma("gp", w2pk_s[:], l1_w2pk[:], writes=["w2pk"])
        p.dma("gp", w2p4_s[0:32, :], l1_w2p4[:], writes=["w2p4"])
        p.dma("gp", d2_s[0:64, :], l1_dec_w2[:], writes=["d2s"])
        p.dma("sp", maav[:, 0:56], l1_maav[:], writes=["maav"])

        ln_stats(lambda tt: [("stream", tt)],
                 lambda tt: stream[:, tt * D:(tt + 1) * D], muA, rhoA,
                 "muA", "rhoA")
        for tt in range(NT):
            p.op("dve", lambda v, c=tt: v.tensor_scalar(
                tmpf[:], stream[:, c * D:(c + 1) * D],
                muA[:, c:c + 1], rhoA[:, c:c + 1], ALU.subtract, ALU.mult),
                reads=[("stream", tt), ("muA", tt), ("rhoA", tt)],
                writes=["tmpf"])
            p.op("dve", lambda v: v.tensor_copy(xbc[:], tmpf[:]),
                 reads=["tmpf"], writes=["xbc"])
            for fc in range(NF):
                tp128(xTb1[:, fc * T + tt * 128: fc * T + (tt + 1) * 128],
                      xbc[:, fc * 128:(fc + 1) * 128],
                      ["xbc"], [("xTb1", fc, tt)], ROT_A)
        xT1_keys = [("xTb1", fc, tt) for fc in range(NF) for tt in range(NT)]

        for tt in range(NT):
            p.dma("sp", spill_x[tt * 128:(tt + 1) * 128, :],
                  stream[:, tt * D:(tt + 1) * D], reads=[("stream", tt)])

        AR.free("tmpf", "xbc")
        sh = AR.place("sh", 32768, 1024, BF16)
        for fc in range(NF):
            xf = xTb1[:, fc * T:(fc + 1) * T]
            p.op("dve", lambda v, x=xf: v.tensor_tensor(
                sh[:, 1:T - 1], x[:, 0:T - 2], x[:, 2:T], ALU.add),
                reads=xT1_keys, writes=["sh"])
            p.op("dve", lambda v, x=xf: v.tensor_copy(sh[:, 0:1], x[:, 1:2]),
                 reads=xT1_keys, writes=["sh"])
            p.op("dve", lambda v, x=xf: v.tensor_copy(
                sh[:, T - 1:T], x[:, T - 2:T - 1]),
                reads=xT1_keys, writes=["sh"])
            p.op("dve", lambda v, x=xf, f=fc: v.scalar_tensor_tensor(
                dxTb[:, f * T:(f + 1) * T], sh[:], 0.5, x,
                ALU.mult, ALU.subtract),
                reads=["sh"], writes=[("dxTb", fc)])
        dx_keys = [("dxTb", fc) for fc in range(NF)]

        AR.free("sh")
        mixb = AR.place("mixb", 33792, 8192, BF16)
        for fc in range(NF):
            p.op("dve", lambda v, f=fc: v.scalar_tensor_tensor(
                mixb[:, f * T:(f + 1) * T], dxTb[:, f * T:(f + 1) * T],
                maav[:, f * 7:f * 7 + 1], xTb1[:, f * T:(f + 1) * T],
                ALU.mult, ALU.add),
                reads=dx_keys + xT1_keys + ["maav"], writes=[("mixb", fc)])
        mix_keys = [("mixb", fc) for fc in range(NF)]
        for mt, bank, bkey, M in ((0, PB[2], ("PB", 2), 128), (1, PB[3], ("PB", 3), 32)):
            for ns in range(4):
                for dc in range(NF):
                    p.op("pe", lambda t, m=mt, n=ns, d=dc, b=bank, MM=M:
                         t.matmul(b[0:MM, 0:512],
                                  w1maa_s[:, d * 160 + m * 128:d * 160 + m * 128 + MM],
                                  mixb[:, d * T + n * 512:d * T + (n + 1) * 512],
                                  start=(d == 0), stop=(d == NF - 1)),
                         reads=[("w1maa", dc)] + (mix_keys if dc == 0 else []),
                         writes=[bkey])
                dst = xxa if mt == 0 else xxb
                p.op("act", lambda s, n=ns, b=bank, MM=M, d2=dst: s.activation(
                    d2[0:MM, n * 512:(n + 1) * 512], b[0:MM, 0:512], AF.Tanh),
                    reads=[bkey], writes=[("xx", mt, ns)])
        xx_keys = [("xx", m, n) for m in range(2) for n in range(4)]

        stg = AR.place("stg", 32768 - 0 + 0, 0) if False else None
        AR_stage = AR.place("stage", 32768, 1024, BF16)
        for j, nm in enumerate(["w", "k", "v", "r", "g"]):
            if j < 4:
                lh = w2pk_s
                lbase = 32 * j
                rsrc = xxa
                lkey, rkeys = "w2pk", [("xx", 0, n) for n in range(4)]
            else:
                lh = w2p4_s
                lbase = 0
                rsrc = xxb
                lkey, rkeys = "w2p4", [("xx", 1, n) for n in range(4)]
            for fc in range(NF):
                for ns in range(4):
                    bank = PB[4 + (ns % 2)]
                    bkey = ("PB", 4 + (ns % 2))
                    p.op("pe", lambda t, f=fc, n=ns, b=bank, l=lh, lb=lbase, r=rsrc:
                         t.matmul(b[:, 0:512],
                                  l[lb:lb + 32, f * 128:(f + 1) * 128],
                                  r[lb:lb + 32, n * 512:(n + 1) * 512],
                                  start=True, stop=True,
                                  tile_position=(lb, 0)),
                         reads=[lkey] + rkeys, writes=[bkey])
                    p.op("dve", lambda v, f=fc, n=ns, b=bank, jj=j:
                         v.scalar_tensor_tensor(
                             msc[:],
                             b[:, 0:512],
                             maav[:, f * 7 + 1 + jj:f * 7 + 2 + jj],
                             dxTb[:, f * T + n * 512:f * T + (n + 1) * 512],
                             ALU.add, ALU.mult),
                         reads=[bkey, "maav"] + dx_keys, writes=["msc"])
                    p.op("dve", lambda v, f=fc, n=ns: v.tensor_tensor(
                        mixb[:, f * T + n * 512:f * T + (n + 1) * 512],
                        msc[:], xTb1[:, f * T + n * 512:f * T + (n + 1) * 512],
                        ALU.add),
                        reads=["msc"] + xT1_keys, writes=[("mixb", fc)])
            if j == 0:
                for ns in range(4):
                    bank, bkey = PB[0], ("PB", 0)
                    for dc in range(NF):
                        p.op("pe", lambda t, n=ns, d=dc, b=bank: t.matmul(
                            b[0:64, 0:512],
                            d1_s[:, d * 64:(d + 1) * 64],
                            mixb[:, d * T + n * 512:d * T + (n + 1) * 512],
                            start=(d == 0), stop=(d == NF - 1)),
                            reads=[("d1s", dc)] + mix_keys, writes=[bkey])
                    p.op("act", lambda s, n=ns, b=bank: s.activation(
                        t1s[0:64, n * 512:(n + 1) * 512], b[0:64, 0:512],
                        AF.Tanh),
                        reads=[bkey], writes=[("t1s", ns)])
            else:
                wsrc = {"k": l1_wk, "v": l1_wv, "r": l1_wr, "g": l1_wg}[nm]
                for dc in range(NF):
                    p.dma("gp", wjs[:, dc * D:(dc + 1) * D],
                          wsrc[dc * 128:(dc + 1) * 128, :],
                          writes=[("wjs", dc)])
                for mt in range(NF):
                    for ns in range(4):
                        bank = PB[(mt * 4 + ns) % 2]
                        bkey = ("PB", (mt * 4 + ns) % 2)
                        for dc in range(NF):
                            p.op("pe", lambda t, m=mt, n=ns, d=dc, b=bank:
                                 t.matmul(b[:, 0:512],
                                          wjs[:, d * D + m * 128:d * D + (m + 1) * 128],
                                          mixb[:, d * T + n * 512:d * T + (n + 1) * 512],
                                          start=(d == 0), stop=(d == NF - 1)),
                                 reads=[("wjs", dc)] + (mix_keys if dc == 0 else []),
                                 writes=[bkey])
                        if nm == "g":
                            p.op("act", lambda s, n=ns, b=bank: s.activation(
                                AR_stage[:, n * 512:(n + 1) * 512],
                                b[:, 0:512], AF.Silu),
                                reads=[bkey], writes=[("stage", ns)])
                        else:
                            p.op("dve", lambda v, n=ns, b=bank: v.tensor_copy(
                                AR_stage[:, n * 512:(n + 1) * 512], b[:, 0:512]),
                                reads=[bkey], writes=[("stage", ns)])
                    p.dma("sp", spill_rkvg[nm][mt * 128:(mt + 1) * 128, :],
                          AR_stage[:],
                          reads=[("stage", 0), ("stage", 1), ("stage", 2),
                                 ("stage", 3)])

        AR.free("xTb1", "dxTb", "mixb", "stage", "w1maa", "w2pk", "w2p4",
                "d1s", "xxa", "xxb", "msc", "stream")
        rg_s = AR.place("rg", 0, 1024, BF16)
        kg_s = AR.place("kg", 1024, 1024, BF16)
        vg_s = AR.place("vg", 2048, 1024, BF16)
        wg_s = AR.place("wg", 3072, 2048)
        cs_s = AR.place("cs", 5120, 2048)
        cb_s = AR.place("cb", 7168, 2048)
        eff = AR.place("eff", 9216, 2048)
        rfb = AR.place("rfb", 11264, 1024, BF16)
        rbb = AR.place("rbb", 12288, 1024, BF16)
        kfT = AR.place("kfT", 13312, 1024, BF16)
        kbT = AR.place("kbT", 14336, 1024, BF16)
        kf_tok = AR.place("kf_tok", 15360, 1024, BF16)
        kb_tok = AR.place("kb_tok", 16384, 1024, BF16)
        v_tok = AR.place("v_tok", 17408, 1024, BF16)
        sS = AR.place("sS", 18432, 32, BF16)
        sSf = AR.place("sSf", 18464, 64)
        merged = AR.place("merged", 18528, 128, BF16)
        yb = AR.place("yb", 18656, 1024, BF16)
        gnsc = AR.place("gnsc", 19680, 96)
        gg_s = AR.place("gg", 19776, 1024, BF16)
        onesT = AR.place("onesT", 20800, 1024, BF16)
        lnw_s = AR.place("lnw", 21824, 512, BF16)
        lnb_s = AR.place("lnb", 22336, 512, BF16)
        sfr = AR.place("sfr", 22848, 64)
        sbr = AR.place("sbr", 22912, 64)
        Sfp = AR.place("Sfp", 22976, 512, BF16)
        Sbp = AR.place("Sbp", 23488, 512, BF16)
        merged2 = AR.place("merged2", 24000, 128, BF16)
        yngT = AR.place("yngT", 24576, 8192, BF16)

        p.op("dve", lambda v: v.memset(onesT[:], 1.0), writes=["onesT"])
        p.dma("gp", lnw_s[:], l1_lnw[:], writes=["lnw"])
        p.dma("gp", lnb_s[:], l1_lnb[:], writes=["lnb"])
        trilm = sbt("trilm_t", [128, 128], U8).ap()
        p.dma("gp", trilm[:], l1_masks[:], writes=["trilm"])

        for g in range(8):
            for nm, dst, key in (("r", rg_s, "rg"), ("k", kg_s, "kg"),
                                 ("v", vg_s, "vg"), ("g", gg_s, "gg")):
                p.dma("sp", dst[:], spill_rkvg[nm][g * 128:(g + 1) * 128, :],
                      writes=[key])
            for ns in range(4):
                bank, bkey = PB[0], ("PB", 0)
                p.op("pe", lambda t, n=ns, b=bank, gg_=g: t.matmul(
                    b[:, 0:512],
                    d2_s[0:64, gg_ * 128:(gg_ + 1) * 128],
                    t1s[0:64, n * 512:(n + 1) * 512],
                    start=True, stop=True),
                    reads=["d2s", ("t1s", ns)], writes=[bkey])
                p.op("act", lambda s, n=ns, b=bank, gg_=g: s.activation(
                    eff[:, n * 512:(n + 1) * 512], b[:, 0:512],
                    AF.Exp, bias=maav[:, gg_ * 7 + 6:gg_ * 7 + 7]),
                    reads=[bkey, "maav"], writes=["eff"])
            p.op("dve", lambda v: v.tensor_tensor_scan(
                cs_s[:], onesT[:], eff[:], 0.0, ALU.mult, ALU.subtract),
                reads=["eff", "onesT"], writes=["cs"])
            p.op("dve", lambda v: v.tensor_tensor(
                cb_s[:], cs_s[:], eff[:], ALU.add),
                reads=["cs", "eff"], writes=["cb"])
            p.op("dve", lambda v: v.tensor_scalar(
                cs_s[:], cs_s[:], cs_s[:, 1024:1025], 60.0,
                ALU.subtract, ALU.min),
                reads=[], writes=["cs"])
            p.op("dve", lambda v: v.tensor_scalar_max(cs_s[:], cs_s[:], -60.0),
                 reads=[], writes=["cs"])
            p.op("dve", lambda v: v.tensor_scalar(
                cb_s[:], cb_s[:], cb_s[:, 1024:1025], 60.0,
                ALU.subtract, ALU.min),
                reads=[], writes=["cb"])
            p.op("dve", lambda v: v.tensor_scalar_max(cb_s[:], cb_s[:], -60.0),
                 reads=[], writes=["cb"])
            p.op("act", lambda s: s.activation(eff[:], cs_s[:], AF.Exp),
                 reads=["cs"], writes=["eff"])
            p.op("act", lambda s: s.activation(wg_s[:], cs_s[:], AF.Exp,
                                               scale=-1.0),
                 reads=["cs"], writes=["wg"])
            p.op("dve", lambda v: v.tensor_tensor(rfb[:], rg_s[:], eff[:],
                                                  ALU.mult),
                 reads=["rg", "eff"], writes=["rfb"])
            p.op("dve", lambda v: v.tensor_tensor(kfT[:], kg_s[:], wg_s[:],
                                                  ALU.mult),
                 reads=["kg", "wg"], writes=["kfT"])
            p.op("act", lambda s: s.activation(eff[:], cb_s[:], AF.Exp,
                                               scale=-1.0),
                 reads=["cb"], writes=["eff"])
            p.op("act", lambda s: s.activation(wg_s[:], cb_s[:], AF.Exp),
                 reads=["cb"], writes=["wg"])
            p.op("dve", lambda v: v.tensor_tensor(rbb[:], rg_s[:], eff[:],
                                                  ALU.mult),
                 reads=["rg", "eff"], writes=["rbb"])
            p.op("dve", lambda v: v.tensor_tensor(kbT[:], kg_s[:], wg_s[:],
                                                  ALU.mult),
                 reads=["kg", "wg"], writes=["kbT"])
            for srcb, skey, dst, dkey in ((kfT, "kfT", kf_tok, "kf_tok"),
                                          (kbT, "kbT", kb_tok, "kb_tok"),
                                          (vg_s, "vg", v_tok, "v_tok")):
                for m in range(NT):
                    tp128(dst[:, m * 128:(m + 1) * 128],
                          srcb[:, m * 128:(m + 1) * 128],
                          [skey], [(dkey, m)], ROT_C)
            kfk = [("kf_tok", m) for m in range(NT)]
            kbk = [("kb_tok", m) for m in range(NT)]
            vk = [("v_tok", m) for m in range(NT)]

            p.op("dve", lambda v: v.memset(sfr[:], 0.0), writes=["sfr"])
            p.op("dve", lambda v: v.memset(sbr[:], 0.0), writes=["sbr"])
            zf_banks = [(PSM, ("PSM",)), (PB[0], ("PB", 0))]
            zb_banks = [(PB[2], ("PB", 2)), (PB[5], ("PB", 5))]

            def tok_ap(buf, m0, h):
                return bass.AP(buf.tensor, buf.offset + m0 + h,
                               [[buf.ap[0][0], 128], [1, 64]])

            for m in range(NT):
                mb = NT - 1 - m
                mm0 = m * 128
                abank = PB[3] if m % 2 == 0 else PB[4]
                akey = ("PB", 3) if m % 2 == 0 else ("PB", 4)
                ybank = PB[1] if m % 2 == 0 else PTRf
                ykey = ("PB", 1) if m % 2 == 0 else ("PTR",)
                mrg = merged if m % 2 == 0 else merged2
                mk = "merged" if m % 2 == 0 else "merged2"
                for hh in range(2):
                    hb = hh * 64
                    co = hh * 256
                    p.op("pe", lambda t, h=hb, c=co, m0=mm0, b=abank: t.matmul(
                        b[0:64, c:c + 64],
                        kfT[h:h + 64, m0:m0 + 64],
                        rfb[h:h + 64, m0:m0 + 64],
                        start=True, stop=True, tile_position=(h, 0)),
                        reads=["kfT", "rfb"], writes=[akey])
                    p.op("pe", lambda t, h=hb, c=co, m0=mm0, b=abank: t.matmul(
                        b[0:128, c + 64:c + 128],
                        kfT[h:h + 64, m0:m0 + 128],
                        rfb[h:h + 64, m0 + 64:m0 + 128],
                        start=True, stop=True, tile_position=(h, 0)),
                        reads=["kfT", "rfb"], writes=[akey])
                    p.op("pe", lambda t, h=hb, c=co, m0=mm0, b=abank: t.matmul(
                        b[0:64, c + 128:c + 192],
                        kbT[h:h + 64, m0:m0 + 64],
                        rbb[h:h + 64, m0:m0 + 64],
                        start=True, stop=True, tile_position=(h, 0)),
                        reads=["kbT", "rbb"], writes=[akey])
                    p.op("pe", lambda t, h=hb, c=co, m0=mm0, b=abank: t.matmul(
                        b[64:128, c + 128:c + 192],
                        kbT[h:h + 64, m0 + 64:m0 + 128],
                        rbb[h:h + 64, m0:m0 + 64],
                        start=True, stop=True, tile_position=(h, 64)),
                        reads=["kbT", "rbb"], writes=[akey])
                    p.op("pe", lambda t, h=hb, c=co, m0=mm0, b=abank: t.matmul(
                        b[64:128, c + 192:c + 256],
                        kbT[h:h + 64, m0 + 64:m0 + 128],
                        rbb[h:h + 64, m0 + 64:m0 + 128],
                        start=True, stop=True, tile_position=(h, 64)),
                        reads=["kbT", "rbb"], writes=[akey])
                    mo = hh * 128
                    p.op("dve", lambda v, c=co, o=mo, b=abank, g_=mrg:
                         v.tensor_copy(g_[0:128, o:o + 128],
                                       b[0:128, c:c + 128]),
                         reads=[akey], writes=[(mk, hh)])
                    p.op("dve", lambda v, c=co, o=mo, b=abank, g_=mrg:
                         v.tensor_copy(g_[64:128, o:o + 64],
                                       b[64:128, c + 128:c + 192]),
                         reads=[akey], writes=[(mk, hh)])
                    p.op("dve", lambda v, c=co, o=mo, b=abank, g_=mrg:
                         v.copy_predicated(
                             g_[0:64, o:o + 64], trilm[0:64, 64:128],
                             b[0:64, c + 128:c + 192]),
                         reads=[akey, "trilm"], writes=[(mk, hh)])
                    p.op("dve", lambda v, c=co, o=mo, b=abank, g_=mrg:
                         v.copy_predicated(
                             g_[64:128, o + 64:o + 128], trilm[64:128, 64:128],
                             b[64:128, c + 192:c + 256]),
                         reads=[akey, "trilm"], writes=[(mk, hh)])
                    p.op("pe", lambda t, h=hb, o=mo, m0=mm0, b=ybank, g_=mrg,
                         st=(hh == 0), sp=(hh == 1): t.matmul(
                        b[:, h:h + 64],
                        g_[:, o:o + 128],
                        tok_ap(v_tok, m0, h),
                        start=st, stop=sp),
                        reads=[(mk, hh)] + vk, writes=[ykey])
                p.op("dve", lambda v, m0=mm0, b=ybank: v.tensor_copy(
                    yb[:, m0:m0 + 128], b[:, 0:128]),
                    reads=[ykey], writes=[("yb", m)])
                p.op("dve", lambda v, m0=mm0: v.tensor_copy(
                    Sfp[:, m0 // 2:m0 // 2 + 64], sfr[:]),
                    reads=["sfr"], writes=["Sfp"])
                for hh in range(2):
                    hb = hh * 64
                    zb_, zk = zf_banks[hh]
                    p.op("pe", lambda t, h=hb, m0=mm0, b=zb_: t.matmul(
                        b[h:h + 64, 0:64],
                        tok_ap(kf_tok, m0, h),
                        tok_ap(v_tok, m0, h),
                        start=True, stop=True, tile_position=(0, h)),
                        reads=kfk + vk, writes=[zk])
                    p.op("dve", lambda v, h=hb, b=zb_: v.tensor_tensor(
                        sfr[h:h + 64, :], sfr[h:h + 64, :], b[h:h + 64, 0:64],
                        ALU.add),
                        reads=[zk], writes=["sfr"])
                p.op("dve", lambda v, m0=mb * 128: v.tensor_copy(
                    Sbp[:, m0 // 2:m0 // 2 + 64], sbr[:]),
                    reads=["sbr"], writes=["Sbp"])
                for hh in range(2):
                    hb = hh * 64
                    zb_, zk = zb_banks[hh]
                    p.op("pe", lambda t, h=hb, m0=mb * 128, b=zb_: t.matmul(
                        b[h:h + 64, 0:64],
                        tok_ap(kb_tok, m0, h),
                        tok_ap(v_tok, m0, h),
                        start=True, stop=True, tile_position=(0, h)),
                        reads=kbk + vk, writes=[zk])
                    p.op("dve", lambda v, h=hb, b=zb_: v.tensor_tensor(
                        sbr[h:h + 64, :], sbr[h:h + 64, :], b[h:h + 64, 0:64],
                        ALU.add),
                        reads=[zk], writes=["sbr"])

            xb_h = [[(PB[3], ("PB", 3)), (PB[4], ("PB", 4))],
                    [(PSM, ("PSM",)), (PB[0], ("PB", 0))]]
            for m in range(NT):
                mm0 = m * 128
                for hh in range(2):
                    hb = hh * 64
                    xbank, xkey = xb_h[hh][m % 2]
                    if m > 0:
                        p.op("pe", lambda t, h=hb, m0=mm0, b=xbank,
                             sp=(m == NT - 1): t.matmul(
                            b[:, h:h + 64],
                            rfb[h:h + 64, m0:m0 + 128],
                            Sfp[h:h + 64, m0 // 2:m0 // 2 + 64],
                            start=True, stop=sp, tile_position=(h, 0)),
                            reads=["rfb", "Sfp"], writes=[xkey])
                    if m < NT - 1:
                        p.op("pe", lambda t, h=hb, m0=mm0, b=xbank,
                             st=(m == 0): t.matmul(
                            b[:, h:h + 64],
                            rbb[h:h + 64, m0:m0 + 128],
                            Sbp[h:h + 64, m0 // 2:m0 // 2 + 64],
                            start=st, stop=True, tile_position=(h, 0)),
                            reads=["rbb", "Sbp"], writes=[xkey])
                    p.op("dve", lambda v, h=hb, m0=mm0, b=xbank:
                         v.tensor_tensor(
                             yb[:, m0 + h:m0 + h + 64],
                             yb[:, m0 + h:m0 + h + 64],
                             b[:, h:h + 64], ALU.add),
                         reads=[xkey, ("yb", m)], writes=[("yb", m)])
            yb_keys = [("yb", m) for m in range(NT)]
            p.op("dve", lambda v: v.tensor_reduce(
                gnsc[:, 0:32],
                bass.AP(yb.tensor, yb.offset,
                        [[yb.ap[0][0], 128], [128, 16], [64, 2], [1, 64]]),
                AX.X, ALU.add),
                reads=yb_keys, writes=["gn0"])
            p.op("act", lambda s: s.activation(eff[:], yb[:], AF.Square),
                 reads=yb_keys, writes=["eff"])
            p.op("dve", lambda v: v.tensor_reduce(
                gnsc[:, 32:64],
                bass.AP(eff.tensor, eff.offset,
                        [[eff.ap[0][0], 128], [128, 16], [64, 2], [1, 64]]),
                AX.X, ALU.add),
                reads=["eff"], writes=["gn1"])
            p.op("dve", lambda v: v.tensor_scalar_mul(
                gnsc[:, 0:32], gnsc[:, 0:32], 1.0 / 64),
                reads=[], writes=["gn0"])
            p.op("dve", lambda v: v.tensor_tensor(
                gnsc[:, 64:96], gnsc[:, 0:32], gnsc[:, 0:32], ALU.mult),
                reads=["gn0"], writes=["gn2"])
            p.op("dve", lambda v: v.scalar_tensor_tensor(
                gnsc[:, 32:64], gnsc[:, 32:64], 1.0 / 64, gnsc[:, 64:96],
                ALU.mult, ALU.subtract),
                reads=["gn2"], writes=["gn1"])
            p.op("dve", lambda v: v.tensor_scalar_add(
                gnsc[:, 32:64], gnsc[:, 32:64], 64e-5),
                reads=[], writes=["gn1"])
            p.op("dve", lambda v: v.reciprocal(gnsc[:, 32:64], gnsc[:, 32:64]),
                 reads=[], writes=["gn1"])
            p.op("act", lambda s: s.activation(
                gnsc[:, 32:64], gnsc[:, 32:64], AF.Sqrt),
                reads=[], writes=["gn1"])
            mu_bc = bass.AP(gnsc.tensor, gnsc.offset,
                            [[gnsc.ap[0][0], 128], [2, 16], [1, 2], [0, 64]])
            rho_bc = bass.AP(gnsc.tensor, gnsc.offset + 32,
                             [[gnsc.ap[0][0], 128], [2, 16], [1, 2], [0, 64]])
            lnw_bc = bass.AP(lnw_s.tensor, lnw_s.offset + g * 128,
                             [[lnw_s.ap[0][0], 128], [0, 16], [64, 2], [1, 64]])
            lnb_bc = bass.AP(lnb_s.tensor, lnb_s.offset + g * 128,
                             [[lnb_s.ap[0][0], 128], [0, 16], [64, 2], [1, 64]])
            p.op("dve", lambda v: v.tensor_tensor(yb[:], yb[:], mu_bc,
                                                  ALU.subtract),
                 reads=["gn0"], writes=yb_keys)
            p.op("dve", lambda v: v.tensor_tensor(yb[:], yb[:], rho_bc,
                                                  ALU.mult),
                 reads=["gn1"], writes=yb_keys)
            p.op("dve", lambda v: v.tensor_tensor(yb[:], yb[:], lnw_bc,
                                                  ALU.mult),
                 reads=["lnw"], writes=yb_keys)
            p.op("dve", lambda v: v.tensor_tensor(yb[:], yb[:], lnb_bc,
                                                 ALU.add),
                 reads=["lnb"], writes=yb_keys)
            for m in range(NT):
                bank, bkey = ROT_B.nxt()
                p.op("pe", lambda t, m0=m * 128, b=bank: t.transpose(
                    b[:, 0:128], yb[:, m0:m0 + 128], ident[:]),
                    reads=yb_keys + ["ident"], writes=[bkey])
                p.op("dve", lambda v, m0=m * 128, gg_=g, b=bank: v.tensor_tensor(
                    yngT[:, gg_ * T + m0:gg_ * T + m0 + 128],
                    b[:, 0:128], gg_s[:, m0:m0 + 128], ALU.mult),
                    reads=[bkey, "gg"], writes=[("yngT", g, m)])

        AR.free("rg", "kg", "vg", "wg", "cs", "cb", "eff", "rfb", "rbb",
                "kfT", "kbT", "kf_tok", "kb_tok", "v_tok", "sS", "merged",
                "yb", "gnsc", "gg", "onesT", "lnw", "lnb", "sSf",
                "sfr", "sbr", "Sfp", "Sbp", "merged2")
        ytmp2 = AR.place("ytmp2", 0, 8192, BF16)
        stream2 = AR.place("stream2", 8192, 16384)
        yng_keys = [("yngT", g, m) for g in range(8) for m in range(NT)]
        for dc in range(NF):
            p.dma("gp", wjs[:, dc * D:(dc + 1) * D],
                  l1_wo[dc * 128:(dc + 1) * 128, :], writes=[("wjs", dc)])
        for tt in range(NT):
            p.dma("sp", stream2[:, tt * D:(tt + 1) * D],
                  spill_x[tt * 128:(tt + 1) * 128, :], writes=[("stream2", tt)])
        for tt in range(NT):
            for ns in range(2):
                bank = PB[ns]
                bkey = ("PB", ns)
                for fc in range(NF):
                    rd = [("wjs", fc)]
                    if fc == 0:
                        rd = rd + yng_keys
                    p.op("pe", lambda t, c=tt, n=ns, f=fc, b=bank: t.matmul(
                        b[:, 0:512],
                        yngT[:, f * T + c * 128:f * T + (c + 1) * 128],
                        wjs[:, f * D + n * 512:f * D + (n + 1) * 512],
                        start=(f == 0), stop=(f == NF - 1)),
                        reads=rd, writes=[bkey])
                p.op("dve", lambda v, c=tt, n=ns, b=bank: v.tensor_copy(
                    ytmp2[:, c * D + n * 512:c * D + (n + 1) * 512],
                    b[:, 0:512]),
                    reads=[bkey], writes=[("ytmp2", tt, ns)])
        ln_stats(lambda tt: [("ytmp2", tt, 0), ("ytmp2", tt, 1)],
                 lambda tt: ytmp2[:, tt * D:(tt + 1) * D], muB, rhoB,
                 "muB", "rhoB")
        for tt in range(NT):
            p.op("dve", lambda v, c=tt: v.tensor_scalar(
                ytmp2[:, c * D:(c + 1) * D], ytmp2[:, c * D:(c + 1) * D],
                muB[:, c:c + 1], rhoB[:, c:c + 1], ALU.subtract, ALU.mult),
                reads=[("muB", tt), ("rhoB", tt)],
                writes=[("ytmp2", tt, 0), ("ytmp2", tt, 1), ("ya3", tt)])
            p.op("dve", lambda v, c=tt: v.tensor_tensor(
                stream2[:, c * D:(c + 1) * D], stream2[:, c * D:(c + 1) * D],
                ytmp2[:, c * D:(c + 1) * D], ALU.add),
                reads=[("ya3", tt)], writes=[("stream2", tt)])
            p.dma("sp", spill_x2[tt * 128:(tt + 1) * 128, :],
                  stream2[:, tt * D:(tt + 1) * D], reads=[("stream2", tt)])
        STREAM_FINAL = ("stream2", stream2)
    else:
        STREAM_FINAL = ("stream", stream)

    DO_L1F = DO_L1A and build_until not in ("l1a",)
    if DO_L1F:
        NX1 = T * D
        NH2 = T * 4096
        ix1 = int(np.floor(0.995 * (8 * NX1 - 1)))
        fr1 = 0.995 * (8 * NX1 - 1) - ix1
        ka1 = 8 * NX1 - 1 - ix1 + 1
        kb1 = ka1 - 1
        ix2 = int(np.floor(0.995 * (8 * NH2 - 1)))
        fr2 = 0.995 * (8 * NH2 - 1) - ix2
        ka2 = 8 * NH2 - 1 - ix2 + 1
        kb2 = ka2 - 1

        AR.free("ytmp2", "yngT", "wjs", "d2s", "t1s", "maav")
        cc_in = nc.dram_tensor("cc_in", [1, 16], F32).ap()
        cc_out = nc.dram_tensor("cc_out", [1, 16], F32).ap()
        ag_in = nc.dram_tensor("ag_in", [128, 32], F32).ap()
        ag_out = nc.dram_tensor("ag_out", [1024, 32], F32).ap()

        xq = AR.place("xq", 24576, 16384)
        cmpscr = AR.place("cmpscr", 0, 8192, BF16)
        qs2 = AR.place("qs2", 49152, 32)
        qsm8 = AR.place("qsm", 49184, 1).bitcast(U8)
        gath = AR.place("gath", 49188, 256)
        gsf = AR.place("gscr2", 49444, 240)
        hscr = AR.place("hscr", 49684, 64, BF16)
        lnwf = AR.place("lnwf", 49748, 512, BF16)
        lnbf = AR.place("lnbf", 50260, 512, BF16)
        qsc = AR.place("qsc", 50772, 1024)

        p.dma("gp", lnwf[:], l1f_lnw[:], writes=["lnwf"])
        p.dma("gp", lnbf[:], l1f_lnb[:], writes=["lnbf"])


        ln_stats(lambda tt: [("stream2", tt)],
                 lambda tt: stream2[:, tt * D:(tt + 1) * D], muA, rhoA,
                 "muA", "rhoA")
        for tt in range(NT):
            p.op("dve", lambda v, c=tt: v.tensor_scalar(
                xq[:, c * D:(c + 1) * D], stream2[:, c * D:(c + 1) * D],
                muA[:, c:c + 1], rhoA[:, c:c + 1], ALU.subtract, ALU.mult),
                reads=[("stream2", tt), ("muA", tt), ("rhoA", tt)],
                writes=[("xq", tt)])
            p.op("dve", lambda v, c=tt: v.tensor_tensor(
                xq[:, c * D:(c + 1) * D], xq[:, c * D:(c + 1) * D],
                lnwf[:], ALU.mult),
                reads=["lnwf"], writes=[("xq", tt)])
            p.op("dve", lambda v, c=tt: v.tensor_tensor(
                xq[:, c * D:(c + 1) * D], xq[:, c * D:(c + 1) * D],
                lnbf[:], ALU.add),
                reads=["lnbf"], writes=[("xq", tt)])
        AR.free("stream2")
        absx = AR.place("absx", 8192, 16384)
        for tt in range(NT):
            p.op("act", lambda s, c=tt: s.activation(
                absx[:, c * D:(c + 1) * D], xq[:, c * D:(c + 1) * D],
                AF.Abs),
                reads=[("xq", tt)], writes=[("absx", tt)])
        xq_keys = [("xq", tt) for tt in range(NT)]
        absx_keys = [("absx", tt) for tt in range(NT)]

        def psum_scalar(src_col, dst_col, bank=None, bkey=None):
            if bank is None:
                bank, bkey = PSM, ("PSM",)
            p.op("pe", lambda t, b=bank, sc=src_col: t.matmul(
                b[:, 0:1], ones_mat[:], qs2[:, sc:sc + 1],
                start=True, stop=True),
                reads=[("qs2", src_col), "ones_mat"], writes=[bkey])
            p.op("dve", lambda v, b=bank, dcol=dst_col: v.tensor_copy(
                qs2[:, dcol:dcol + 1], b[:, 0:1]),
                reads=[bkey], writes=[("qs2", dst_col)])

        def allreduce_scalar(col_from, col_to, op=ALU.add):
            p.dma("gp", cc_in[0:1, 0:1], qs2[0:1, col_from:col_from + 1],
                  reads=[("qs2", col_from)], writes=["cc_in"])
            p.op("gp", lambda e, o=op: e.collective_compute(
                "AllReduce", o, replica_groups=[list(range(8))],
                ins=[cc_in.tensor.ap().opt()], outs=[cc_out.tensor.ap().opt()]),
                reads=["cc_in"], writes=["cc_out"])
            p.dma("gp", qs2[0:1, 28:29],
                  cc_out[0:1, 0:1],
                  reads=["cc_out"], writes=[("qs2", 28)])
            p.op("pe", lambda t: t.matmul(
                PSM[:, 2:3], ones_row[0:1, 0:128], qs2[0:1, 28:29],
                start=True, stop=True),
                reads=[("qs2", 28), "ones_row"], writes=[("PSM",)])
            p.op("dve", lambda v, dcol=col_to: v.tensor_copy(
                qs2[:, dcol:dcol + 1], PSM[:, 2:3]),
                reads=[("PSM",)], writes=[("qs2", col_to)])

        def midpoint(lo, hi, tmp, tc):
            p.op("dve", lambda v, a=lo, b_=hi, c=tmp: v.tensor_tensor(
                qs2[:, c:c + 1], qs2[:, a:a + 1], qs2[:, b_:b_ + 1], ALU.add),
                reads=[("qs2", lo), ("qs2", hi)], writes=[("qs2", tmp)])
            p.op("dve", lambda v, c=tmp, d=tc: v.tensor_scalar_mul(
                qs2[:, d:d + 1], qs2[:, c:c + 1], 0.5),
                reads=[("qs2", tmp)], writes=[("qs2", tc)])

        def update_lohi(cnt_col, k, lo, hi, tc, m0, m1, track_cnthi):
            p.op("dve", lambda v, c=cnt_col, kk=float(k), m=m0: v.tensor_scalar(
                qsm8[:, m:m + 1], qs2[:, c:c + 1], kk, None, ALU.is_ge),
                reads=[("qs2", cnt_col)], writes=[("qsm", m0)])
            p.op("dve", lambda v, c=cnt_col, kk=float(k), m=m1: v.tensor_scalar(
                qsm8[:, m:m + 1], qs2[:, c:c + 1], kk, None, ALU.is_lt),
                reads=[("qs2", cnt_col)], writes=[("qsm", m1)])
            p.op("dve", lambda v, a=lo, d=tc, m=m0: v.copy_predicated(
                qs2[:, a:a + 1], qsm8[:, m:m + 1], qs2[:, d:d + 1]),
                reads=[("qsm", m0), ("qs2", tc)], writes=[("qs2", lo)])
            p.op("dve", lambda v, a=hi, d=tc, m=m1: v.copy_predicated(
                qs2[:, a:a + 1], qsm8[:, m:m + 1], qs2[:, d:d + 1]),
                reads=[("qsm", m1), ("qs2", tc)], writes=[("qs2", hi)])
            if track_cnthi:
                p.op("dve", lambda v, c=cnt_col, m=m1: v.copy_predicated(
                    qs2[:, 4:5], qsm8[:, m:m + 1], qs2[:, c:c + 1]),
                    reads=[("qsm", m1), ("qs2", cnt_col)], writes=[("qs2", 4)])

        def head_rounds(data_ap, data_keys, nH, mbuf, mname, k_head,
                        hi0, rounds, filler):
            p.op("dve", lambda v: v.memset(qs2[:, 0:1], 0.0),
                 writes=[("qs2", 0)])
            p.op("dve", lambda v, h0=hi0: v.memset(qs2[:, 1:2], h0),
                 writes=[("qs2", 1)])
            p.op("dve", lambda v: v.memset(qs2[:, 4:5], 0.0),
                 writes=[("qs2", 4)])
            for r in range(rounds):
                midpoint(0, 1, 2, 16)
                for hh in range(nH):
                    par = hh % 2
                    p.op("dve", lambda v, h=hh, pr=par: v.tensor_scalar(
                        mbuf[:, pr * 8192:(pr + 1) * 8192],
                        data_ap[:, h * 8192:(h + 1) * 8192],
                        qs2[:, 16:17], None, ALU.is_gt),
                        reads=data_keys + [("qs2", 16)],
                        writes=[(mname, par)])
                    for j in range(16):
                        p.op("pe", lambda t, pr=par, jj=j,
                             st=(hh == 0 and j == 0),
                             sp=(hh == nH - 1 and j == 15): t.matmul(
                            PSM[0:1, 0:512], ones_bf[:, 0:1],
                            mbuf[:, pr * 8192 + jj * 512:
                                 pr * 8192 + (jj + 1) * 512],
                            start=st, stop=sp),
                            reads=[(mname, par), "ones_bf"],
                            writes=[("PSM",)])
                p.op("dve", lambda v: v.tensor_reduce(
                    qs2[0:1, 18:19], PSM[0:1, 0:512], AX.X, ALU.add),
                    reads=[("PSM",)], writes=[("qs2", 18)])
                allreduce_scalar(18, 3)
                update_lohi(3, k_head, 0, 1, 16, 0, 1, True)
                if filler is not None:
                    filler(r)
            p.op("dve", lambda v: v.tensor_copy(qs2[:, 16:17], qs2[:, 1:2]),
                 reads=[("qs2", 1)], writes=[("qs2", 16)])

        def head_rounds3(data_ap, data_keys, nH, mbuf, mname, k_head,
                         hi0, rounds, filler):
            TCOL = (10, 16, 13)
            CCOL = (18, 19, 20)
            RCOL = (3, 2, 7)
            p.op("dve", lambda v: v.memset(qs2[:, 0:1], 0.0),
                 writes=[("qs2", 0)])
            p.op("dve", lambda v, h0=hi0: v.memset(qs2[:, 1:2], h0),
                 writes=[("qs2", 1)])
            p.op("dve", lambda v: v.memset(qs2[:, 4:5], 0.0),
                 writes=[("qs2", 4)])
            for r in range(rounds):
                midpoint(0, 1, 2, TCOL[1])
                midpoint(0, TCOL[1], 2, TCOL[0])
                midpoint(TCOL[1], 1, 2, TCOL[2])
                for j in range(3):
                    bank = PSM if j % 2 == 0 else PB[4]
                    bkey = ("PSM",) if j % 2 == 0 else ("PB", 4)
                    for hh in range(nH):
                        par = (j * nH + hh) % 2
                        p.op("dve", lambda v, h=hh, pr=par, tc=TCOL[j]:
                             v.tensor_scalar(
                                 mbuf[:, pr * 8192:(pr + 1) * 8192],
                                 data_ap[:, h * 8192:(h + 1) * 8192],
                                 qs2[:, tc:tc + 1], None, ALU.is_gt),
                             reads=data_keys + [("qs2", TCOL[j])],
                             writes=[(mname, par)])
                        for jj in range(16):
                            p.op("pe", lambda t, pr=par, j2=jj, b=bank,
                                 st=(hh == 0 and jj == 0),
                                 sp=(hh == nH - 1 and jj == 15): t.matmul(
                                b[0:1, 0:512], ones_bf[:, 0:1],
                                mbuf[:, pr * 8192 + j2 * 512:
                                     pr * 8192 + (j2 + 1) * 512],
                                start=st, stop=sp),
                                reads=[(mname, par), "ones_bf"],
                                writes=[bkey])
                    p.op("dve", lambda v, b=bank, cc=CCOL[j]: v.tensor_reduce(
                        qs2[0:1, cc:cc + 1], b[0:1, 0:512], AX.X, ALU.add),
                        reads=[bkey], writes=[("qs2", CCOL[j])])
                p.dma("gp", cc_in[0:1, 0:3], qs2[0:1, 18:21],
                      reads=[("qs2", 18), ("qs2", 19), ("qs2", 20)],
                      writes=["cc_in"])
                p.op("gp", lambda e: e.collective_compute(
                    "AllReduce", ALU.add, replica_groups=[list(range(8))],
                    ins=[cc_in.tensor.ap().opt()],
                    outs=[cc_out.tensor.ap().opt()]),
                    reads=["cc_in"], writes=["cc_out"])
                p.dma("gp", qs2[0:1, 18:21], cc_out[0:1, 0:3],
                      reads=["cc_out"],
                      writes=[("qs2", 18), ("qs2", 19), ("qs2", 20)])
                for j in range(3):
                    p.op("pe", lambda t, cc=CCOL[j], j2=j: t.matmul(
                        PSM[:, 2 + j2:3 + j2], ones_row[0:1, 0:128],
                        qs2[0:1, cc:cc + 1], start=True, stop=True),
                        reads=[("qs2", CCOL[j]), "ones_row"],
                        writes=[("PSM",)])
                    p.op("dve", lambda v, rc=RCOL[j], j2=j: v.tensor_copy(
                        qs2[:, rc:rc + 1], PSM[:, 2 + j2:3 + j2]),
                        reads=[("PSM",)], writes=[("qs2", RCOL[j])])
                for j in range(3):
                    p.op("dve", lambda v, rc=RCOL[j], kk=float(k_head):
                         v.tensor_scalar(
                             qsm8[:, 0:1], qs2[:, rc:rc + 1], kk, None,
                             ALU.is_ge),
                         reads=[("qs2", RCOL[j])], writes=[("qsm", 0)])
                    p.op("dve", lambda v, tc=TCOL[j]: v.copy_predicated(
                        qs2[:, 0:1], qsm8[:, 0:1], qs2[:, tc:tc + 1]),
                        reads=[("qsm", 0), ("qs2", TCOL[j])],
                        writes=[("qs2", 0)])
                for j in (2, 1, 0):
                    p.op("dve", lambda v, rc=RCOL[j], kk=float(k_head):
                         v.tensor_scalar(
                             qsm8[:, 1:2], qs2[:, rc:rc + 1], kk, None,
                             ALU.is_lt),
                         reads=[("qs2", RCOL[j])], writes=[("qsm", 1)])
                    p.op("dve", lambda v, tc=TCOL[j]: v.copy_predicated(
                        qs2[:, 1:2], qsm8[:, 1:2], qs2[:, tc:tc + 1]),
                        reads=[("qsm", 1), ("qs2", TCOL[j])],
                        writes=[("qs2", 1)])
                    p.op("dve", lambda v, rc=RCOL[j]: v.copy_predicated(
                        qs2[:, 4:5], qsm8[:, 1:2], qs2[:, rc:rc + 1]),
                        reads=[("qsm", 1), ("qs2", RCOL[j])],
                        writes=[("qs2", 4)])
                if filler is not None:
                    filler(r)
            p.op("dve", lambda v: v.tensor_copy(qs2[:, 16:17], qs2[:, 1:2]),
                 reads=[("qs2", 1)], writes=[("qs2", 16)])

        def allgather_cands():
            p.dma("gp", ag_in[:, 0:32], gsf[:, 0:32],
                  reads=["gscr2"], writes=["ag_in"])
            p.op("gp", lambda e: e.collective_compute(
                "AllGather", ALU.bypass, replica_groups=[list(range(8))],
                ins=[ag_in.tensor.ap().opt()], outs=[ag_out.tensor.ap().opt()]),
                reads=["ag_in"], writes=["ag_out"])
            for r8 in range(8):
                p.dma("gp", gath[:, r8 * 32:(r8 + 1) * 32],
                      ag_out[r8 * 128:(r8 + 1) * 128, :],
                      reads=["ag_out"], writes=["gath"])

        def fine_pair(ka, kb):
            lanes = [
                dict(rank=ka, vcol=14, lo=5, hi=6, tc=10, cnt=13, tmp=7,
                     acc=19, bank=PB[5], bkey=("PB", 5), m0=2, m1=3,
                     dm=qsc[:, 256:512], rounds=27),
                dict(rank=kb, vcol=15, lo=0, hi=1, tc=16, cnt=3, tmp=2,
                     acc=18, bank=PSM, bkey=("PSM",), m0=0, m1=1,
                     dm=qsc[:, 0:256], rounds=18),
            ]
            p.op("dve", lambda v: v.memset(qs2[:, 5:6], 0.0),
                 writes=[("qs2", 5)])
            p.op("dve", lambda v: v.tensor_copy(qs2[:, 6:7], qs2[:, 1:2]),
                 reads=[("qs2", 1)], writes=[("qs2", 6)])
            for r in range(max(L["rounds"] for L in lanes)):
                for L in lanes:
                    if r >= L["rounds"]:
                        continue
                    midpoint(L["lo"], L["hi"], L["tmp"], L["tc"])
                    p.op("dve", lambda v, d=L["tc"], a=L["acc"], dm=L["dm"]:
                         v.tensor_scalar(
                             dm, gath[:, 0:256], qs2[:, d:d + 1], None,
                             ALU.is_gt, ALU.add, accum_out=qs2[:, a:a + 1]),
                         reads=["gath", ("qs2", L["tc"])],
                         writes=["qsc", ("qs2", L["acc"])])
                    psum_scalar(L["acc"], L["cnt"], L["bank"], L["bkey"])
                    p.op("dve", lambda v, c=L["cnt"]: v.tensor_tensor(
                        qs2[:, c:c + 1], qs2[:, c:c + 1], qs2[:, 4:5],
                        ALU.add),
                        reads=[("qs2", 4)], writes=[("qs2", L["cnt"])])
                    update_lohi(L["cnt"], L["rank"], L["lo"], L["hi"],
                                L["tc"], L["m0"], L["m1"], False)
            for L in lanes:
                p.op("dve", lambda v, a=L["hi"], vc=L["vcol"]: v.tensor_copy(
                    qs2[:, vc:vc + 1], qs2[:, a:a + 1]),
                    reads=[("qs2", L["hi"])], writes=[("qs2", L["vcol"])])

        def amax_tail(frac, amax_col):
            p.op("dve", lambda v, f=frac: v.tensor_scalar_mul(
                qs2[:, 7:8], qs2[:, 14:15], float(1.0 - f)),
                reads=[("qs2", 14)], writes=[("qs2", 7)])
            p.op("dve", lambda v, f=frac: v.tensor_scalar(
                qs2[:, 19:20], qs2[:, 15:16], float(f), None, ALU.mult),
                reads=[("qs2", 15)], writes=[("qs2", 19)])
            p.op("dve", lambda v: v.tensor_tensor(
                qs2[:, 7:8], qs2[:, 7:8], qs2[:, 19:20], ALU.add),
                reads=[("qs2", 19)], writes=[("qs2", 7)])
            p.op("dve", lambda v, ac=amax_col: v.tensor_scalar_max(
                qs2[:, ac:ac + 1], qs2[:, 7:8], 1e-5),
                reads=[("qs2", 7)], writes=[("qs2", amax_col)])

        def mk_filler(chunks, per_round):
            it = iter(chunks)

            def filler(r):
                for _ in range(per_round):
                    f = next(it, None)
                    if f is not None:
                        f()
            return filler

        wst = AR.place("wst", 40960, 2048)
        p.op("dve", lambda v: v.memset(qs2[:, 21:22], 0.0),
             writes=[("qs2", 21)])
        g1_chunks = []
        for ck in range(16):
            def _g1(ck=ck):
                p.dma("sp", wst[:],
                      l1f_w1t[(ck // 2) * 128:(ck // 2 + 1) * 128,
                              (ck % 2) * 2048:(ck % 2 + 1) * 2048],
                      writes=["wst"])
                p.op("dve", lambda v: v.tensor_reduce(
                    qs2[:, 22:23], wst[:], AX.X, ALU.add,
                    apply_absolute_value=True),
                    reads=["wst"], writes=[("qs2", 22)])
                p.op("dve", lambda v: v.tensor_tensor(
                    qs2[:, 21:22], qs2[:, 21:22], qs2[:, 22:23], ALU.add),
                    reads=[("qs2", 22)], writes=[("qs2", 21)])
            g1_chunks.append(_g1)

        head_rounds3(absx, absx_keys, 2, cmpscr, "cmpscr", kb1, 40.0, 5,
                     mk_filler(g1_chunks, 4))
        cmF = AR.t.ap()[0:128, 0:4096]
        for s in range(4):
            p.op("dve", lambda v, s0=s: v.scalar_tensor_tensor(
                cmF, absx[:, s0 * 4096:(s0 + 1) * 4096], qs2[:, 16:17],
                absx[:, s0 * 4096:(s0 + 1) * 4096], ALU.is_le, ALU.mult),
                reads=absx_keys + [("qs2", 16)], writes=[("cmpscr", 0)])
            p.op("dve", lambda v, s0=s: v.max(
                gsf[:, 64 + s0 * 16:64 + s0 * 16 + 8], cmF),
                reads=[("cmpscr", 0)], writes=["gscr2"])
            p.op("dve", lambda v, s0=s: v.match_replace(
                cmF, gsf[:, 64 + s0 * 16:64 + s0 * 16 + 8], cmF, -1.0),
                reads=["gscr2"], writes=[("cmpscr", 0)])
            p.op("dve", lambda v, s0=s: v.max(
                gsf[:, 64 + s0 * 16 + 8:64 + s0 * 16 + 16], cmF),
                reads=[("cmpscr", 0)], writes=["gscr2"])
        for k4 in range(4):
            p.op("dve", lambda v, k=k4: v.max(
                gsf[:, k * 8:(k + 1) * 8], gsf[:, 64:128]),
                reads=["gscr2"], writes=["gscr2"])
            if k4 < 3:
                p.op("dve", lambda v, k=k4: v.match_replace(
                    gsf[:, 64:128], gsf[:, k * 8:(k + 1) * 8],
                    gsf[:, 64:128], -1.0),
                    reads=["gscr2"], writes=["gscr2"])
        allgather_cands()
        fine_pair(ka1, kb1)
        amax_tail(fr1, 8)
        p.op("dve", lambda v: v.reciprocal(qs2[:, 9:10], qs2[:, 8:9]),
             reads=[("qs2", 8)], writes=[("qs2", 9)])
        p.op("dve", lambda v: v.tensor_scalar_mul(
            qs2[:, 9:10], qs2[:, 9:10], 127.0),
            reads=[], writes=[("qs2", 9)])
        p.op("dve", lambda v: v.tensor_copy(qs2[:, 20:21], qs2[:, 9:10]),
             reads=[("qs2", 9)], writes=[("qs2", 20)])

        p.op("dve", lambda v: v.tensor_copy(qs2[:, 18:19], qs2[:, 21:22]),
             reads=[("qs2", 21)], writes=[("qs2", 18)])
        psum_scalar(18, 3)
        p.op("dve", lambda v: v.tensor_scalar(
            qs2[:, 3:4], qs2[:, 3:4], 1.0 / (4096.0 * 1024.0), 1e-5,
            ALU.mult, ALU.add),
            reads=[("qs2", 3)], writes=[("qs2", 3)])
        p.op("dve", lambda v: v.tensor_copy(qs2[:, 23:24], qs2[:, 3:4]),
             reads=[("qs2", 3)], writes=[("qs2", 23)])
        p.op("dve", lambda v: v.tensor_scalar_mul(
            qs2[:, 24:25], qs2[:, 23:24], 0.5),
            reads=[("qs2", 23)], writes=[("qs2", 24)])
        p.op("dve", lambda v: v.tensor_scalar_mul(
            qs2[:, 26:27], qs2[:, 23:24], -0.5),
            reads=[("qs2", 23)], writes=[("qs2", 26)])
        p.op("dve", lambda v: v.tensor_tensor(
            qs2[:, 7:8], qs2[:, 23:24], qs2[:, 8:9], ALU.mult),
            reads=[("qs2", 23), ("qs2", 8)], writes=[("qs2", 7)])
        p.op("dve", lambda v: v.tensor_scalar_mul(
            qs2[:, 25:26], qs2[:, 7:8], -1.0 / 127.0),
            reads=[("qs2", 7)], writes=[("qs2", 25)])

        AR.free("cmpscr", "absx", "wst")
        qbuf = AR.place("qbuf", 0, 8192, BF16)
        va = qsc[:, 0:1024]
        iax = AR.place("iax", 40960, 1024)
        ia = iax.bitcast(I32)

        def quantize3(src_ap_fn, src_keys_fn, dst_ap_fn, dst_keys_fn, ntiles,
                      s_bc_col):
            for tt in range(ntiles):
                p.op("dve", lambda v, c=tt: v.tensor_scalar_mul(
                    va, src_ap_fn(c), qs2[:, s_bc_col:s_bc_col + 1]),
                    reads=src_keys_fn(tt) + [("qs2", s_bc_col)],
                    writes=["qsc"])
                p.op("dve", lambda v: v.tensor_copy(ia, va),
                     reads=["qsc"], writes=["iax"])
                p.op("dve", lambda v, c=tt: v.tensor_scalar(
                    dst_ap_fn(c), ia, -128.0, 127.0, ALU.max, ALU.min),
                    reads=["iax"], writes=dst_keys_fn(tt))

        quantize3(lambda c: xq[:, c * D:(c + 1) * D],
                  lambda c: [("xq", c)],
                  lambda c: qbuf[:, c * D:(c + 1) * D],
                  lambda c: [("qbuf", c)], NT, 20)
        qb_keys = [("qbuf", tt) for tt in range(NT)]

        AR.free("iax")
        qT1 = AR.place("qT1", 40960, 8192, BF16)
        for tt in range(NT):
            for fc in range(NF):
                tp128(qT1[:, fc * T + tt * 128:fc * T + (tt + 1) * 128],
                      qbuf[:, tt * D + fc * 128:tt * D + (fc + 1) * 128],
                      [("qbuf", tt)], [("qT1", fc, tt)], ROT_A)
        qT1_keys = [("qT1", fc, tt) for fc in range(NF) for tt in range(NT)]

        AR.free("xq", "qbuf")
        hb_ = AR.place("hb", 8192, 32768, BF16)
        ternq = AR.place("ternq", 0, 4096, BF16)
        wst2a = AR.place("wst2a", 4096, 1024)
        wst2b = AR.place("wst2b", 5120, 1024)

        for qb4 in range(4):
            for dc in range(NF):
                wpar = (qb4 * NF + dc) % 2
                wbuf = wst2a if wpar == 0 else wst2b
                wkey = "wst2a" if wpar == 0 else "wst2b"
                p.dma("sp", wbuf[:],
                      l1f_w1t[dc * 128:(dc + 1) * 128,
                              qb4 * 1024:(qb4 + 1) * 1024],
                      writes=[wkey])
                p.op("dve", lambda v, d=dc, w=wbuf, wk=wkey: v.tensor_scalar(
                    ternq[:, d * D:(d + 1) * D], w[:],
                    qs2[:, 24:25], None, ALU.is_gt),
                    reads=[wkey, ("qs2", 24)], writes=[("ternq", dc)])
                p.op("dve", lambda v, d=dc, w=wbuf, wk=wkey:
                     v.scalar_tensor_tensor(
                         ternq[:, d * D:(d + 1) * D], w[:],
                         qs2[:, 26:27], ternq[:, d * D:(d + 1) * D],
                         ALU.is_lt, ALU.subtract),
                     reads=[wkey, ("qs2", 26)], writes=[("ternq", dc)])
            for tt in range(NT):
                for sg in range(2):
                    bank = PB[(tt * 2 + sg) % 2]
                    bkey = ("PB", (tt * 2 + sg) % 2)
                    for dc in range(NF):
                        p.op("pe", lambda t, c=tt, s=sg, d=dc, b=bank: t.matmul(
                            b[:, 0:512],
                            qT1[:, d * T + c * 128:d * T + (c + 1) * 128],
                            ternq[:, d * D + s * 512:d * D + (s + 1) * 512],
                            start=(d == 0), stop=(d == NF - 1)),
                            reads=[("ternq", dc)] + ([("qT1", dd, tt)
                                   for dd in range(NF)] if dc == 0 else []),
                            writes=[bkey])
                    p.op("act", lambda s_, c=tt, s=sg, q4=qb4, b=bank:
                         s_.activation(
                             hb_[:, c * 4096 + q4 * 1024 + s * 512:
                                 c * 4096 + q4 * 1024 + (s + 1) * 512],
                             b[:, 0:512], AF.Gelu, scale=qs2[:, 25:26]),
                         reads=[bkey, ("qs2", 25)], writes=[("hb", tt, qb4, sg)])
        hb_keys = [("hb", tt, q4, sg) for tt in range(NT) for q4 in range(4)
                   for sg in range(2)]

        p.op("dve", lambda v: v.memset(qs2[:, 21:22], 0.0),
             writes=[("qs2", 21)])
        g2_chunks = []
        for ck in range(32):
            def _g2(ck=ck):
                p.dma("sp", qsc[:, 0:1024],
                      l1f_w2t[ck * 128:(ck + 1) * 128, :],
                      writes=["qsc"])
                p.op("dve", lambda v: v.tensor_reduce(
                    qs2[:, 22:23], qsc[:, 0:1024], AX.X, ALU.add,
                    apply_absolute_value=True),
                    reads=["qsc"], writes=[("qs2", 22)])
                p.op("dve", lambda v: v.tensor_tensor(
                    qs2[:, 21:22], qs2[:, 21:22], qs2[:, 22:23], ALU.add),
                    reads=[("qs2", 22)], writes=[("qs2", 21)])
            g2_chunks.append(_g2)

        AR.free("qT1")
        cmp2 = AR.place("cmp2", 40960, 8192, BF16)
        head_rounds(hb_, hb_keys, 8, cmp2, "cmp2", kb2, 64.0, 13,
                    mk_filler(g2_chunks, 3))
        cm2 = cmp2[:, 0:16384]
        for s in range(4):
            p.op("dve", lambda v, s0=s: v.scalar_tensor_tensor(
                cm2, hb_[:, s0 * 16384:(s0 + 1) * 16384], qs2[:, 16:17],
                hb_[:, s0 * 16384:(s0 + 1) * 16384], ALU.is_le, ALU.mult),
                reads=hb_keys + [("qs2", 16)],
                writes=[("cmp2", 0), ("cmp2", 1)])
            p.op("dve", lambda v, s0=s: v.max(
                hscr[:, s0 * 16:s0 * 16 + 8], cm2),
                reads=[("cmp2", 0), ("cmp2", 1)], writes=["hscr"])
            p.op("dve", lambda v, s0=s: v.match_replace(
                cm2, hscr[:, s0 * 16:s0 * 16 + 8], cm2, -1.0),
                reads=["hscr"], writes=[("cmp2", 0), ("cmp2", 1)])
            p.op("dve", lambda v, s0=s: v.max(
                hscr[:, s0 * 16 + 8:s0 * 16 + 16], cm2),
                reads=[("cmp2", 0), ("cmp2", 1)], writes=["hscr"])
        for k4 in range(4):
            p.op("dve", lambda v, k=k4: v.max(
                hscr[:, 64 + k * 8:64 + (k + 1) * 8], hscr[:, 0:64]),
                reads=["hscr"], writes=["hscr"])
            if k4 < 3:
                p.op("dve", lambda v, k=k4: v.match_replace(
                    hscr[:, 0:64], hscr[:, 64 + k * 8:64 + (k + 1) * 8],
                    hscr[:, 0:64], -1.0),
                    reads=["hscr"], writes=["hscr"])
        p.op("dve", lambda v: v.tensor_copy(gsf[:, 0:32], hscr[:, 64:96]),
             reads=["hscr"], writes=["gscr2"])
        allgather_cands()
        fine_pair(ka2, kb2)
        amax_tail(fr2, 11)
        p.op("dve", lambda v: v.reciprocal(qs2[:, 12:13], qs2[:, 11:12]),
             reads=[("qs2", 11)], writes=[("qs2", 12)])
        p.op("dve", lambda v: v.tensor_scalar_mul(
            qs2[:, 12:13], qs2[:, 12:13], 127.0),
            reads=[], writes=[("qs2", 12)])
        p.op("dve", lambda v: v.tensor_copy(qs2[:, 27:28], qs2[:, 12:13]),
             reads=[("qs2", 12)], writes=[("qs2", 27)])

        p.op("dve", lambda v: v.tensor_copy(qs2[:, 18:19], qs2[:, 21:22]),
             reads=[("qs2", 21)], writes=[("qs2", 18)])
        psum_scalar(18, 3)
        p.op("dve", lambda v: v.tensor_scalar(
            qs2[:, 3:4], qs2[:, 3:4], 1.0 / (4096.0 * 1024.0), 1e-5,
            ALU.mult, ALU.add),
            reads=[("qs2", 3)], writes=[("qs2", 3)])
        p.op("dve", lambda v: v.tensor_copy(qs2[:, 28:29], qs2[:, 3:4]),
             reads=[("qs2", 3)], writes=[("qs2", 28)])
        p.op("dve", lambda v: v.tensor_scalar_mul(
            qs2[:, 29:30], qs2[:, 28:29], 0.5),
            reads=[("qs2", 28)], writes=[("qs2", 29)])
        p.op("dve", lambda v: v.tensor_scalar_mul(
            qs2[:, 30:31], qs2[:, 28:29], -0.5),
            reads=[("qs2", 28)], writes=[("qs2", 30)])
        p.op("dve", lambda v: v.tensor_tensor(
            qs2[:, 7:8], qs2[:, 28:29], qs2[:, 11:12], ALU.mult),
            reads=[("qs2", 28), ("qs2", 11)], writes=[("qs2", 7)])
        p.op("dve", lambda v: v.tensor_scalar_mul(
            qs2[:, 31:32], qs2[:, 7:8], -1.0 / 127.0),
            reads=[("qs2", 7)], writes=[("qs2", 31)])

        vh = AR.t.ap()[0:128, 40960:45056]
        ih = AR.t.ap().bitcast(I32)[0:128, 45056:49152]

        def htile_keys(tt):
            return [("hb", tt, q4, sg) for q4 in range(4) for sg in range(2)]

        for tt in range(NT):
            p.op("dve", lambda v, c=tt: v.tensor_scalar_mul(
                vh, hb_[:, c * 4096:(c + 1) * 4096], qs2[:, 27:28]),
                reads=htile_keys(tt) + [("qs2", 27)], writes=[("cmp2", 0)])
            p.op("dve", lambda v: v.tensor_copy(ih, vh),
                 reads=[("cmp2", 0)], writes=[("cmp2", 1)])
            p.op("dve", lambda v, c=tt: v.tensor_scalar(
                hb_[:, c * 4096:(c + 1) * 4096], ih, -128.0, 127.0,
                ALU.max, ALU.min),
                reads=[("cmp2", 1)], writes=htile_keys(tt))

        AR.free("cmp2", "ternq", "wst2a", "wst2b", "lnwf", "lnbf")
        q2Tg = AR.place("q2Tg", 40960, 4096, BF16)
        tern2g = AR.place("tern2g", 45056, 2048, BF16)
        wst3a = AR.place("wst3a", 47104, 1024)
        wst3b = AR.place("wst3b", 48128, 1024)
        T2 = 1024
        accs = []
        hbs = []
        for half in range(2):
            if half == 0:
                acc = AR.place("acc2a", 0, 8192)
                anm = "acc2a"
                hsrc = hb_
            else:
                AR.free("hb")
                acc = AR.place("acc2b", 8192, 8192)
                anm = "acc2b"
                hsrc = AR.place("hb", 24576, 16384, BF16)
            accs.append((anm, acc))
            hbs.append(hsrc)
            for grp in range(8):
                for k4 in range(4):
                    kc = grp * 4 + k4
                    wpar = kc % 2
                    wbuf = wst3a if wpar == 0 else wst3b
                    wkey = "wst3a" if wpar == 0 else "wst3b"
                    p.dma("sp", wbuf[:],
                          l1f_w2t[kc * 128:(kc + 1) * 128, :],
                          writes=[wkey])
                    p.op("dve", lambda v, k=k4, w=wbuf, wk=wkey: v.tensor_scalar(
                        tern2g[:, k * D:(k + 1) * D], w[:],
                        qs2[:, 29:30], None, ALU.is_gt),
                        reads=[wkey, ("qs2", 29)], writes=[("tern2g", k4)])
                    p.op("dve", lambda v, k=k4, w=wbuf, wk=wkey:
                         v.scalar_tensor_tensor(
                             tern2g[:, k * D:(k + 1) * D], w[:],
                             qs2[:, 30:31], tern2g[:, k * D:(k + 1) * D],
                             ALU.is_lt, ALU.subtract),
                         reads=[wkey, ("qs2", 30)], writes=[("tern2g", k4)])
                    for t8 in range(8):
                        tt = half * 8 + t8
                        tp128(q2Tg[:, k4 * T2 + t8 * 128:
                                   k4 * T2 + (t8 + 1) * 128],
                              hsrc[:, t8 * 4096 + kc * 128:
                                   t8 * 4096 + (kc + 1) * 128],
                              [("hb", tt, kc // 8, (kc % 8) // 4)],
                              [("q2Tg", k4, t8)], ROT_C)
                for t8 in range(8):
                    for sg in range(2):
                        bank = PB[(t8 * 2 + sg) % 2]
                        bkey = ("PB", (t8 * 2 + sg) % 2)
                        for k4 in range(4):
                            p.op("pe", lambda t, k=k4, c=t8, s=sg, b=bank:
                                 t.matmul(
                                     b[:, 0:512],
                                     q2Tg[:, k * T2 + c * 128:
                                          k * T2 + (c + 1) * 128],
                                     tern2g[:, k * D + s * 512:
                                            k * D + (s + 1) * 512],
                                     start=(k == 0), stop=(k == 3)),
                                 reads=[("tern2g", k4), ("q2Tg", k4, t8)],
                                 writes=[bkey])
                        if grp == 0:
                            p.op("dve", lambda v, c=t8, s=sg, b=bank,
                                 a=acc, an=anm: v.tensor_copy(
                                     a[:, c * D + s * 512:c * D + (s + 1) * 512],
                                     b[:, 0:512]),
                                 reads=[bkey], writes=[(anm, t8, sg)])
                        else:
                            p.op("dve", lambda v, c=t8, s=sg, b=bank,
                                 a=acc, an=anm: v.tensor_tensor(
                                     a[:, c * D + s * 512:c * D + (s + 1) * 512],
                                     a[:, c * D + s * 512:c * D + (s + 1) * 512],
                                     b[:, 0:512], ALU.add),
                                 reads=[bkey, (anm, t8, sg)], writes=[(anm, t8, sg)])
            for t8 in range(8):
                p.op("dve", lambda v, c=t8, a=acc, an=anm: v.tensor_scalar_mul(
                    a[:, c * D:(c + 1) * D], a[:, c * D:(c + 1) * D],
                    qs2[:, 31:32]),
                    reads=[(anm, t8, 0), (anm, t8, 1), ("qs2", 31)],
                    writes=[(anm, t8, 0), (anm, t8, 1)])

        AR.free("hb", "q2Tg", "tern2g", "wst3a", "wst3b", "qsc")
        stream3 = AR.place("stream3", 24576, 16384)

        def accv(tt):
            return accs[tt // 8][1][:, (tt % 8) * D:(tt % 8 + 1) * D]

        def acck(tt):
            an = accs[tt // 8][0]
            return [(an, tt % 8, 0), (an, tt % 8, 1)]

        for tt in range(NT):
            p.dma("sp", stream3[:, tt * D:(tt + 1) * D],
                  spill_x2[tt * 128:(tt + 1) * 128, :],
                  writes=[("stream3", tt)])
        ln_stats(acck, accv, muB, rhoB, "muB", "rhoB")
        for tt in range(NT):
            p.op("dve", lambda v, c=tt: v.tensor_scalar(
                accv(c), accv(c),
                muB[:, c:c + 1], rhoB[:, c:c + 1], ALU.subtract, ALU.mult),
                reads=[("muB", tt), ("rhoB", tt)] + acck(tt),
                writes=acck(tt) + [("yq3", tt)])
            p.op("dve", lambda v, c=tt: v.tensor_tensor(
                stream3[:, c * D:(c + 1) * D], stream3[:, c * D:(c + 1) * D],
                accv(c), ALU.add),
                reads=[("yq3", tt)], writes=[("stream3", tt)])
        STREAM_FINAL = ("stream3", stream3)

    skey, sap = STREAM_FINAL
    final = []
    for tt in range(NT):
        final.append(p.dma("sp", y_out[tt * 128:(tt + 1) * 128, :],
                           sap[:, tt * D:(tt + 1) * D],
                           reads=[(skey, tt)]))
    if dbg is not None:
        for tt in range(NT):
            final.append(p.dma("sp", dbg[tt * 128:(tt + 1) * 128, :],
                               sap[:, tt * D:(tt + 1) * D],
                               reads=[(skey, tt)]))

    import os
    lim = int(os.environ.get("KBISECT", "0"))
    if lim:
        p.ops = p.ops[:lim]
        final = [d for d in final if d < lim]
    streams, stack = p.build(final)
    with stack:
        with nc.Block() as block:
            block.tensor(lambda e: [f(e) for f in streams["pe"]])
            block.scalar(lambda e: [f(e) for f in streams["act"]])
            block.vector(lambda e: [f(e) for f in streams["dve"]])
            block.sync(lambda e: [f(e) for f in streams["sp"]])
            block.gpsimd(lambda e: [f(e) for f in streams["gp"]])
    return nc


def host_prep(inputs):
    import ml_dtypes
    d = {k: np.asarray(v) for k, v in inputs.items()}
    cst = {}
    cst["identc"] = np.eye(128).astype(ml_dtypes.bfloat16)
    vecs = np.concatenate([
        np.broadcast_to(d["L0_attn_b_out"], (128, D)),
        np.broadcast_to(d["L0_attn_gamma"], (128, D)),
        np.broadcast_to(d["L0_attn_beta"], (128, D))], axis=1)
    cst["l0a_vecs"] = np.ascontiguousarray(vecs).astype(ml_dtypes.bfloat16)
    cst["w_qkv"] = np.ascontiguousarray(d["L0_attn_w_qkv"], dtype=np.float32)
    cst["w_out"] = np.ascontiguousarray(d["L0_attn_w_out"], dtype=np.float32)
    W = np.asarray(d["L0_ffn_W"], np.float32)
    R = np.asarray(d["L0_ffn_R"], np.float32)
    A = np.asarray(d["L0_ffn_A"], np.float32).reshape(128, D)
    cst["l0f_w"] = np.ascontiguousarray(
        np.concatenate([W.T, R.T, A.T], axis=1), np.float32)
    Bm = np.asarray(d["L0_ffn_B"], np.float32)
    cst["l0f_bt"] = np.ascontiguousarray(
        Bm.transpose(0, 2, 1).reshape(128, D), np.float32)
    cst["l0f_b"] = np.ascontiguousarray(
        np.broadcast_to(d["L0_ffn_b"], (128, D))).astype(ml_dtypes.bfloat16)
    cst["l1_maa_w1"] = np.ascontiguousarray(d["L1_attn_time_maa_w1"], np.float32)
    w2 = np.asarray(d["L1_attn_time_maa_w2"], np.float32)
    pk = np.zeros((128, D), np.float32)
    for f in range(4):
        pk[32 * f:32 * (f + 1), :] = w2[f]
    cst["l1_w2pk"] = pk
    cst["l1_w2p4"] = np.ascontiguousarray(w2[4], np.float32)
    cst["l1_dec_w1"] = np.ascontiguousarray(d["L1_attn_time_decay_w1"], np.float32)
    cst["l1_dec_w2"] = np.ascontiguousarray(d["L1_attn_time_decay_w2"], np.float32)
    mv = np.zeros((8, 128, 7), np.float32)
    for j, nm in enumerate(["time_maa_x", "time_maa_w", "time_maa_k",
                            "time_maa_v", "time_maa_r", "time_maa_g",
                            "time_decay"]):
        mv[:, :, j] = np.asarray(d["L1_attn_" + nm], np.float32).reshape(8, 128)
    cst["l1_maav"] = np.ascontiguousarray(mv.transpose(1, 0, 2).reshape(128, 56))
    for nm in ("r", "k", "v", "g", "o"):
        cst["l1_w" + nm] = np.ascontiguousarray(
            np.asarray(d["L1_attn_W_" + nm], np.float32).T)
    cst["l1_lnw"] = np.ascontiguousarray(np.broadcast_to(
        d["L1_attn_ln_w"], (128, D))).astype(ml_dtypes.bfloat16)
    cst["l1_lnb"] = np.ascontiguousarray(np.broadcast_to(
        d["L1_attn_ln_b"], (128, D))).astype(ml_dtypes.bfloat16)
    msk = np.zeros((128, 128), np.float32)
    sl = np.tril(np.ones((64, 64), np.float32), -1)
    msk[0:64, 64:128] = sl
    msk[64:128, 64:128] = sl
    cst["l1_masks"] = msk.astype(np.uint8)
    cst["l1f_w1t"] = np.ascontiguousarray(np.asarray(d["L1_ffn_w1"], np.float32).T)
    cst["l1f_w2t"] = np.ascontiguousarray(np.asarray(d["L1_ffn_w2"], np.float32).T)
    cst["l1f_lnw"] = np.ascontiguousarray(np.broadcast_to(
        d["L1_ffn_ln_g"], (128, D))).astype(ml_dtypes.bfloat16)
    cst["l1f_lnb"] = np.ascontiguousarray(np.broadcast_to(
        d["L1_ffn_ln_b"], (128, D))).astype(ml_dtypes.bfloat16)
    return cst


def kernel(**inputs):
    x = np.ascontiguousarray(np.asarray(inputs["L0_attn_x"], dtype=np.float32))
    B = x.shape[0]
    cst = host_prep(inputs)
    nc = build_nc()
    in_maps = [{"x": x[c], **cst} for c in range(B)]
    res = run_bass_kernel_spmd(nc, in_maps, core_ids=list(range(B)))
    return np.stack([r["y"] for r in res.results], axis=0)


# version 19
# speedup vs baseline: 1.0021x; 1.0021x over previous
import contextlib
import numpy as np
import concourse.bass as bass
import concourse.mybir as mybir
from concourse.bass_utils import run_bass_kernel_spmd

F32 = mybir.dt.float32
BF16 = mybir.dt.bfloat16
I32 = mybir.dt.int32
U8 = mybir.dt.uint8
AF = mybir.ActivationFunctionType
ALU = mybir.AluOpType
AX = mybir.AxisListType

T = 2048
D = 1024
NT = T // 128
NF = D // 128
NH = 16
HD = 64
LN_EPS = 1e-5

ENGS = ("pe", "act", "dve", "sp", "gp")


class Op:
    __slots__ = ("idx", "eng", "fn", "deps", "is_dma", "inc", "val")

    def __init__(self, idx, eng, fn, deps, is_dma):
        self.idx = idx
        self.eng = eng
        self.fn = fn
        self.deps = deps
        self.is_dma = is_dma
        self.inc = is_dma
        self.val = 0


class Rgn:
    __slots__ = ("w", "rs")

    def __init__(self):
        self.w = None
        self.rs = []


def _pref(k):
    return k if isinstance(k, str) else k[0]


class Prog:

    def __init__(self, nc):
        self.nc = nc
        self.ops = []
        self.rgn = {}
        self.alias_deps = {}

    def R(self, key):
        if key not in self.rgn:
            self.rgn[key] = Rgn()
        return self.rgn[key]

    def keys_of(self, bufname):
        out = set()
        for k, r in self.rgn.items():
            if _pref(k) == bufname:
                if r.w is not None:
                    out.add(r.w)
                out.update(r.rs)
        return out

    def _emit(self, eng, fn, reads, writes, extra, is_dma):
        deps = set(extra)
        for k in writes:
            ad = self.alias_deps.get(_pref(k))
            if ad:
                deps.update(ad)
        for k in reads:
            r = self.R(k)
            if r.w is not None:
                deps.add(r.w)
        for k in writes:
            r = self.R(k)
            if r.w is not None:
                deps.add(r.w)
            deps.update(r.rs)
        op = Op(len(self.ops), eng, fn, sorted(deps), is_dma)
        self.ops.append(op)
        for k in reads:
            self.R(k).rs.append(op.idx)
        for k in writes:
            r = self.R(k)
            r.w = op.idx
            r.rs = []
        return op.idx

    def op(self, eng, fn, reads=(), writes=(), extra=()):
        return self._emit(eng, fn, reads, writes, extra, False)

    def dma(self, q, dst, src, reads=(), writes=(), extra=(), **kw):
        fn = (lambda e, d=dst, s=src, kw=kw: e.dma_start(out=d, in_=s, **kw))
        return self._emit(q, fn, reads, writes, extra, True)

    NDMALANES = 8

    def build(self, final_deps):
        ops = self.ops
        needed = [[] for _ in ops]
        for o in ops:
            for d in o.deps:
                src = ops[d]
                if (src.eng == o.eng and src.eng == "pe"
                        and not src.is_dma and not o.is_dma):
                    continue
                needed[o.idx].append(d)
                src.inc = True
        for d in final_deps:
            ops[d].inc = True
        cnt = {}
        dma_n = {}
        semkeys = {}
        prior_lane_val = {}
        for o in ops:
            if o.is_dma:
                i = dma_n.get(o.eng, 0)
                dma_n[o.eng] = i + 1
                key = f"d_{o.eng}{i % self.NDMALANES}"
                semkeys[o.idx] = key
                prior_lane_val[o.idx] = cnt.get(key, 0)
                cnt[key] = cnt.get(key, 0) + 16
                o.val = cnt[key]
            elif o.inc:
                key = "c_" + o.eng
                semkeys[o.idx] = key
                cnt[key] = cnt.get(key, 0) + 1
                o.val = cnt[key]
            else:
                semkeys[o.idx] = "c_" + o.eng
                o.val = None
        streams = {e: [] for e in ENGS}
        water = {e: {} for e in ENGS}
        sems = {}
        stack = contextlib.ExitStack()
        for key in sorted(set(semkeys.values()) | set(cnt)):
            sems[key] = stack.enter_context(self.nc.semaphore(key))

        def emit_wait(eng, sk, v):
            w = water[eng]
            if w.get(sk, 0) >= v:
                return
            w[sk] = v
            streams[eng].append(lambda e, s=sems[sk], vv=v: e.wait_ge(s, vv))

        for o in ops:
            eng = o.eng
            for d in needed[o.idx]:
                emit_wait(eng, semkeys[d], ops[d].val)
            if o.is_dma:
                pv = prior_lane_val[o.idx]
                if pv > 0:
                    emit_wait(eng, semkeys[o.idx], pv)
                streams[eng].append(
                    lambda e, f=o.fn, s=sems[semkeys[o.idx]]:
                    f(e).then_inc(s, 16))
            elif o.inc:
                streams[eng].append(
                    lambda e, f=o.fn, s=sems[semkeys[o.idx]]:
                    f(e).then_inc(s, 1))
            else:
                streams[eng].append(lambda e, f=o.fn: f(e))
        for d in final_deps:
            emit_wait("gp", semkeys[d], ops[d].val)
        return streams, stack


class Arena:

    def __init__(self, p, nc, words):
        self.p = p
        self.t = nc.alloc_sbuf_tensor("arena", [128, words], F32)
        self.tb = self.t.bitcast(BF16)
        self.words = words
        self.live = {}
        self.dead = {}

    def place(self, name, off, w, dt=F32, parts=128):
        assert off + w <= self.words, (name, off, w)
        for nm, (o2, w2) in self.live.items():
            assert off + w <= o2 or o2 + w2 <= off, \
                f"arena overlap: {name}@{off}+{w} vs live {nm}@{o2}+{w2}"
        deps = set()
        for nm, (o2, w2) in self.dead.items():
            if not (off + w <= o2 or o2 + w2 <= off):
                deps.update(self.p.keys_of(nm))
        if deps:
            self.p.alias_deps[name] = sorted(deps)
        self.live[name] = (off, w)
        self.dead.pop(name, None)
        if dt == F32:
            return self.t.ap()[0:parts, off:off + w]
        return self.tb.ap()[0:parts, 2 * off:2 * (off + w)]

    def free(self, *names):
        for name in names:
            self.dead[name] = self.live.pop(name)


def strided(view, elem_off, free_dims, nparts=128):
    pstep = view.ap[0][0]
    return bass.AP(view.tensor, view.offset + elem_off,
                   [[pstep, nparts]] + free_dims)



def build_nc(debug_stage=None, build_until=None):
    nc = bass.Bass("TRN2", target_bir_lowering=False, debug=False)

    def din(name, shape, dt=F32):
        return nc.dram_tensor(name, shape, dt, kind="ExternalInput").ap()

    x_in = din("x", [T, D])
    w_qkv = din("w_qkv", [D, 3 * D])
    w_out = din("w_out", [D, D])
    l0a_vecs = din("l0a_vecs", [128, 3 * D], BF16)
    identc = din("identc", [128, 128], BF16)
    l0f_w = din("l0f_w", [D, 1160])
    l0f_bt = din("l0f_bt", [128, D])
    l0f_b = din("l0f_b", [128, D], BF16)
    l1_maa_w1 = din("l1_maa_w1", [D, 160])
    l1_w2pk = din("l1_w2pk", [128, D])
    l1_w2p4 = din("l1_w2p4", [32, D])
    l1_dec_w1 = din("l1_dec_w1", [D, 64])
    l1_dec_w2 = din("l1_dec_w2", [64, D])
    l1_maav = din("l1_maav", [128, 56])
    l1_wr = din("l1_wr", [D, D])
    l1_wk = din("l1_wk", [D, D])
    l1_wv = din("l1_wv", [D, D])
    l1_wg = din("l1_wg", [D, D])
    l1_wo = din("l1_wo", [D, D])
    l1_lnw = din("l1_lnw", [128, D], BF16)
    l1_lnb = din("l1_lnb", [128, D], BF16)
    l1_masks = din("l1_masks", [128, 128], U8)
    l1f_w1t = din("l1f_w1t", [D, 4096])
    l1f_w2t = din("l1f_w2t", [4096, D])
    l1f_lnw = din("l1f_lnw", [128, D], BF16)
    l1f_lnb = din("l1f_lnb", [128, D], BF16)

    y_out = nc.dram_tensor("y", [T, D], F32, kind="ExternalOutput").ap()
    dbg = None
    if debug_stage is not None:
        dbg = nc.dram_tensor("dbg", [T, D], F32, kind="ExternalOutput").ap()

    def sbt(name, shape, dt):
        return nc.alloc_sbuf_tensor(name, shape, dt)

    muA = sbt("muA", [128, NT], F32).ap()
    rhoA = sbt("rhoA", [128, NT], F32).ap()
    muB = sbt("muB", [128, NT], F32).ap()
    rhoB = sbt("rhoB", [128, NT], F32).ap()
    stat6 = sbt("stat6", [128, 12 * NT], F32).ap()
    stat2 = sbt("stat2", [128, 2 * NT], F32).ap()
    ident = sbt("ident", [128, 128], BF16).ap()
    ones_row = sbt("ones_row", [1, 128], F32).ap()
    ones_mat = sbt("ones_mat", [128, 128], F32).ap()

    PBt = [nc.alloc_psum_tensor(f"pb{i}", [128, 512], F32) for i in range(6)]
    PB = [t.ap() for t in PBt]
    PBb = [t.bitcast(BF16).ap() for t in PBt]
    spill_x2 = nc.dram_tensor("spill_x2", [T, D], F32).ap()
    PTRt = nc.alloc_psum_tensor("ptr", [128, 1024], BF16)
    PTR = PTRt.ap()
    PTRf = PTRt.bitcast(F32).ap()
    PSM = nc.alloc_psum_tensor("psm", [128, 512], F32).ap()

    class TRot:

        def __init__(self, pairs):
            self.pairs = pairs
            self.i = 0

        def nxt(self):
            pr = self.pairs[self.i % len(self.pairs)]
            self.i += 1
            return pr

    p = Prog(nc)
    AR = Arena(p, nc, 52400)
    ROT_A = TRot([(PTR, ("PTR",))] + [(PBb[i], ("PB", i)) for i in range(4)])
    ROT_B = TRot([(PTR, ("PTR",)), (PBb[4], ("PB", 4)), (PBb[5], ("PB", 5))])
    ROT_C = TRot([(PTR, ("PTR",)), (PBb[2], ("PB", 2)), (PBb[4], ("PB", 4)),
                  (PBb[5], ("PB", 5))])

    stream = AR.place("stream", 0, 16384)
    xTb = AR.place("xTb", 16384, 8192, BF16)
    aoT = AR.place("aoT", 24576, 8192, BF16)
    tI_a = AR.place("tI_a", 32768, 1024)
    tIb_a = AR.place("tIb_a", 33792, 512, BF16)
    wout_s = AR.place("wout", 34304, 4096, BF16)
    vecs_s = AR.place("vecs", 38400, 1536, BF16)
    wq_g = [AR.place("wq0", 39936, 1536, BF16),
            AR.place("wq1", 41472, 1536, BF16)]
    qT_g = AR.place("qT", 43008, 1024, BF16)
    kT_g = AR.place("kT", 44032, 1024, BF16)
    vT_g = AR.place("vT", 45056, 1024, BF16)
    Vaug = [AR.place("vaug0", 46080, 1024, BF16),
            AR.place("vaug1", 47104, 1024, BF16)]
    attn_ch = AR.place("ach", 48128, 512, BF16)
    rb_bc = AR.place("rbc", 48640, 512)

    p.dma("sp", ident[:], identc[:], writes=["ident"])
    p.dma("sp", vecs_s[:], l0a_vecs[:], writes=["vecs"])
    for tt in range(NT):
        p.dma("sp", stream[:, tt * D:(tt + 1) * D],
              x_in[tt * 128:(tt + 1) * 128, :], writes=[("stream", tt)])
    for fc in range(NF):
        p.dma("gp", wout_s[:, fc * D:(fc + 1) * D],
              w_out[fc * 128:(fc + 1) * 128, :], writes=[("wout", fc)])
    p.op("dve", lambda v: v.memset(ones_row[:], 1.0), writes=["ones_row"])
    p.op("dve", lambda v: v.memset(ones_mat[:], 1.0), writes=["ones_mat"])
    ones_bf = sbt("ones_bf", [128, 1], BF16).ap()
    p.op("dve", lambda v: v.memset(ones_bf[:], 1.0), writes=["ones_bf"])
    for i in range(2):
        p.op("dve", lambda v, i=i: v.memset(Vaug[i][:], 0.0),
             writes=[("vaug_ones", i)])
        p.op("dve", lambda v, i=i: v.memset(
            strided(Vaug[i], 64, [[128, NT]]), 1.0),
            writes=[("vaug_ones", i)])

    def ln_stats(src_keys_fn, src_ap_fn, mu, rho, mukey, rhokey):
        for tt in range(NT):
            o6 = 12 * tt
            o2 = 2 * tt
            p.op("dve", lambda v, c=tt, o=o6: v.bn_stats(
                stat6[:, o:o + 6], src_ap_fn(c)[:, 0:512]),
                reads=src_keys_fn(tt), writes=[("stat6", tt, 0)])
            p.op("dve", lambda v, c=tt, o=o6: v.bn_stats(
                stat6[:, o + 6:o + 12], src_ap_fn(c)[:, 512:1024]),
                reads=src_keys_fn(tt), writes=[("stat6", tt, 1)])
            p.op("dve", lambda v, o=o6, o2_=o2: v.bn_aggr(
                stat2[:, o2_:o2_ + 2], stat6[:, o:o + 12]),
                reads=[("stat6", tt, 0), ("stat6", tt, 1)],
                writes=[("stat2", tt)])
            p.op("dve", lambda v, c=tt, m=mu, o2_=o2: v.tensor_copy(
                m[:, c:c + 1], stat2[:, o2_:o2_ + 1]),
                reads=[("stat2", tt)], writes=[(mukey, tt)])
            p.op("dve", lambda v, o2_=o2: v.tensor_scalar_add(
                stat2[:, o2_ + 1:o2_ + 2], stat2[:, o2_ + 1:o2_ + 2], LN_EPS),
                reads=[], writes=[("stat2", tt)])
            p.op("dve", lambda v, o2_=o2: v.reciprocal(
                stat2[:, o2_ + 1:o2_ + 2], stat2[:, o2_ + 1:o2_ + 2]),
                reads=[], writes=[("stat2", tt)])
            p.op("act", lambda s, c=tt, r=rho, o2_=o2: s.activation(
                r[:, c:c + 1], stat2[:, o2_ + 1:o2_ + 2], AF.Sqrt),
                reads=[("stat2", tt)], writes=[(rhokey, tt)])

    def tp128(dst_ap, src_ap, rds, wrs, rot, ncols=128, idbase=0):
        bank, key = rot.nxt()
        p.op("pe", lambda t, b=bank: t.transpose(
            b[:, 0:ncols], src_ap,
            ident[idbase:idbase + ncols, idbase:idbase + ncols]
            if ncols != 128 or idbase else ident[:]),
            reads=list(rds) + ["ident"], writes=[key])
        p.op("dve", lambda v, b=bank: v.tensor_copy(dst_ap, b[:, 0:ncols]),
             reads=[key], writes=list(wrs))

    ln_stats(lambda tt: [("stream", tt)],
             lambda tt: stream[:, tt * D:(tt + 1) * D], muA, rhoA, "muA", "rhoA")

    for tt in range(NT):
        p.op("dve", lambda v, c=tt: v.tensor_scalar(
            tI_a[:], stream[:, c * D:(c + 1) * D],
            muA[:, c:c + 1], rhoA[:, c:c + 1], ALU.subtract, ALU.mult),
            reads=[("stream", tt), ("muA", tt), ("rhoA", tt)],
            writes=["tI_a"])
        p.op("dve", lambda v: v.tensor_copy(tIb_a[:], tI_a[:]),
             reads=["tI_a"], writes=["tIb_a"])
        for fc in range(NF):
            tp128(xTb[:, fc * T + tt * 128: fc * T + (tt + 1) * 128],
                  tIb_a[:, fc * 128:(fc + 1) * 128],
                  ["tIb_a"], [("xTb", fc, tt)], ROT_A)
    AR.free("tI_a", "tIb_a")
    rowsum = AR.place("rowsum", 32768, 512, parts=1)
    tI_b = AR.place("tI_b", 33280, 1024)

    xT_keys = [("xTb", fc, tt) for fc in range(NF) for tt in range(NT)]

    for g in range(NH // 2 if build_until != "xT" else 0):
        wqb = wq_g[g % 2]
        wqn = "wq" + str(g % 2)
        for dc in range(NF):
            p.dma("gp", wqb[:, dc * 384:(dc + 1) * 384],
                  w_qkv[dc * 128:(dc + 1) * 128, 384 * g:384 * (g + 1)],
                  writes=[(wqn, dc)])

        for mt in range(3):
            for ns in range(4):
                bi = (mt * 4 + ns) % 6
                bank = PB[bi]
                bkey = ("PB", bi)
                for dc in range(NF):
                    rd = [(wqn, dc)]
                    if dc == 0:
                        rd = rd + xT_keys
                    p.op("pe", lambda t, b=bank, m=mt, n=ns, d=dc, w=wqb:
                         t.matmul(b[:, 0:512],
                                  w[:, d * 384 + m * 128:d * 384 + (m + 1) * 128],
                                  xTb[:, d * T + n * 512:d * T + (n + 1) * 512],
                                  start=(d == 0), stop=(d == NF - 1)),
                         reads=rd, writes=[bkey])
                cs = slice(ns * 512, (ns + 1) * 512)
                if mt == 0:
                    p.op("dve", lambda v, b=bank, c=cs: v.tensor_copy(
                        qT_g[0:64, c], b[0:64, 0:512]),
                        reads=[bkey], writes=[("qT", 0, ns)])
                    p.op("dve", lambda v, b=bank, c=cs: v.tensor_scalar_mul(
                        kT_g[0:64, c], b[64:128, 0:512], 0.125),
                        reads=[bkey], writes=[("kT", 0, ns)])
                elif mt == 1:
                    p.op("dve", lambda v, b=bank, c=cs: v.tensor_copy(
                        vT_g[0:64, c], b[0:64, 0:512]),
                        reads=[bkey], writes=[("vT", 0, ns)])
                    p.op("dve", lambda v, b=bank, c=cs: v.tensor_copy(
                        qT_g[64:128, c], b[64:128, 0:512]),
                        reads=[bkey], writes=[("qT", 1, ns)])
                else:
                    p.op("dve", lambda v, b=bank, c=cs: v.tensor_scalar_mul(
                        kT_g[64:128, c], b[0:64, 0:512], 0.125),
                        reads=[bkey], writes=[("kT", 1, ns)])
                    p.op("dve", lambda v, b=bank, c=cs: v.tensor_copy(
                        vT_g[64:128, c], b[64:128, 0:512]),
                        reads=[bkey], writes=[("vT", 1, ns)])

        qT_all = [("qT", h, n) for h in range(2) for n in range(4)]
        kT_all = [("kT", h, n) for h in range(2) for n in range(4)]
        vT_all = [("vT", h, n) for h in range(2) for n in range(4)]

        vaug_keys = [[], []]
        for hh in range(2):
            for tt in range(NT):
                k = ("vaug" + str(hh), tt)
                tp128(Vaug[hh][:, tt * 128:tt * 128 + 64],
                      vT_g[hh * 64:hh * 64 + 64, tt * 128:(tt + 1) * 128],
                      vT_all, [k, ("vaug_ones", hh)], ROT_B,
                      ncols=64, idbase=hh * 64)
                vaug_keys[hh].append(k)

        for hh in range(2):
            base = hh * 64
            h_abs = 2 * g + hh
            avb = [PB[2 * hh + 0], PB[2 * hh + 1]]
            avkeys = [("PB", 2 * hh + 0), ("PB", 2 * hh + 1)]
            if hh == 0:
                scb = [PSM, PB[4], PB[2], PB[3]]
                sck = [("PSM",), ("PB", 4), ("PB", 2), ("PB", 3)]
            else:
                scb = [PTRf, PB[5], PB[0], PB[1]]
                sck = [("PTR",), ("PB", 5), ("PB", 0), ("PB", 1)]
            fc = (h_abs * 64) // 128
            rhalf = (h_abs * 64) % 128
            pend = []

            def mk_div(q, n, h=h_abs, f=fc, r=rhalf, sb=scb[1], sk=sck[1],
                       ab=avb, abk=avkeys, g_=g):
                def _div(q=q, n=n, h=h, f=f, r=r, sb=sb, sk=sk,
                         ab=ab[n], abk=abk[n]):
                    p.op("dve", lambda v, a_=ab: v.tensor_scalar_add(
                        rowsum[:], a_[64:65, 0:512], 1e-6),
                        reads=[abk], writes=["rowsum"])
                    p.op("act", lambda s: s.activation(
                        rowsum[:], rowsum[:], AF.Ln),
                        reads=["rowsum"], writes=["rowsum"])
                    p.op("pe", lambda t, s_=sb: t.matmul(
                        s_[0:64, 0:512], ones_row[0:1, 0:64], rowsum[:],
                        start=True, stop=True),
                        reads=["rowsum", "ones_row"], writes=[sk])
                    p.op("act", lambda s, s_=sb: s.activation(
                        rb_bc[0:64, 0:512], s_[0:64, 0:512], AF.Exp,
                        scale=-1.0),
                        reads=[sk], writes=["rbc"])
                    p.op("dve", lambda v, a_=ab, q_=q, n_=n, f_=f, r_=r:
                         v.tensor_tensor(
                             aoT[r_:r_ + 64, f_ * T + q_ * 1024 + n_ * 512:
                                 f_ * T + q_ * 1024 + (n_ + 1) * 512],
                             a_[0:64, 0:512], rb_bc[0:64, 0:512], ALU.mult),
                         reads=[abk, "rbc"],
                         writes=[("aoT", f, h, q, n)])
                return _div

            for ci, (qh, n2) in enumerate([(q, n) for q in range(2)
                                           for n in range(2)]):
                for kt in range(NT):
                    sb = scb[kt % 4]
                    sk = sck[kt % 4]
                    ach = attn_ch[:, (kt % 2) * 512:(kt % 2 + 1) * 512]
                    akey = ("ach", kt % 2)
                    p.op("pe", lambda t, b=base, k=kt, q=qh, n=n2, s_=sb:
                         t.matmul(s_[:, 0:512],
                                  kT_g[b:b + 64, k * 128:(k + 1) * 128],
                                  qT_g[b:b + 64,
                                       q * 1024 + n * 512:q * 1024 + (n + 1) * 512],
                                  start=True, stop=True,
                                  tile_position=(b, 0)),
                         reads=kT_all + qT_all, writes=[sk])
                    if kt % 2 == 0:
                        p.op("act", lambda s, d=ach, s_=sb: s.activation(
                            d, s_[:, 0:512], AF.Relu),
                            reads=[sk], writes=[akey])
                    else:
                        p.op("dve", lambda v, d=ach, s_=sb:
                             v.tensor_scalar_max(d, s_[:, 0:512], 0.0),
                             reads=[sk], writes=[akey])
                    if kt >= 1:
                        kp = kt - 1
                        p.op("pe", lambda t, k=kp, n=n2, h=hh,
                             d=attn_ch[:, (kp % 2) * 512:(kp % 2 + 1) * 512],
                             ab=avb: t.matmul(
                                 ab[n][0:128, 0:512],
                                 Vaug[h][:, k * 128:(k + 1) * 128],
                                 d, start=(k == 0), stop=False),
                             reads=[("ach", kp % 2)] + vaug_keys[hh],
                             writes=[avkeys[n2]])
                    if kt == 1 and pend:
                        pend.pop()()
                kp = NT - 1
                p.op("pe", lambda t, k=kp, n=n2, h=hh,
                     d=attn_ch[:, (kp % 2) * 512:(kp % 2 + 1) * 512],
                     ab=avb: t.matmul(
                         ab[n][0:128, 0:512],
                         Vaug[h][:, k * 128:(k + 1) * 128],
                         d, start=False, stop=True),
                     reads=[("ach", kp % 2)] + vaug_keys[hh],
                     writes=[avkeys[n2]])
                pend.append(mk_div(qh, n2))
            while pend:
                pend.pop()()

    AR.free("wq0", "wq1", "qT", "kT", "vT", "vaug0", "vaug1", "rowsum", "xTb")
    aoT_all = [k for k in p.rgn if isinstance(k, tuple) and k and k[0] == "aoT"]

    ytmp = AR.place("ytmp", 16384, 8192, BF16)

    for tt in range(NT if build_until not in ("xT", "groups") else 0):
        for ns in range(2):
            bank = PB[ns]
            bkey = ("PB", ns)
            for fc in range(NF):
                rd = [("wout", fc)]
                if fc == 0:
                    rd = rd + aoT_all
                p.op("pe", lambda t, c=tt, n=ns, f=fc, b=bank:
                     t.matmul(b[:, 0:512],
                              aoT[:, f * T + c * 128:f * T + (c + 1) * 128],
                              wout_s[:, f * D + n * 512:f * D + (n + 1) * 512],
                              start=(f == 0), stop=(f == NF - 1)),
                     reads=rd, writes=[bkey])
            p.op("dve", lambda v, c=tt, n=ns, b=bank: v.tensor_tensor(
                ytmp[:, c * D + n * 512:c * D + (n + 1) * 512],
                b[:, 0:512], vecs_s[:, n * 512:(n + 1) * 512], ALU.add),
                reads=[bkey, "vecs"], writes=[("ytmp", tt, ns)])
            p.op("dve", lambda v, c=tt, n=ns: v.tensor_scalar(
                tI_b[:, n * 512:(n + 1) * 512],
                stream[:, c * D + n * 512:c * D + (n + 1) * 512],
                muA[:, c:c + 1], rhoA[:, c:c + 1], ALU.subtract, ALU.mult),
                reads=[("stream", tt), ("muA", tt), ("rhoA", tt)],
                writes=[("tI_b", ns)])
            p.op("dve", lambda v, c=tt, n=ns: v.tensor_tensor(
                ytmp[:, c * D + n * 512:c * D + (n + 1) * 512],
                ytmp[:, c * D + n * 512:c * D + (n + 1) * 512],
                tI_b[:, n * 512:(n + 1) * 512], ALU.add),
                reads=[("tI_b", ns)],
                writes=[("ytmp", tt, ns)])

    if build_until in ("xT", "groups"):
        NT_E = 0
    else:
        NT_E = NT
    if NT_E:
        ln_stats(lambda tt: [("ytmp", tt, 0), ("ytmp", tt, 1)],
                 lambda tt: ytmp[:, tt * D:(tt + 1) * D], muB, rhoB, "muB", "rhoB")
    for tt in range(NT_E):
        p.op("dve", lambda v, c=tt: v.tensor_scalar(
            ytmp[:, c * D:(c + 1) * D], ytmp[:, c * D:(c + 1) * D],
            muB[:, c:c + 1], rhoB[:, c:c + 1], ALU.subtract, ALU.mult),
            reads=[("muB", tt), ("rhoB", tt)],
            writes=[("ytmp", tt, 0), ("ytmp", tt, 1), ("y2", tt)])
        p.op("dve", lambda v, c=tt: v.tensor_tensor(
            ytmp[:, c * D:(c + 1) * D], ytmp[:, c * D:(c + 1) * D],
            vecs_s[:, D:2 * D], ALU.mult),
            reads=["vecs"], writes=[("y2", tt)])
        p.op("dve", lambda v, c=tt: v.tensor_tensor(
            ytmp[:, c * D:(c + 1) * D], ytmp[:, c * D:(c + 1) * D],
            vecs_s[:, 2 * D:3 * D], ALU.add),
            reads=["vecs"], writes=[("y2", tt)])
    if NT_E:
        ln_stats(lambda tt: [("y2", tt)],
                 lambda tt: ytmp[:, tt * D:(tt + 1) * D], muA, rhoA, "muA", "rhoA")
    for tt in range(NT_E):
        p.op("dve", lambda v, c=tt: v.tensor_scalar(
            ytmp[:, c * D:(c + 1) * D], ytmp[:, c * D:(c + 1) * D],
            muA[:, c:c + 1], rhoA[:, c:c + 1], ALU.subtract, ALU.mult),
            reads=[("muA", tt), ("rhoA", tt)],
            writes=[("y2", tt), ("y3", tt)])
        p.op("dve", lambda v, c=tt: v.tensor_tensor(
            stream[:, c * D:(c + 1) * D], stream[:, c * D:(c + 1) * D],
            ytmp[:, c * D:(c + 1) * D], ALU.add),
            reads=[("y3", tt)], writes=[("stream", tt)])

    DO_L0F = build_until not in ("xT", "groups", "l0a")
    AR.free("aoT", "wout", "vecs", "ach", "rbc")
    xT2 = AR.place("xT2", 24576, 8192, BF16)
    l0fw_s = AR.place("l0fw", 34304, 4640, BF16)
    bt_s = AR.place("l0fbt", 39936, 512, BF16)
    bfvec = AR.place("l0fb", 40448, 512, BF16)
    gsc = AR.place("gsc", 41100, 200)
    ghb = AR.place("ghb", 41300, 64, BF16)
    ghT = AR.place("ghT", 41364, 64, BF16)

    for dc in range(NF if DO_L0F else 0):
        p.dma("gp", l0fw_s[:, dc * 1160:(dc + 1) * 1160],
              l0f_w[dc * 128:(dc + 1) * 128, :], writes=[("l0fw", dc)])
    if DO_L0F:
        p.dma("gp", bt_s[:], l0f_bt[:], writes=["l0fbt"])
        p.dma("gp", bfvec[:], l0f_b[:], writes=["l0fb"])

    if DO_L0F:
        ln_stats(lambda tt: [("stream", tt)],
                 lambda tt: stream[:, tt * D:(tt + 1) * D], muA, rhoA, "muA", "rhoA")
    xcast = AR.place("xcast", 41428, 512, BF16)
    for tt in range(NT if DO_L0F else 0):
        p.op("dve", lambda v, c=tt: v.tensor_scalar(
            tI_b[:, 0:1024], stream[:, c * D:(c + 1) * D],
            muA[:, c:c + 1], rhoA[:, c:c + 1], ALU.subtract, ALU.mult),
            reads=[("stream", tt), ("muA", tt), ("rhoA", tt)], writes=["tI_bf"])
        p.op("dve", lambda v: v.tensor_copy(xcast[:], tI_b[:, 0:1024]),
             reads=["tI_bf"], writes=["xcast"])
        for fc in range(NF):
            tp128(xT2[:, fc * T + tt * 128: fc * T + (tt + 1) * 128],
                  xcast[:, fc * 128:(fc + 1) * 128],
                  ["xcast"], [("xT2", fc, tt)], ROT_A)
    xT2_keys = [("xT2", fc, tt) for fc in range(NF) for tt in range(NT)]

    for tt in range(NT if DO_L0F else 0):
        boff = 0 if tt % 2 == 0 else 3
        b0, b1, b2 = PB[boff], PB[boff + 1], PB[boff + 2]
        k0, k1, k2 = ("PB", boff), ("PB", boff + 1), ("PB", boff + 2)
        for dc in range(NF):
            rd = [("l0fw", dc), ("xT2", dc, tt)]
            p.op("pe", lambda t, c=tt, d=dc, b=b0: t.matmul(
                b[:, 0:512],
                xT2[:, d * T + c * 128:d * T + (c + 1) * 128],
                l0fw_s[:, d * 1160:d * 1160 + 512],
                start=(d == 0), stop=False),
                reads=rd, writes=[k0])
            p.op("pe", lambda t, c=tt, d=dc, b=b1: t.matmul(
                b[:, 0:512],
                xT2[:, d * T + c * 128:d * T + (c + 1) * 128],
                l0fw_s[:, d * 1160 + 512:d * 1160 + 1024],
                start=(d == 0), stop=False),
                reads=rd, writes=[k1])
            p.op("pe", lambda t, c=tt, d=dc, b=b2: t.matmul(
                b[:, 0:136],
                xT2[:, d * T + c * 128:d * T + (c + 1) * 128],
                l0fw_s[:, d * 1160 + 1024:d * 1160 + 1160],
                start=(d == 0), stop=(d == NF - 1)),
                reads=rd, writes=[k2])
        GA = gsc
        p.op("dve", lambda v, b=b2: v.tensor_reduce(
            GA[:, 24:25], b[:, 0:8], AX.X, ALU.max, negate=True),
            reads=[k2], writes=[("gsc", "negm1")])
        p.op("act", lambda s, b=b2: s.activation(
            GA[:, 0:8], b[:, 0:8], AF.Exp, bias=GA[:, 24:25], scale=1.0),
            reads=[k2, ("gsc", "negm1")], writes=[("gsc", "ge")])
        p.op("dve", lambda v: v.tensor_reduce(
            GA[:, 25:26], GA[:, 0:8], AX.X, ALU.add),
            reads=[("gsc", "ge")], writes=[("gsc", "s")])
        p.op("dve", lambda v: v.reciprocal(GA[:, 25:26], GA[:, 25:26]),
             reads=[("gsc", "s")], writes=[("gsc", "s")])
        p.op("dve", lambda v: v.tensor_scalar_mul(
            GA[:, 8:16], GA[:, 0:8], GA[:, 25:26]),
            reads=[("gsc", "ge"), ("gsc", "s")], writes=[("gsc", "gate")])
        p.op("dve", lambda v: v.tensor_reduce(
            GA[:, 26:27], GA[:, 8:16], AX.X, ALU.max),
            reads=[("gsc", "gate")], writes=[("gsc", "m1g")])
        p.op("dve", lambda v: v.tensor_scalar(
            GA[:, 16:24], GA[:, 8:16], GA[:, 26:27], None, ALU.is_equal),
            reads=[("gsc", "gate"), ("gsc", "m1g")], writes=[("gsc", "scr")])
        p.op("dve", lambda v: v.scalar_tensor_tensor(
            GA[:, 16:24], GA[:, 16:24], -1e30, GA[:, 8:16],
            ALU.mult, ALU.add),
            reads=[("gsc", "gate")], writes=[("gsc", "scr")])
        p.op("dve", lambda v: v.tensor_reduce(
            GA[:, 27:28], GA[:, 16:24], AX.X, ALU.max),
            reads=[("gsc", "scr")], writes=[("gsc", "m2")])
        p.op("dve", lambda v: v.tensor_scalar(
            GA[:, 16:24], GA[:, 8:16], GA[:, 27:28], None, ALU.is_ge),
            reads=[("gsc", "gate"), ("gsc", "m2")], writes=[("gsc", "scr")])
        p.op("dve", lambda v: v.tensor_tensor(
            GA[:, 8:16], GA[:, 8:16], GA[:, 16:24], ALU.mult),
            reads=[("gsc", "scr")], writes=[("gsc", "gate")])
        p.op("dve", lambda v: v.tensor_reduce(
            GA[:, 28:29], GA[:, 8:16], AX.X, ALU.add),
            reads=[("gsc", "gate")], writes=[("gsc", "s2")])
        p.op("dve", lambda v: v.tensor_scalar_add(
            GA[:, 28:29], GA[:, 28:29], 1e-6),
            reads=[], writes=[("gsc", "s2")])
        p.op("dve", lambda v: v.reciprocal(GA[:, 28:29], GA[:, 28:29]),
             reads=[], writes=[("gsc", "s2")])
        p.op("dve", lambda v: v.tensor_scalar_mul(
            GA[:, 8:16], GA[:, 8:16], GA[:, 28:29]),
            reads=[("gsc", "s2")], writes=[("gsc", "gate")])
        p.op("dve", lambda v, b=b2: v.tensor_tensor(
            ghb[:], b[:, 8:136],
            strided(GA, 8, [[1, 8], [0, 16]]), ALU.mult),
            reads=[k2, ("gsc", "gate")], writes=["ghb"])
        slot = tt % 8
        pkey = ("PTR",)
        p.op("pe", lambda t, s=slot: t.transpose(
            PTR[:, s * 128:s * 128 + 128], ghb[:], ident[:]),
            reads=["ghb", "ident"], writes=[pkey])
        p.op("dve", lambda v, s=slot: v.tensor_copy(
            ghT[:], PTR[:, s * 128:s * 128 + 128]),
            reads=[pkey], writes=["ghT"])
        p.op("pe", lambda t, b=b0: t.matmul(
            b[:, 0:512], ghT[:], bt_s[:, 0:512],
            start=False, stop=True),
            reads=["ghT", "l0fbt"], writes=[k0])
        p.op("pe", lambda t, b=b1: t.matmul(
            b[:, 0:512], ghT[:], bt_s[:, 512:1024],
            start=False, stop=True),
            reads=["ghT", "l0fbt"], writes=[k1])
        for ns, bb, kk in ((0, b0, k0), (1, b1, k1)):
            p.op("dve", lambda v, c=tt, n=ns, b=bb: v.tensor_tensor(
                ytmp[:, c * D + n * 512:c * D + (n + 1) * 512],
                b[:, 0:512], bfvec[:, n * 512:(n + 1) * 512], ALU.add),
                reads=[kk, "l0fb"], writes=[("ytmp", tt, ns)])

    if DO_L0F:
        ln_stats(lambda tt: [("ytmp", tt, 0), ("ytmp", tt, 1)],
                 lambda tt: ytmp[:, tt * D:(tt + 1) * D], muB, rhoB, "muB", "rhoB")
    for tt in range(NT if DO_L0F else 0):
        p.op("dve", lambda v, c=tt: v.tensor_scalar(
            ytmp[:, c * D:(c + 1) * D], ytmp[:, c * D:(c + 1) * D],
            muB[:, c:c + 1], rhoB[:, c:c + 1], ALU.subtract, ALU.mult),
            reads=[("muB", tt), ("rhoB", tt)],
            writes=[("ytmp", tt, 0), ("ytmp", tt, 1), ("yf3", tt)])
        p.op("dve", lambda v, c=tt: v.tensor_tensor(
            stream[:, c * D:(c + 1) * D], stream[:, c * D:(c + 1) * D],
            ytmp[:, c * D:(c + 1) * D], ALU.add),
            reads=[("yf3", tt)], writes=[("stream", tt)])

    DO_L1A = build_until not in ("xT", "groups", "l0a", "l0f")
    AR.free("ytmp", "xT2", "l0fw", "l0fbt", "l0fb", "gsc", "ghb", "ghT",
            "xcast", "tI_b")
    spill_x = nc.dram_tensor("spill_x", [T, D], F32).ap()
    spill_rkvg = {nm: nc.dram_tensor("spill_" + nm, [D, T], BF16).ap()
                  for nm in ("r", "k", "v", "g")}

    xTb1 = AR.place("xTb1", 16384, 8192, BF16)
    dxTb = AR.place("dxTb", 24576, 8192, BF16)
    tmpf = AR.place("tmpf", 32768, 1024)
    xbc = AR.place("xbc", 33792, 512, BF16)
    w1maa_s = AR.place("w1maa", 41984, 640, BF16)
    w2pk_s = AR.place("w2pk", 42624, 512, BF16)
    w2p4_s = AR.place("w2p4", 43136, 512, BF16)
    d1_s = AR.place("d1s", 43648, 256, BF16)
    d2_s = AR.place("d2s", 43904, 512, BF16)
    maav = AR.place("maav", 44416, 64)
    wjs = AR.place("wjs", 44480, 4096, BF16)
    xxa = AR.place("xxa", 48576, 1024, BF16)
    xxb = AR.place("xxb", 49600, 1024, BF16)
    t1s = AR.place("t1s", 50624, 1024, BF16)
    msc = AR.place("msc", 51648, 512)

    if DO_L1A:
        for dc in range(NF):
            p.dma("gp", w1maa_s[:, dc * 160:(dc + 1) * 160],
                  l1_maa_w1[dc * 128:(dc + 1) * 128, :], writes=[("w1maa", dc)])
            p.dma("gp", d1_s[:, dc * 64:(dc + 1) * 64],
                  l1_dec_w1[dc * 128:(dc + 1) * 128, :], writes=[("d1s", dc)])
        p.dma("gp", w2pk_s[:], l1_w2pk[:], writes=["w2pk"])
        p.dma("gp", w2p4_s[0:32, :], l1_w2p4[:], writes=["w2p4"])
        p.dma("gp", d2_s[0:64, :], l1_dec_w2[:], writes=["d2s"])
        p.dma("sp", maav[:, 0:56], l1_maav[:], writes=["maav"])

        ln_stats(lambda tt: [("stream", tt)],
                 lambda tt: stream[:, tt * D:(tt + 1) * D], muA, rhoA,
                 "muA", "rhoA")
        for tt in range(NT):
            p.op("dve", lambda v, c=tt: v.tensor_scalar(
                tmpf[:], stream[:, c * D:(c + 1) * D],
                muA[:, c:c + 1], rhoA[:, c:c + 1], ALU.subtract, ALU.mult),
                reads=[("stream", tt), ("muA", tt), ("rhoA", tt)],
                writes=["tmpf"])
            p.op("dve", lambda v: v.tensor_copy(xbc[:], tmpf[:]),
                 reads=["tmpf"], writes=["xbc"])
            for fc in range(NF):
                tp128(xTb1[:, fc * T + tt * 128: fc * T + (tt + 1) * 128],
                      xbc[:, fc * 128:(fc + 1) * 128],
                      ["xbc"], [("xTb1", fc, tt)], ROT_A)
        xT1_keys = [("xTb1", fc, tt) for fc in range(NF) for tt in range(NT)]

        for tt in range(NT):
            p.dma("sp", spill_x[tt * 128:(tt + 1) * 128, :],
                  stream[:, tt * D:(tt + 1) * D], reads=[("stream", tt)])

        AR.free("tmpf", "xbc")
        sh = AR.place("sh", 32768, 1024, BF16)
        for fc in range(NF):
            xf = xTb1[:, fc * T:(fc + 1) * T]
            p.op("dve", lambda v, x=xf: v.tensor_tensor(
                sh[:, 1:T - 1], x[:, 0:T - 2], x[:, 2:T], ALU.add),
                reads=xT1_keys, writes=["sh"])
            p.op("dve", lambda v, x=xf: v.tensor_copy(sh[:, 0:1], x[:, 1:2]),
                 reads=xT1_keys, writes=["sh"])
            p.op("dve", lambda v, x=xf: v.tensor_copy(
                sh[:, T - 1:T], x[:, T - 2:T - 1]),
                reads=xT1_keys, writes=["sh"])
            p.op("dve", lambda v, x=xf, f=fc: v.scalar_tensor_tensor(
                dxTb[:, f * T:(f + 1) * T], sh[:], 0.5, x,
                ALU.mult, ALU.subtract),
                reads=["sh"], writes=[("dxTb", fc)])
        dx_keys = [("dxTb", fc) for fc in range(NF)]

        AR.free("sh")
        mixb = AR.place("mixb", 33792, 8192, BF16)
        for fc in range(NF):
            p.op("dve", lambda v, f=fc: v.scalar_tensor_tensor(
                mixb[:, f * T:(f + 1) * T], dxTb[:, f * T:(f + 1) * T],
                maav[:, f * 7:f * 7 + 1], xTb1[:, f * T:(f + 1) * T],
                ALU.mult, ALU.add),
                reads=dx_keys + xT1_keys + ["maav"], writes=[("mixb", fc)])
        mix_keys = [("mixb", fc) for fc in range(NF)]
        for mt, bank, bkey, M in ((0, PB[2], ("PB", 2), 128), (1, PB[3], ("PB", 3), 32)):
            for ns in range(4):
                for dc in range(NF):
                    p.op("pe", lambda t, m=mt, n=ns, d=dc, b=bank, MM=M:
                         t.matmul(b[0:MM, 0:512],
                                  w1maa_s[:, d * 160 + m * 128:d * 160 + m * 128 + MM],
                                  mixb[:, d * T + n * 512:d * T + (n + 1) * 512],
                                  start=(d == 0), stop=(d == NF - 1)),
                         reads=[("w1maa", dc)] + (mix_keys if dc == 0 else []),
                         writes=[bkey])
                dst = xxa if mt == 0 else xxb
                p.op("act", lambda s, n=ns, b=bank, MM=M, d2=dst: s.activation(
                    d2[0:MM, n * 512:(n + 1) * 512], b[0:MM, 0:512], AF.Tanh),
                    reads=[bkey], writes=[("xx", mt, ns)])
        xx_keys = [("xx", m, n) for m in range(2) for n in range(4)]

        stg = AR.place("stg", 32768 - 0 + 0, 0) if False else None
        AR_stage = AR.place("stage", 32768, 1024, BF16)
        for j, nm in enumerate(["w", "k", "v", "r", "g"]):
            if j < 4:
                lh = w2pk_s
                lbase = 32 * j
                rsrc = xxa
                lkey, rkeys = "w2pk", [("xx", 0, n) for n in range(4)]
            else:
                lh = w2p4_s
                lbase = 0
                rsrc = xxb
                lkey, rkeys = "w2p4", [("xx", 1, n) for n in range(4)]
            for fc in range(NF):
                for ns in range(4):
                    bank = PB[4 + (ns % 2)]
                    bkey = ("PB", 4 + (ns % 2))
                    p.op("pe", lambda t, f=fc, n=ns, b=bank, l=lh, lb=lbase, r=rsrc:
                         t.matmul(b[:, 0:512],
                                  l[lb:lb + 32, f * 128:(f + 1) * 128],
                                  r[lb:lb + 32, n * 512:(n + 1) * 512],
                                  start=True, stop=True,
                                  tile_position=(lb, 0)),
                         reads=[lkey] + rkeys, writes=[bkey])
                    p.op("dve", lambda v, f=fc, n=ns, b=bank, jj=j:
                         v.scalar_tensor_tensor(
                             msc[:],
                             b[:, 0:512],
                             maav[:, f * 7 + 1 + jj:f * 7 + 2 + jj],
                             dxTb[:, f * T + n * 512:f * T + (n + 1) * 512],
                             ALU.add, ALU.mult),
                         reads=[bkey, "maav"] + dx_keys, writes=["msc"])
                    p.op("dve", lambda v, f=fc, n=ns: v.tensor_tensor(
                        mixb[:, f * T + n * 512:f * T + (n + 1) * 512],
                        msc[:], xTb1[:, f * T + n * 512:f * T + (n + 1) * 512],
                        ALU.add),
                        reads=["msc"] + xT1_keys, writes=[("mixb", fc)])
            if j == 0:
                for ns in range(4):
                    bank, bkey = PB[0], ("PB", 0)
                    for dc in range(NF):
                        p.op("pe", lambda t, n=ns, d=dc, b=bank: t.matmul(
                            b[0:64, 0:512],
                            d1_s[:, d * 64:(d + 1) * 64],
                            mixb[:, d * T + n * 512:d * T + (n + 1) * 512],
                            start=(d == 0), stop=(d == NF - 1)),
                            reads=[("d1s", dc)] + mix_keys, writes=[bkey])
                    p.op("act", lambda s, n=ns, b=bank: s.activation(
                        t1s[0:64, n * 512:(n + 1) * 512], b[0:64, 0:512],
                        AF.Tanh),
                        reads=[bkey], writes=[("t1s", ns)])
            else:
                wsrc = {"k": l1_wk, "v": l1_wv, "r": l1_wr, "g": l1_wg}[nm]
                for dc in range(NF):
                    p.dma("gp", wjs[:, dc * D:(dc + 1) * D],
                          wsrc[dc * 128:(dc + 1) * 128, :],
                          writes=[("wjs", dc)])
                for mt in range(NF):
                    for ns in range(4):
                        bank = PB[(mt * 4 + ns) % 2]
                        bkey = ("PB", (mt * 4 + ns) % 2)
                        for dc in range(NF):
                            p.op("pe", lambda t, m=mt, n=ns, d=dc, b=bank:
                                 t.matmul(b[:, 0:512],
                                          wjs[:, d * D + m * 128:d * D + (m + 1) * 128],
                                          mixb[:, d * T + n * 512:d * T + (n + 1) * 512],
                                          start=(d == 0), stop=(d == NF - 1)),
                                 reads=[("wjs", dc)] + (mix_keys if dc == 0 else []),
                                 writes=[bkey])
                        if nm == "g":
                            p.op("act", lambda s, n=ns, b=bank: s.activation(
                                AR_stage[:, n * 512:(n + 1) * 512],
                                b[:, 0:512], AF.Silu),
                                reads=[bkey], writes=[("stage", ns)])
                        else:
                            p.op("dve", lambda v, n=ns, b=bank: v.tensor_copy(
                                AR_stage[:, n * 512:(n + 1) * 512], b[:, 0:512]),
                                reads=[bkey], writes=[("stage", ns)])
                    p.dma("sp", spill_rkvg[nm][mt * 128:(mt + 1) * 128, :],
                          AR_stage[:],
                          reads=[("stage", 0), ("stage", 1), ("stage", 2),
                                 ("stage", 3)])

        AR.free("xTb1", "dxTb", "mixb", "stage", "w1maa", "w2pk", "w2p4",
                "d1s", "xxa", "xxb", "msc", "stream")
        rg_s = AR.place("rg", 0, 1024, BF16)
        kg_s = AR.place("kg", 1024, 1024, BF16)
        vg_s = AR.place("vg", 2048, 1024, BF16)
        wg_s = AR.place("wg", 3072, 2048)
        cs_s = AR.place("cs", 5120, 2048)
        cb_s = AR.place("cb", 7168, 2048)
        eff = AR.place("eff", 9216, 2048)
        rfb = AR.place("rfb", 11264, 1024, BF16)
        rbb = AR.place("rbb", 12288, 1024, BF16)
        kfT = AR.place("kfT", 13312, 1024, BF16)
        kbT = AR.place("kbT", 14336, 1024, BF16)
        kf_tok = AR.place("kf_tok", 15360, 1024, BF16)
        kb_tok = AR.place("kb_tok", 16384, 1024, BF16)
        v_tok = AR.place("v_tok", 17408, 1024, BF16)
        sS = AR.place("sS", 18432, 32, BF16)
        sSf = AR.place("sSf", 18464, 64)
        merged = AR.place("merged", 18528, 128, BF16)
        yb = AR.place("yb", 18656, 1024, BF16)
        gnsc = AR.place("gnsc", 19680, 96)
        gg_s = AR.place("gg", 19776, 1024, BF16)
        onesT = AR.place("onesT", 20800, 1024, BF16)
        lnw_s = AR.place("lnw", 21824, 512, BF16)
        lnb_s = AR.place("lnb", 22336, 512, BF16)
        sfr = AR.place("sfr", 22848, 64)
        sbr = AR.place("sbr", 22912, 64)
        Sfp = AR.place("Sfp", 22976, 512, BF16)
        Sbp = AR.place("Sbp", 23488, 512, BF16)
        merged2 = AR.place("merged2", 24000, 128, BF16)
        yngT = AR.place("yngT", 24576, 8192, BF16)

        p.op("dve", lambda v: v.memset(onesT[:], 1.0), writes=["onesT"])
        p.dma("gp", lnw_s[:], l1_lnw[:], writes=["lnw"])
        p.dma("gp", lnb_s[:], l1_lnb[:], writes=["lnb"])
        trilm = sbt("trilm_t", [128, 128], U8).ap()
        p.dma("gp", trilm[:], l1_masks[:], writes=["trilm"])

        for g in range(8):
            for nm, dst, key in (("r", rg_s, "rg"), ("k", kg_s, "kg"),
                                 ("v", vg_s, "vg"), ("g", gg_s, "gg")):
                p.dma("sp", dst[:], spill_rkvg[nm][g * 128:(g + 1) * 128, :],
                      writes=[key])
            for ns in range(4):
                bank, bkey = PB[0], ("PB", 0)
                p.op("pe", lambda t, n=ns, b=bank, gg_=g: t.matmul(
                    b[:, 0:512],
                    d2_s[0:64, gg_ * 128:(gg_ + 1) * 128],
                    t1s[0:64, n * 512:(n + 1) * 512],
                    start=True, stop=True),
                    reads=["d2s", ("t1s", ns)], writes=[bkey])
                p.op("act", lambda s, n=ns, b=bank, gg_=g: s.activation(
                    eff[:, n * 512:(n + 1) * 512], b[:, 0:512],
                    AF.Exp, bias=maav[:, gg_ * 7 + 6:gg_ * 7 + 7]),
                    reads=[bkey, "maav"], writes=["eff"])
            p.op("dve", lambda v: v.tensor_tensor_scan(
                cs_s[:], onesT[:], eff[:], 0.0, ALU.mult, ALU.subtract),
                reads=["eff", "onesT"], writes=["cs"])
            p.op("dve", lambda v: v.tensor_tensor(
                cb_s[:], cs_s[:], eff[:], ALU.add),
                reads=["cs", "eff"], writes=["cb"])
            p.op("dve", lambda v: v.tensor_scalar(
                cs_s[:], cs_s[:], cs_s[:, 1024:1025], 60.0,
                ALU.subtract, ALU.min),
                reads=[], writes=["cs"])
            p.op("dve", lambda v: v.tensor_scalar_max(cs_s[:], cs_s[:], -60.0),
                 reads=[], writes=["cs"])
            p.op("dve", lambda v: v.tensor_scalar(
                cb_s[:], cb_s[:], cb_s[:, 1024:1025], 60.0,
                ALU.subtract, ALU.min),
                reads=[], writes=["cb"])
            p.op("dve", lambda v: v.tensor_scalar_max(cb_s[:], cb_s[:], -60.0),
                 reads=[], writes=["cb"])
            p.op("act", lambda s: s.activation(eff[:], cs_s[:], AF.Exp),
                 reads=["cs"], writes=["eff"])
            p.op("act", lambda s: s.activation(wg_s[:], cs_s[:], AF.Exp,
                                               scale=-1.0),
                 reads=["cs"], writes=["wg"])
            p.op("dve", lambda v: v.tensor_tensor(rfb[:], rg_s[:], eff[:],
                                                  ALU.mult),
                 reads=["rg", "eff"], writes=["rfb"])
            p.op("dve", lambda v: v.tensor_tensor(kfT[:], kg_s[:], wg_s[:],
                                                  ALU.mult),
                 reads=["kg", "wg"], writes=["kfT"])
            p.op("act", lambda s: s.activation(eff[:], cb_s[:], AF.Exp,
                                               scale=-1.0),
                 reads=["cb"], writes=["eff"])
            p.op("act", lambda s: s.activation(wg_s[:], cb_s[:], AF.Exp),
                 reads=["cb"], writes=["wg"])
            p.op("dve", lambda v: v.tensor_tensor(rbb[:], rg_s[:], eff[:],
                                                  ALU.mult),
                 reads=["rg", "eff"], writes=["rbb"])
            p.op("dve", lambda v: v.tensor_tensor(kbT[:], kg_s[:], wg_s[:],
                                                  ALU.mult),
                 reads=["kg", "wg"], writes=["kbT"])
            for srcb, skey, dst, dkey in ((kfT, "kfT", kf_tok, "kf_tok"),
                                          (kbT, "kbT", kb_tok, "kb_tok"),
                                          (vg_s, "vg", v_tok, "v_tok")):
                for m in range(NT):
                    tp128(dst[:, m * 128:(m + 1) * 128],
                          srcb[:, m * 128:(m + 1) * 128],
                          [skey], [(dkey, m)], ROT_C)
            kfk = [("kf_tok", m) for m in range(NT)]
            kbk = [("kb_tok", m) for m in range(NT)]
            vk = [("v_tok", m) for m in range(NT)]

            p.op("dve", lambda v: v.memset(sfr[:], 0.0), writes=["sfr"])
            p.op("dve", lambda v: v.memset(sbr[:], 0.0), writes=["sbr"])
            zf_banks = [(PSM, ("PSM",)), (PB[0], ("PB", 0))]
            zb_banks = [(PB[2], ("PB", 2)), (PB[5], ("PB", 5))]

            def tok_ap(buf, m0, h):
                return bass.AP(buf.tensor, buf.offset + m0 + h,
                               [[buf.ap[0][0], 128], [1, 64]])

            for m in range(NT):
                mb = NT - 1 - m
                mm0 = m * 128
                abank = PB[3] if m % 2 == 0 else PB[4]
                akey = ("PB", 3) if m % 2 == 0 else ("PB", 4)
                ybank = PB[1] if m % 2 == 0 else PTRf
                ykey = ("PB", 1) if m % 2 == 0 else ("PTR",)
                mrg = merged if m % 2 == 0 else merged2
                mk = "merged" if m % 2 == 0 else "merged2"
                for hh in range(2):
                    hb = hh * 64
                    co = hh * 256
                    p.op("pe", lambda t, h=hb, c=co, m0=mm0, b=abank: t.matmul(
                        b[0:64, c:c + 64],
                        kfT[h:h + 64, m0:m0 + 64],
                        rfb[h:h + 64, m0:m0 + 64],
                        start=True, stop=True, tile_position=(h, 0)),
                        reads=["kfT", "rfb"], writes=[akey])
                    p.op("pe", lambda t, h=hb, c=co, m0=mm0, b=abank: t.matmul(
                        b[0:128, c + 64:c + 128],
                        kfT[h:h + 64, m0:m0 + 128],
                        rfb[h:h + 64, m0 + 64:m0 + 128],
                        start=True, stop=True, tile_position=(h, 0)),
                        reads=["kfT", "rfb"], writes=[akey])
                    p.op("pe", lambda t, h=hb, c=co, m0=mm0, b=abank: t.matmul(
                        b[0:64, c + 128:c + 192],
                        kbT[h:h + 64, m0:m0 + 64],
                        rbb[h:h + 64, m0:m0 + 64],
                        start=True, stop=True, tile_position=(h, 0)),
                        reads=["kbT", "rbb"], writes=[akey])
                    p.op("pe", lambda t, h=hb, c=co, m0=mm0, b=abank: t.matmul(
                        b[64:128, c + 128:c + 192],
                        kbT[h:h + 64, m0 + 64:m0 + 128],
                        rbb[h:h + 64, m0:m0 + 64],
                        start=True, stop=True, tile_position=(h, 64)),
                        reads=["kbT", "rbb"], writes=[akey])
                    p.op("pe", lambda t, h=hb, c=co, m0=mm0, b=abank: t.matmul(
                        b[64:128, c + 192:c + 256],
                        kbT[h:h + 64, m0 + 64:m0 + 128],
                        rbb[h:h + 64, m0 + 64:m0 + 128],
                        start=True, stop=True, tile_position=(h, 64)),
                        reads=["kbT", "rbb"], writes=[akey])
                    mo = hh * 128
                    p.op("dve", lambda v, c=co, o=mo, b=abank, g_=mrg:
                         v.tensor_copy(g_[0:128, o:o + 128],
                                       b[0:128, c:c + 128]),
                         reads=[akey], writes=[(mk, hh)])
                    p.op("dve", lambda v, c=co, o=mo, b=abank, g_=mrg:
                         v.tensor_copy(g_[64:128, o:o + 64],
                                       b[64:128, c + 128:c + 192]),
                         reads=[akey], writes=[(mk, hh)])
                    p.op("dve", lambda v, c=co, o=mo, b=abank, g_=mrg:
                         v.copy_predicated(
                             g_[0:64, o:o + 64], trilm[0:64, 64:128],
                             b[0:64, c + 128:c + 192]),
                         reads=[akey, "trilm"], writes=[(mk, hh)])
                    p.op("dve", lambda v, c=co, o=mo, b=abank, g_=mrg:
                         v.copy_predicated(
                             g_[64:128, o + 64:o + 128], trilm[64:128, 64:128],
                             b[64:128, c + 192:c + 256]),
                         reads=[akey, "trilm"], writes=[(mk, hh)])
                    p.op("pe", lambda t, h=hb, o=mo, m0=mm0, b=ybank, g_=mrg,
                         st=(hh == 0), sp=(hh == 1): t.matmul(
                        b[:, h:h + 64],
                        g_[:, o:o + 128],
                        tok_ap(v_tok, m0, h),
                        start=st, stop=sp),
                        reads=[(mk, hh)] + vk, writes=[ykey])
                p.op("dve", lambda v, m0=mm0, b=ybank: v.tensor_copy(
                    yb[:, m0:m0 + 128], b[:, 0:128]),
                    reads=[ykey], writes=[("yb", m)])
                p.op("dve", lambda v, m0=mm0: v.tensor_copy(
                    Sfp[:, m0 // 2:m0 // 2 + 64], sfr[:]),
                    reads=["sfr"], writes=["Sfp"])
                for hh in range(2):
                    hb = hh * 64
                    zb_, zk = zf_banks[hh]
                    p.op("pe", lambda t, h=hb, m0=mm0, b=zb_: t.matmul(
                        b[h:h + 64, 0:64],
                        tok_ap(kf_tok, m0, h),
                        tok_ap(v_tok, m0, h),
                        start=True, stop=True, tile_position=(0, h)),
                        reads=kfk + vk, writes=[zk])
                    p.op("dve", lambda v, h=hb, b=zb_: v.tensor_tensor(
                        sfr[h:h + 64, :], sfr[h:h + 64, :], b[h:h + 64, 0:64],
                        ALU.add),
                        reads=[zk], writes=["sfr"])
                p.op("dve", lambda v, m0=mb * 128: v.tensor_copy(
                    Sbp[:, m0 // 2:m0 // 2 + 64], sbr[:]),
                    reads=["sbr"], writes=["Sbp"])
                for hh in range(2):
                    hb = hh * 64
                    zb_, zk = zb_banks[hh]
                    p.op("pe", lambda t, h=hb, m0=mb * 128, b=zb_: t.matmul(
                        b[h:h + 64, 0:64],
                        tok_ap(kb_tok, m0, h),
                        tok_ap(v_tok, m0, h),
                        start=True, stop=True, tile_position=(0, h)),
                        reads=kbk + vk, writes=[zk])
                    p.op("dve", lambda v, h=hb, b=zb_: v.tensor_tensor(
                        sbr[h:h + 64, :], sbr[h:h + 64, :], b[h:h + 64, 0:64],
                        ALU.add),
                        reads=[zk], writes=["sbr"])

            xb_h = [[(PB[3], ("PB", 3)), (PB[4], ("PB", 4))],
                    [(PSM, ("PSM",)), (PB[0], ("PB", 0))]]
            for m in range(NT):
                mm0 = m * 128
                for hh in range(2):
                    hb = hh * 64
                    xbank, xkey = xb_h[hh][m % 2]
                    if m > 0:
                        p.op("pe", lambda t, h=hb, m0=mm0, b=xbank,
                             sp=(m == NT - 1): t.matmul(
                            b[:, h:h + 64],
                            rfb[h:h + 64, m0:m0 + 128],
                            Sfp[h:h + 64, m0 // 2:m0 // 2 + 64],
                            start=True, stop=sp, tile_position=(h, 0)),
                            reads=["rfb", "Sfp"], writes=[xkey])
                    if m < NT - 1:
                        p.op("pe", lambda t, h=hb, m0=mm0, b=xbank,
                             st=(m == 0): t.matmul(
                            b[:, h:h + 64],
                            rbb[h:h + 64, m0:m0 + 128],
                            Sbp[h:h + 64, m0 // 2:m0 // 2 + 64],
                            start=st, stop=True, tile_position=(h, 0)),
                            reads=["rbb", "Sbp"], writes=[xkey])
                    p.op("dve", lambda v, h=hb, m0=mm0, b=xbank:
                         v.tensor_tensor(
                             yb[:, m0 + h:m0 + h + 64],
                             yb[:, m0 + h:m0 + h + 64],
                             b[:, h:h + 64], ALU.add),
                         reads=[xkey, ("yb", m)], writes=[("yb", m)])
            yb_keys = [("yb", m) for m in range(NT)]
            p.op("dve", lambda v: v.tensor_reduce(
                gnsc[:, 0:32],
                bass.AP(yb.tensor, yb.offset,
                        [[yb.ap[0][0], 128], [128, 16], [64, 2], [1, 64]]),
                AX.X, ALU.add),
                reads=yb_keys, writes=["gn0"])
            p.op("act", lambda s: s.activation(eff[:], yb[:], AF.Square),
                 reads=yb_keys, writes=["eff"])
            p.op("dve", lambda v: v.tensor_reduce(
                gnsc[:, 32:64],
                bass.AP(eff.tensor, eff.offset,
                        [[eff.ap[0][0], 128], [128, 16], [64, 2], [1, 64]]),
                AX.X, ALU.add),
                reads=["eff"], writes=["gn1"])
            p.op("dve", lambda v: v.tensor_scalar_mul(
                gnsc[:, 0:32], gnsc[:, 0:32], 1.0 / 64),
                reads=[], writes=["gn0"])
            p.op("dve", lambda v: v.tensor_tensor(
                gnsc[:, 64:96], gnsc[:, 0:32], gnsc[:, 0:32], ALU.mult),
                reads=["gn0"], writes=["gn2"])
            p.op("dve", lambda v: v.scalar_tensor_tensor(
                gnsc[:, 32:64], gnsc[:, 32:64], 1.0 / 64, gnsc[:, 64:96],
                ALU.mult, ALU.subtract),
                reads=["gn2"], writes=["gn1"])
            p.op("dve", lambda v: v.tensor_scalar_add(
                gnsc[:, 32:64], gnsc[:, 32:64], 64e-5),
                reads=[], writes=["gn1"])
            p.op("dve", lambda v: v.reciprocal(gnsc[:, 32:64], gnsc[:, 32:64]),
                 reads=[], writes=["gn1"])
            p.op("act", lambda s: s.activation(
                gnsc[:, 32:64], gnsc[:, 32:64], AF.Sqrt),
                reads=[], writes=["gn1"])
            mu_bc = bass.AP(gnsc.tensor, gnsc.offset,
                            [[gnsc.ap[0][0], 128], [2, 16], [1, 2], [0, 64]])
            rho_bc = bass.AP(gnsc.tensor, gnsc.offset + 32,
                             [[gnsc.ap[0][0], 128], [2, 16], [1, 2], [0, 64]])
            lnw_bc = bass.AP(lnw_s.tensor, lnw_s.offset + g * 128,
                             [[lnw_s.ap[0][0], 128], [0, 16], [64, 2], [1, 64]])
            lnb_bc = bass.AP(lnb_s.tensor, lnb_s.offset + g * 128,
                             [[lnb_s.ap[0][0], 128], [0, 16], [64, 2], [1, 64]])
            p.op("dve", lambda v: v.tensor_tensor(yb[:], yb[:], mu_bc,
                                                  ALU.subtract),
                 reads=["gn0"], writes=yb_keys)
            p.op("dve", lambda v: v.tensor_tensor(yb[:], yb[:], rho_bc,
                                                  ALU.mult),
                 reads=["gn1"], writes=yb_keys)
            p.op("dve", lambda v: v.tensor_tensor(yb[:], yb[:], lnw_bc,
                                                  ALU.mult),
                 reads=["lnw"], writes=yb_keys)
            p.op("dve", lambda v: v.tensor_tensor(yb[:], yb[:], lnb_bc,
                                                 ALU.add),
                 reads=["lnb"], writes=yb_keys)
            for m in range(NT):
                bank, bkey = ROT_B.nxt()
                p.op("pe", lambda t, m0=m * 128, b=bank: t.transpose(
                    b[:, 0:128], yb[:, m0:m0 + 128], ident[:]),
                    reads=yb_keys + ["ident"], writes=[bkey])
                p.op("dve", lambda v, m0=m * 128, gg_=g, b=bank: v.tensor_tensor(
                    yngT[:, gg_ * T + m0:gg_ * T + m0 + 128],
                    b[:, 0:128], gg_s[:, m0:m0 + 128], ALU.mult),
                    reads=[bkey, "gg"], writes=[("yngT", g, m)])

        AR.free("rg", "kg", "vg", "wg", "cs", "cb", "eff", "rfb", "rbb",
                "kfT", "kbT", "kf_tok", "kb_tok", "v_tok", "sS", "merged",
                "yb", "gnsc", "gg", "onesT", "lnw", "lnb", "sSf",
                "sfr", "sbr", "Sfp", "Sbp", "merged2")
        ytmp2 = AR.place("ytmp2", 0, 8192, BF16)
        stream2 = AR.place("stream2", 8192, 16384)
        yng_keys = [("yngT", g, m) for g in range(8) for m in range(NT)]
        for dc in range(NF):
            p.dma("gp", wjs[:, dc * D:(dc + 1) * D],
                  l1_wo[dc * 128:(dc + 1) * 128, :], writes=[("wjs", dc)])
        for tt in range(NT):
            p.dma("sp", stream2[:, tt * D:(tt + 1) * D],
                  spill_x[tt * 128:(tt + 1) * 128, :], writes=[("stream2", tt)])
        for tt in range(NT):
            for ns in range(2):
                bank = PB[ns]
                bkey = ("PB", ns)
                for fc in range(NF):
                    rd = [("wjs", fc)]
                    if fc == 0:
                        rd = rd + yng_keys
                    p.op("pe", lambda t, c=tt, n=ns, f=fc, b=bank: t.matmul(
                        b[:, 0:512],
                        yngT[:, f * T + c * 128:f * T + (c + 1) * 128],
                        wjs[:, f * D + n * 512:f * D + (n + 1) * 512],
                        start=(f == 0), stop=(f == NF - 1)),
                        reads=rd, writes=[bkey])
                p.op("dve", lambda v, c=tt, n=ns, b=bank: v.tensor_copy(
                    ytmp2[:, c * D + n * 512:c * D + (n + 1) * 512],
                    b[:, 0:512]),
                    reads=[bkey], writes=[("ytmp2", tt, ns)])
        ln_stats(lambda tt: [("ytmp2", tt, 0), ("ytmp2", tt, 1)],
                 lambda tt: ytmp2[:, tt * D:(tt + 1) * D], muB, rhoB,
                 "muB", "rhoB")
        for tt in range(NT):
            p.op("dve", lambda v, c=tt: v.tensor_scalar(
                ytmp2[:, c * D:(c + 1) * D], ytmp2[:, c * D:(c + 1) * D],
                muB[:, c:c + 1], rhoB[:, c:c + 1], ALU.subtract, ALU.mult),
                reads=[("muB", tt), ("rhoB", tt)],
                writes=[("ytmp2", tt, 0), ("ytmp2", tt, 1), ("ya3", tt)])
            p.op("dve", lambda v, c=tt: v.tensor_tensor(
                stream2[:, c * D:(c + 1) * D], stream2[:, c * D:(c + 1) * D],
                ytmp2[:, c * D:(c + 1) * D], ALU.add),
                reads=[("ya3", tt)], writes=[("stream2", tt)])
            p.dma("sp", spill_x2[tt * 128:(tt + 1) * 128, :],
                  stream2[:, tt * D:(tt + 1) * D], reads=[("stream2", tt)])
        STREAM_FINAL = ("stream2", stream2)
    else:
        STREAM_FINAL = ("stream", stream)

    DO_L1F = DO_L1A and build_until not in ("l1a",)
    if DO_L1F:
        NX1 = T * D
        NH2 = T * 4096
        ix1 = int(np.floor(0.995 * (8 * NX1 - 1)))
        fr1 = 0.995 * (8 * NX1 - 1) - ix1
        ka1 = 8 * NX1 - 1 - ix1 + 1
        kb1 = ka1 - 1
        ix2 = int(np.floor(0.995 * (8 * NH2 - 1)))
        fr2 = 0.995 * (8 * NH2 - 1) - ix2
        ka2 = 8 * NH2 - 1 - ix2 + 1
        kb2 = ka2 - 1

        AR.free("ytmp2", "yngT", "wjs", "d2s", "t1s", "maav")
        cc_in = nc.dram_tensor("cc_in", [1, 16], F32).ap()
        cc_out = nc.dram_tensor("cc_out", [1, 16], F32).ap()
        ag_in = nc.dram_tensor("ag_in", [128, 32], F32).ap()
        ag_out = nc.dram_tensor("ag_out", [1024, 32], F32).ap()

        xq = AR.place("xq", 24576, 16384)
        cmpscr = AR.place("cmpscr", 0, 8192, BF16)
        qs2 = AR.place("qs2", 49152, 32)
        qsm8 = AR.place("qsm", 49184, 1).bitcast(U8)
        gath = AR.place("gath", 49188, 256)
        gsf = AR.place("gscr2", 49444, 240)
        hscr = AR.place("hscr", 49684, 64, BF16)
        lnwf = AR.place("lnwf", 49748, 512, BF16)
        lnbf = AR.place("lnbf", 50260, 512, BF16)
        qsc = AR.place("qsc", 50772, 1024)

        p.dma("gp", lnwf[:], l1f_lnw[:], writes=["lnwf"])
        p.dma("gp", lnbf[:], l1f_lnb[:], writes=["lnbf"])


        ln_stats(lambda tt: [("stream2", tt)],
                 lambda tt: stream2[:, tt * D:(tt + 1) * D], muA, rhoA,
                 "muA", "rhoA")
        for tt in range(NT):
            p.op("dve", lambda v, c=tt: v.tensor_scalar(
                xq[:, c * D:(c + 1) * D], stream2[:, c * D:(c + 1) * D],
                muA[:, c:c + 1], rhoA[:, c:c + 1], ALU.subtract, ALU.mult),
                reads=[("stream2", tt), ("muA", tt), ("rhoA", tt)],
                writes=[("xq", tt)])
            p.op("dve", lambda v, c=tt: v.tensor_tensor(
                xq[:, c * D:(c + 1) * D], xq[:, c * D:(c + 1) * D],
                lnwf[:], ALU.mult),
                reads=["lnwf"], writes=[("xq", tt)])
            p.op("dve", lambda v, c=tt: v.tensor_tensor(
                xq[:, c * D:(c + 1) * D], xq[:, c * D:(c + 1) * D],
                lnbf[:], ALU.add),
                reads=["lnbf"], writes=[("xq", tt)])
        AR.free("stream2")
        absx = AR.place("absx", 8192, 16384)
        for tt in range(NT):
            p.op("act", lambda s, c=tt: s.activation(
                absx[:, c * D:(c + 1) * D], xq[:, c * D:(c + 1) * D],
                AF.Abs),
                reads=[("xq", tt)], writes=[("absx", tt)])
        xq_keys = [("xq", tt) for tt in range(NT)]
        absx_keys = [("absx", tt) for tt in range(NT)]

        def psum_scalar(src_col, dst_col, bank=None, bkey=None):
            if bank is None:
                bank, bkey = PSM, ("PSM",)
            p.op("pe", lambda t, b=bank, sc=src_col: t.matmul(
                b[:, 0:1], ones_mat[:], qs2[:, sc:sc + 1],
                start=True, stop=True),
                reads=[("qs2", src_col), "ones_mat"], writes=[bkey])
            p.op("dve", lambda v, b=bank, dcol=dst_col: v.tensor_copy(
                qs2[:, dcol:dcol + 1], b[:, 0:1]),
                reads=[bkey], writes=[("qs2", dst_col)])

        def allreduce_scalar(col_from, col_to, op=ALU.add):
            p.dma("gp", cc_in[0:1, 0:1], qs2[0:1, col_from:col_from + 1],
                  reads=[("qs2", col_from)], writes=["cc_in"])
            p.op("gp", lambda e, o=op: e.collective_compute(
                "AllReduce", o, replica_groups=[list(range(8))],
                ins=[cc_in.tensor.ap().opt()], outs=[cc_out.tensor.ap().opt()]),
                reads=["cc_in"], writes=["cc_out"])
            p.dma("gp", qs2[0:1, 28:29],
                  cc_out[0:1, 0:1],
                  reads=["cc_out"], writes=[("qs2", 28)])
            p.op("pe", lambda t: t.matmul(
                PSM[:, 2:3], ones_row[0:1, 0:128], qs2[0:1, 28:29],
                start=True, stop=True),
                reads=[("qs2", 28), "ones_row"], writes=[("PSM",)])
            p.op("dve", lambda v, dcol=col_to: v.tensor_copy(
                qs2[:, dcol:dcol + 1], PSM[:, 2:3]),
                reads=[("PSM",)], writes=[("qs2", col_to)])

        def midpoint(lo, hi, tmp, tc):
            p.op("dve", lambda v, a=lo, b_=hi, c=tmp: v.tensor_tensor(
                qs2[:, c:c + 1], qs2[:, a:a + 1], qs2[:, b_:b_ + 1], ALU.add),
                reads=[("qs2", lo), ("qs2", hi)], writes=[("qs2", tmp)])
            p.op("dve", lambda v, c=tmp, d=tc: v.tensor_scalar_mul(
                qs2[:, d:d + 1], qs2[:, c:c + 1], 0.5),
                reads=[("qs2", tmp)], writes=[("qs2", tc)])

        def update_lohi(cnt_col, k, lo, hi, tc, m0, m1, track_cnthi):
            p.op("dve", lambda v, c=cnt_col, kk=float(k), m=m0: v.tensor_scalar(
                qsm8[:, m:m + 1], qs2[:, c:c + 1], kk, None, ALU.is_ge),
                reads=[("qs2", cnt_col)], writes=[("qsm", m0)])
            p.op("dve", lambda v, c=cnt_col, kk=float(k), m=m1: v.tensor_scalar(
                qsm8[:, m:m + 1], qs2[:, c:c + 1], kk, None, ALU.is_lt),
                reads=[("qs2", cnt_col)], writes=[("qsm", m1)])
            p.op("dve", lambda v, a=lo, d=tc, m=m0: v.copy_predicated(
                qs2[:, a:a + 1], qsm8[:, m:m + 1], qs2[:, d:d + 1]),
                reads=[("qsm", m0), ("qs2", tc)], writes=[("qs2", lo)])
            p.op("dve", lambda v, a=hi, d=tc, m=m1: v.copy_predicated(
                qs2[:, a:a + 1], qsm8[:, m:m + 1], qs2[:, d:d + 1]),
                reads=[("qsm", m1), ("qs2", tc)], writes=[("qs2", hi)])
            if track_cnthi:
                p.op("dve", lambda v, c=cnt_col, m=m1: v.copy_predicated(
                    qs2[:, 4:5], qsm8[:, m:m + 1], qs2[:, c:c + 1]),
                    reads=[("qsm", m1), ("qs2", cnt_col)], writes=[("qs2", 4)])

        def head_rounds(data_ap, data_keys, nH, mbuf, mname, k_head,
                        hi0, rounds, filler):
            p.op("dve", lambda v: v.memset(qs2[:, 0:1], 0.0),
                 writes=[("qs2", 0)])
            p.op("dve", lambda v, h0=hi0: v.memset(qs2[:, 1:2], h0),
                 writes=[("qs2", 1)])
            p.op("dve", lambda v: v.memset(qs2[:, 4:5], 0.0),
                 writes=[("qs2", 4)])
            for r in range(rounds):
                midpoint(0, 1, 2, 16)
                for hh in range(nH):
                    par = hh % 2
                    p.op("dve", lambda v, h=hh, pr=par: v.tensor_scalar(
                        mbuf[:, pr * 8192:(pr + 1) * 8192],
                        data_ap[:, h * 8192:(h + 1) * 8192],
                        qs2[:, 16:17], None, ALU.is_gt),
                        reads=data_keys + [("qs2", 16)],
                        writes=[(mname, par)])
                    for j in range(16):
                        p.op("pe", lambda t, pr=par, jj=j,
                             st=(hh == 0 and j == 0),
                             sp=(hh == nH - 1 and j == 15): t.matmul(
                            PSM[0:1, 0:512], ones_bf[:, 0:1],
                            mbuf[:, pr * 8192 + jj * 512:
                                 pr * 8192 + (jj + 1) * 512],
                            start=st, stop=sp),
                            reads=[(mname, par), "ones_bf"],
                            writes=[("PSM",)])
                p.op("dve", lambda v: v.tensor_reduce(
                    qs2[0:1, 18:19], PSM[0:1, 0:512], AX.X, ALU.add),
                    reads=[("PSM",)], writes=[("qs2", 18)])
                allreduce_scalar(18, 3)
                update_lohi(3, k_head, 0, 1, 16, 0, 1, True)
                if filler is not None:
                    filler(r)
            p.op("dve", lambda v: v.tensor_copy(qs2[:, 16:17], qs2[:, 1:2]),
                 reads=[("qs2", 1)], writes=[("qs2", 16)])

        def head_rounds3(data_ap, data_keys, nH, mbuf, mname, k_head,
                         hi0, rounds, filler):
            TCOL = (10, 16, 13)
            CCOL = (18, 19, 20)
            RCOL = (3, 2, 7)
            p.op("dve", lambda v: v.memset(qs2[:, 0:1], 0.0),
                 writes=[("qs2", 0)])
            p.op("dve", lambda v, h0=hi0: v.memset(qs2[:, 1:2], h0),
                 writes=[("qs2", 1)])
            p.op("dve", lambda v: v.memset(qs2[:, 4:5], 0.0),
                 writes=[("qs2", 4)])
            for r in range(rounds):
                midpoint(0, 1, 2, TCOL[1])
                midpoint(0, TCOL[1], 2, TCOL[0])
                midpoint(TCOL[1], 1, 2, TCOL[2])
                for j in range(3):
                    bank = PSM if j % 2 == 0 else PB[4]
                    bkey = ("PSM",) if j % 2 == 0 else ("PB", 4)
                    for hh in range(nH):
                        par = (j * nH + hh) % 2
                        p.op("dve", lambda v, h=hh, pr=par, tc=TCOL[j]:
                             v.tensor_scalar(
                                 mbuf[:, pr * 8192:(pr + 1) * 8192],
                                 data_ap[:, h * 8192:(h + 1) * 8192],
                                 qs2[:, tc:tc + 1], None, ALU.is_gt),
                             reads=data_keys + [("qs2", TCOL[j])],
                             writes=[(mname, par)])
                        for jj in range(16):
                            p.op("pe", lambda t, pr=par, j2=jj, b=bank,
                                 st=(hh == 0 and jj == 0),
                                 sp=(hh == nH - 1 and jj == 15): t.matmul(
                                b[0:1, 0:512], ones_bf[:, 0:1],
                                mbuf[:, pr * 8192 + j2 * 512:
                                     pr * 8192 + (j2 + 1) * 512],
                                start=st, stop=sp),
                                reads=[(mname, par), "ones_bf"],
                                writes=[bkey])
                    p.op("dve", lambda v, b=bank, cc=CCOL[j]: v.tensor_reduce(
                        qs2[0:1, cc:cc + 1], b[0:1, 0:512], AX.X, ALU.add),
                        reads=[bkey], writes=[("qs2", CCOL[j])])
                p.dma("gp", cc_in[0:1, 0:3], qs2[0:1, 18:21],
                      reads=[("qs2", 18), ("qs2", 19), ("qs2", 20)],
                      writes=["cc_in"])
                p.op("gp", lambda e: e.collective_compute(
                    "AllReduce", ALU.add, replica_groups=[list(range(8))],
                    ins=[cc_in.tensor.ap().opt()],
                    outs=[cc_out.tensor.ap().opt()]),
                    reads=["cc_in"], writes=["cc_out"])
                p.dma("gp", qs2[0:1, 18:21], cc_out[0:1, 0:3],
                      reads=["cc_out"],
                      writes=[("qs2", 18), ("qs2", 19), ("qs2", 20)])
                for j in range(3):
                    p.op("pe", lambda t, cc=CCOL[j], j2=j: t.matmul(
                        PSM[:, 2 + j2:3 + j2], ones_row[0:1, 0:128],
                        qs2[0:1, cc:cc + 1], start=True, stop=True),
                        reads=[("qs2", CCOL[j]), "ones_row"],
                        writes=[("PSM",)])
                    p.op("dve", lambda v, rc=RCOL[j], j2=j: v.tensor_copy(
                        qs2[:, rc:rc + 1], PSM[:, 2 + j2:3 + j2]),
                        reads=[("PSM",)], writes=[("qs2", RCOL[j])])
                for j in range(3):
                    p.op("dve", lambda v, rc=RCOL[j], kk=float(k_head):
                         v.tensor_scalar(
                             qsm8[:, 0:1], qs2[:, rc:rc + 1], kk, None,
                             ALU.is_ge),
                         reads=[("qs2", RCOL[j])], writes=[("qsm", 0)])
                    p.op("dve", lambda v, tc=TCOL[j]: v.copy_predicated(
                        qs2[:, 0:1], qsm8[:, 0:1], qs2[:, tc:tc + 1]),
                        reads=[("qsm", 0), ("qs2", TCOL[j])],
                        writes=[("qs2", 0)])
                for j in (2, 1, 0):
                    p.op("dve", lambda v, rc=RCOL[j], kk=float(k_head):
                         v.tensor_scalar(
                             qsm8[:, 1:2], qs2[:, rc:rc + 1], kk, None,
                             ALU.is_lt),
                         reads=[("qs2", RCOL[j])], writes=[("qsm", 1)])
                    p.op("dve", lambda v, tc=TCOL[j]: v.copy_predicated(
                        qs2[:, 1:2], qsm8[:, 1:2], qs2[:, tc:tc + 1]),
                        reads=[("qsm", 1), ("qs2", TCOL[j])],
                        writes=[("qs2", 1)])
                    p.op("dve", lambda v, rc=RCOL[j]: v.copy_predicated(
                        qs2[:, 4:5], qsm8[:, 1:2], qs2[:, rc:rc + 1]),
                        reads=[("qsm", 1), ("qs2", RCOL[j])],
                        writes=[("qs2", 4)])
                if filler is not None:
                    filler(r)
            p.op("dve", lambda v: v.tensor_copy(qs2[:, 16:17], qs2[:, 1:2]),
                 reads=[("qs2", 1)], writes=[("qs2", 16)])

        def allgather_cands():
            p.dma("gp", ag_in[:, 0:32], gsf[:, 0:32],
                  reads=["gscr2"], writes=["ag_in"])
            p.op("gp", lambda e: e.collective_compute(
                "AllGather", ALU.bypass, replica_groups=[list(range(8))],
                ins=[ag_in.tensor.ap().opt()], outs=[ag_out.tensor.ap().opt()]),
                reads=["ag_in"], writes=["ag_out"])
            for r8 in range(8):
                p.dma("gp", gath[:, r8 * 32:(r8 + 1) * 32],
                      ag_out[r8 * 128:(r8 + 1) * 128, :],
                      reads=["ag_out"], writes=["gath"])

        def fine_pair(ka, kb):
            lanes = [
                dict(rank=ka, vcol=14, lo=5, hi=6, tc=10, cnt=13, tmp=7,
                     acc=19, bank=PB[5], bkey=("PB", 5), m0=2, m1=3,
                     dm=qsc[:, 256:512], rounds=27),
                dict(rank=kb, vcol=15, lo=0, hi=1, tc=16, cnt=3, tmp=2,
                     acc=18, bank=PSM, bkey=("PSM",), m0=0, m1=1,
                     dm=qsc[:, 0:256], rounds=18),
            ]
            p.op("dve", lambda v: v.memset(qs2[:, 5:6], 0.0),
                 writes=[("qs2", 5)])
            p.op("dve", lambda v: v.tensor_copy(qs2[:, 6:7], qs2[:, 1:2]),
                 reads=[("qs2", 1)], writes=[("qs2", 6)])
            for r in range(max(L["rounds"] for L in lanes)):
                for L in lanes:
                    if r >= L["rounds"]:
                        continue
                    midpoint(L["lo"], L["hi"], L["tmp"], L["tc"])
                    p.op("dve", lambda v, d=L["tc"], a=L["acc"], dm=L["dm"]:
                         v.tensor_scalar(
                             dm, gath[:, 0:256], qs2[:, d:d + 1], None,
                             ALU.is_gt, ALU.add, accum_out=qs2[:, a:a + 1]),
                         reads=["gath", ("qs2", L["tc"])],
                         writes=["qsc", ("qs2", L["acc"])])
                    psum_scalar(L["acc"], L["cnt"], L["bank"], L["bkey"])
                    p.op("dve", lambda v, c=L["cnt"]: v.tensor_tensor(
                        qs2[:, c:c + 1], qs2[:, c:c + 1], qs2[:, 4:5],
                        ALU.add),
                        reads=[("qs2", 4)], writes=[("qs2", L["cnt"])])
                    update_lohi(L["cnt"], L["rank"], L["lo"], L["hi"],
                                L["tc"], L["m0"], L["m1"], False)
            for L in lanes:
                p.op("dve", lambda v, a=L["hi"], vc=L["vcol"]: v.tensor_copy(
                    qs2[:, vc:vc + 1], qs2[:, a:a + 1]),
                    reads=[("qs2", L["hi"])], writes=[("qs2", L["vcol"])])

        def amax_tail(frac, amax_col):
            p.op("dve", lambda v, f=frac: v.tensor_scalar_mul(
                qs2[:, 7:8], qs2[:, 14:15], float(1.0 - f)),
                reads=[("qs2", 14)], writes=[("qs2", 7)])
            p.op("dve", lambda v, f=frac: v.tensor_scalar(
                qs2[:, 19:20], qs2[:, 15:16], float(f), None, ALU.mult),
                reads=[("qs2", 15)], writes=[("qs2", 19)])
            p.op("dve", lambda v: v.tensor_tensor(
                qs2[:, 7:8], qs2[:, 7:8], qs2[:, 19:20], ALU.add),
                reads=[("qs2", 19)], writes=[("qs2", 7)])
            p.op("dve", lambda v, ac=amax_col: v.tensor_scalar_max(
                qs2[:, ac:ac + 1], qs2[:, 7:8], 1e-5),
                reads=[("qs2", 7)], writes=[("qs2", amax_col)])

        def mk_filler(chunks, per_round):
            it = iter(chunks)

            def filler(r):
                for _ in range(per_round):
                    f = next(it, None)
                    if f is not None:
                        f()
            return filler

        wst = AR.place("wst", 40960, 2048)
        p.op("dve", lambda v: v.memset(qs2[:, 21:22], 0.0),
             writes=[("qs2", 21)])
        g1_chunks = []
        for ck in range(16):
            def _g1(ck=ck):
                p.dma("sp", wst[:],
                      l1f_w1t[(ck // 2) * 128:(ck // 2 + 1) * 128,
                              (ck % 2) * 2048:(ck % 2 + 1) * 2048],
                      writes=["wst"])
                p.op("dve", lambda v: v.tensor_reduce(
                    qs2[:, 22:23], wst[:], AX.X, ALU.add,
                    apply_absolute_value=True),
                    reads=["wst"], writes=[("qs2", 22)])
                p.op("dve", lambda v: v.tensor_tensor(
                    qs2[:, 21:22], qs2[:, 21:22], qs2[:, 22:23], ALU.add),
                    reads=[("qs2", 22)], writes=[("qs2", 21)])
            g1_chunks.append(_g1)

        head_rounds3(absx, absx_keys, 2, cmpscr, "cmpscr", kb1, 40.0, 5,
                     mk_filler(g1_chunks, 4))
        cmF = AR.t.ap()[0:128, 0:4096]
        for s in range(4):
            p.op("dve", lambda v, s0=s: v.scalar_tensor_tensor(
                cmF, absx[:, s0 * 4096:(s0 + 1) * 4096], qs2[:, 16:17],
                absx[:, s0 * 4096:(s0 + 1) * 4096], ALU.is_le, ALU.mult),
                reads=absx_keys + [("qs2", 16)], writes=[("cmpscr", 0)])
            p.op("dve", lambda v, s0=s: v.max(
                gsf[:, 64 + s0 * 16:64 + s0 * 16 + 8], cmF),
                reads=[("cmpscr", 0)], writes=["gscr2"])
            p.op("dve", lambda v, s0=s: v.match_replace(
                cmF, gsf[:, 64 + s0 * 16:64 + s0 * 16 + 8], cmF, -1.0),
                reads=["gscr2"], writes=[("cmpscr", 0)])
            p.op("dve", lambda v, s0=s: v.max(
                gsf[:, 64 + s0 * 16 + 8:64 + s0 * 16 + 16], cmF),
                reads=[("cmpscr", 0)], writes=["gscr2"])
        for k4 in range(4):
            p.op("dve", lambda v, k=k4: v.max(
                gsf[:, k * 8:(k + 1) * 8], gsf[:, 64:128]),
                reads=["gscr2"], writes=["gscr2"])
            if k4 < 3:
                p.op("dve", lambda v, k=k4: v.match_replace(
                    gsf[:, 64:128], gsf[:, k * 8:(k + 1) * 8],
                    gsf[:, 64:128], -1.0),
                    reads=["gscr2"], writes=["gscr2"])
        allgather_cands()
        fine_pair(ka1, kb1)
        amax_tail(fr1, 8)
        p.op("dve", lambda v: v.reciprocal(qs2[:, 9:10], qs2[:, 8:9]),
             reads=[("qs2", 8)], writes=[("qs2", 9)])
        p.op("dve", lambda v: v.tensor_scalar_mul(
            qs2[:, 9:10], qs2[:, 9:10], 127.0),
            reads=[], writes=[("qs2", 9)])
        p.op("dve", lambda v: v.tensor_copy(qs2[:, 20:21], qs2[:, 9:10]),
             reads=[("qs2", 9)], writes=[("qs2", 20)])

        p.op("dve", lambda v: v.tensor_copy(qs2[:, 18:19], qs2[:, 21:22]),
             reads=[("qs2", 21)], writes=[("qs2", 18)])
        psum_scalar(18, 3)
        p.op("dve", lambda v: v.tensor_scalar(
            qs2[:, 3:4], qs2[:, 3:4], 1.0 / (4096.0 * 1024.0), 1e-5,
            ALU.mult, ALU.add),
            reads=[("qs2", 3)], writes=[("qs2", 3)])
        p.op("dve", lambda v: v.tensor_copy(qs2[:, 23:24], qs2[:, 3:4]),
             reads=[("qs2", 3)], writes=[("qs2", 23)])
        p.op("dve", lambda v: v.tensor_scalar_mul(
            qs2[:, 24:25], qs2[:, 23:24], 0.5),
            reads=[("qs2", 23)], writes=[("qs2", 24)])
        p.op("dve", lambda v: v.tensor_scalar_mul(
            qs2[:, 26:27], qs2[:, 23:24], -0.5),
            reads=[("qs2", 23)], writes=[("qs2", 26)])
        p.op("dve", lambda v: v.tensor_tensor(
            qs2[:, 7:8], qs2[:, 23:24], qs2[:, 8:9], ALU.mult),
            reads=[("qs2", 23), ("qs2", 8)], writes=[("qs2", 7)])
        p.op("dve", lambda v: v.tensor_scalar_mul(
            qs2[:, 25:26], qs2[:, 7:8], -1.0 / 127.0),
            reads=[("qs2", 7)], writes=[("qs2", 25)])

        AR.free("cmpscr", "absx", "wst")
        qbuf = AR.place("qbuf", 0, 8192, BF16)
        va = qsc[:, 0:1024]
        iax = AR.place("iax", 40960, 1024)
        ia = iax.bitcast(I32)

        def quantize3(src_ap_fn, src_keys_fn, dst_ap_fn, dst_keys_fn, ntiles,
                      s_bc_col):
            for tt in range(ntiles):
                p.op("dve", lambda v, c=tt: v.tensor_scalar_mul(
                    va, src_ap_fn(c), qs2[:, s_bc_col:s_bc_col + 1]),
                    reads=src_keys_fn(tt) + [("qs2", s_bc_col)],
                    writes=["qsc"])
                p.op("dve", lambda v: v.tensor_copy(ia, va),
                     reads=["qsc"], writes=["iax"])
                p.op("dve", lambda v, c=tt: v.tensor_scalar(
                    dst_ap_fn(c), ia, -128.0, 127.0, ALU.max, ALU.min),
                    reads=["iax"], writes=dst_keys_fn(tt))

        quantize3(lambda c: xq[:, c * D:(c + 1) * D],
                  lambda c: [("xq", c)],
                  lambda c: qbuf[:, c * D:(c + 1) * D],
                  lambda c: [("qbuf", c)], NT, 20)
        qb_keys = [("qbuf", tt) for tt in range(NT)]

        AR.free("iax")
        qT1 = AR.place("qT1", 40960, 8192, BF16)
        for tt in range(NT):
            for fc in range(NF):
                tp128(qT1[:, fc * T + tt * 128:fc * T + (tt + 1) * 128],
                      qbuf[:, tt * D + fc * 128:tt * D + (fc + 1) * 128],
                      [("qbuf", tt)], [("qT1", fc, tt)], ROT_A)
        qT1_keys = [("qT1", fc, tt) for fc in range(NF) for tt in range(NT)]

        AR.free("xq", "qbuf")
        hb_ = AR.place("hb", 8192, 32768, BF16)
        ternq = AR.place("ternq", 0, 4096, BF16)
        wst2a = AR.place("wst2a", 4096, 1024)
        wst2b = AR.place("wst2b", 5120, 1024)

        for qb4 in range(4):
            for dc in range(NF):
                wpar = (qb4 * NF + dc) % 2
                wbuf = wst2a if wpar == 0 else wst2b
                wkey = "wst2a" if wpar == 0 else "wst2b"
                p.dma("sp" if wpar == 0 else "gp", wbuf[:],
                      l1f_w1t[dc * 128:(dc + 1) * 128,
                              qb4 * 1024:(qb4 + 1) * 1024],
                      writes=[wkey])
                p.op("dve", lambda v, d=dc, w=wbuf, wk=wkey: v.tensor_scalar(
                    ternq[:, d * D:(d + 1) * D], w[:],
                    qs2[:, 24:25], None, ALU.is_gt),
                    reads=[wkey, ("qs2", 24)], writes=[("ternq", dc)])
                p.op("dve", lambda v, d=dc, w=wbuf, wk=wkey:
                     v.scalar_tensor_tensor(
                         ternq[:, d * D:(d + 1) * D], w[:],
                         qs2[:, 26:27], ternq[:, d * D:(d + 1) * D],
                         ALU.is_lt, ALU.subtract),
                     reads=[wkey, ("qs2", 26)], writes=[("ternq", dc)])
            for tt in range(NT):
                for sg in range(2):
                    bank = PB[(tt * 2 + sg) % 2]
                    bkey = ("PB", (tt * 2 + sg) % 2)
                    for dc in range(NF):
                        p.op("pe", lambda t, c=tt, s=sg, d=dc, b=bank: t.matmul(
                            b[:, 0:512],
                            qT1[:, d * T + c * 128:d * T + (c + 1) * 128],
                            ternq[:, d * D + s * 512:d * D + (s + 1) * 512],
                            start=(d == 0), stop=(d == NF - 1)),
                            reads=[("ternq", dc)] + ([("qT1", dd, tt)
                                   for dd in range(NF)] if dc == 0 else []),
                            writes=[bkey])
                    p.op("act", lambda s_, c=tt, s=sg, q4=qb4, b=bank:
                         s_.activation(
                             hb_[:, c * 4096 + q4 * 1024 + s * 512:
                                 c * 4096 + q4 * 1024 + (s + 1) * 512],
                             b[:, 0:512], AF.Gelu, scale=qs2[:, 25:26]),
                         reads=[bkey, ("qs2", 25)], writes=[("hb", tt, qb4, sg)])
        hb_keys = [("hb", tt, q4, sg) for tt in range(NT) for q4 in range(4)
                   for sg in range(2)]

        p.op("dve", lambda v: v.memset(qs2[:, 21:22], 0.0),
             writes=[("qs2", 21)])
        g2_chunks = []
        for ck in range(32):
            def _g2(ck=ck):
                p.dma("sp", qsc[:, 0:1024],
                      l1f_w2t[ck * 128:(ck + 1) * 128, :],
                      writes=["qsc"])
                p.op("dve", lambda v: v.tensor_reduce(
                    qs2[:, 22:23], qsc[:, 0:1024], AX.X, ALU.add,
                    apply_absolute_value=True),
                    reads=["qsc"], writes=[("qs2", 22)])
                p.op("dve", lambda v: v.tensor_tensor(
                    qs2[:, 21:22], qs2[:, 21:22], qs2[:, 22:23], ALU.add),
                    reads=[("qs2", 22)], writes=[("qs2", 21)])
            g2_chunks.append(_g2)

        AR.free("qT1")
        cmp2 = AR.place("cmp2", 40960, 8192, BF16)
        head_rounds(hb_, hb_keys, 8, cmp2, "cmp2", kb2, 64.0, 13,
                    mk_filler(g2_chunks, 3))
        cm2 = cmp2[:, 0:16384]
        for s in range(4):
            p.op("dve", lambda v, s0=s: v.scalar_tensor_tensor(
                cm2, hb_[:, s0 * 16384:(s0 + 1) * 16384], qs2[:, 16:17],
                hb_[:, s0 * 16384:(s0 + 1) * 16384], ALU.is_le, ALU.mult),
                reads=hb_keys + [("qs2", 16)],
                writes=[("cmp2", 0), ("cmp2", 1)])
            p.op("dve", lambda v, s0=s: v.max(
                hscr[:, s0 * 16:s0 * 16 + 8], cm2),
                reads=[("cmp2", 0), ("cmp2", 1)], writes=["hscr"])
            p.op("dve", lambda v, s0=s: v.match_replace(
                cm2, hscr[:, s0 * 16:s0 * 16 + 8], cm2, -1.0),
                reads=["hscr"], writes=[("cmp2", 0), ("cmp2", 1)])
            p.op("dve", lambda v, s0=s: v.max(
                hscr[:, s0 * 16 + 8:s0 * 16 + 16], cm2),
                reads=[("cmp2", 0), ("cmp2", 1)], writes=["hscr"])
        for k4 in range(4):
            p.op("dve", lambda v, k=k4: v.max(
                hscr[:, 64 + k * 8:64 + (k + 1) * 8], hscr[:, 0:64]),
                reads=["hscr"], writes=["hscr"])
            if k4 < 3:
                p.op("dve", lambda v, k=k4: v.match_replace(
                    hscr[:, 0:64], hscr[:, 64 + k * 8:64 + (k + 1) * 8],
                    hscr[:, 0:64], -1.0),
                    reads=["hscr"], writes=["hscr"])
        p.op("dve", lambda v: v.tensor_copy(gsf[:, 0:32], hscr[:, 64:96]),
             reads=["hscr"], writes=["gscr2"])
        allgather_cands()
        fine_pair(ka2, kb2)
        amax_tail(fr2, 11)
        p.op("dve", lambda v: v.reciprocal(qs2[:, 12:13], qs2[:, 11:12]),
             reads=[("qs2", 11)], writes=[("qs2", 12)])
        p.op("dve", lambda v: v.tensor_scalar_mul(
            qs2[:, 12:13], qs2[:, 12:13], 127.0),
            reads=[], writes=[("qs2", 12)])
        p.op("dve", lambda v: v.tensor_copy(qs2[:, 27:28], qs2[:, 12:13]),
             reads=[("qs2", 12)], writes=[("qs2", 27)])

        p.op("dve", lambda v: v.tensor_copy(qs2[:, 18:19], qs2[:, 21:22]),
             reads=[("qs2", 21)], writes=[("qs2", 18)])
        psum_scalar(18, 3)
        p.op("dve", lambda v: v.tensor_scalar(
            qs2[:, 3:4], qs2[:, 3:4], 1.0 / (4096.0 * 1024.0), 1e-5,
            ALU.mult, ALU.add),
            reads=[("qs2", 3)], writes=[("qs2", 3)])
        p.op("dve", lambda v: v.tensor_copy(qs2[:, 28:29], qs2[:, 3:4]),
             reads=[("qs2", 3)], writes=[("qs2", 28)])
        p.op("dve", lambda v: v.tensor_scalar_mul(
            qs2[:, 29:30], qs2[:, 28:29], 0.5),
            reads=[("qs2", 28)], writes=[("qs2", 29)])
        p.op("dve", lambda v: v.tensor_scalar_mul(
            qs2[:, 30:31], qs2[:, 28:29], -0.5),
            reads=[("qs2", 28)], writes=[("qs2", 30)])
        p.op("dve", lambda v: v.tensor_tensor(
            qs2[:, 7:8], qs2[:, 28:29], qs2[:, 11:12], ALU.mult),
            reads=[("qs2", 28), ("qs2", 11)], writes=[("qs2", 7)])
        p.op("dve", lambda v: v.tensor_scalar_mul(
            qs2[:, 31:32], qs2[:, 7:8], -1.0 / 127.0),
            reads=[("qs2", 7)], writes=[("qs2", 31)])

        vh = AR.t.ap()[0:128, 40960:45056]
        ih = AR.t.ap().bitcast(I32)[0:128, 45056:49152]

        def htile_keys(tt):
            return [("hb", tt, q4, sg) for q4 in range(4) for sg in range(2)]

        for tt in range(NT):
            p.op("dve", lambda v, c=tt: v.tensor_scalar_mul(
                vh, hb_[:, c * 4096:(c + 1) * 4096], qs2[:, 27:28]),
                reads=htile_keys(tt) + [("qs2", 27)], writes=[("cmp2", 0)])
            p.op("dve", lambda v: v.tensor_copy(ih, vh),
                 reads=[("cmp2", 0)], writes=[("cmp2", 1)])
            p.op("dve", lambda v, c=tt: v.tensor_scalar(
                hb_[:, c * 4096:(c + 1) * 4096], ih, -128.0, 127.0,
                ALU.max, ALU.min),
                reads=[("cmp2", 1)], writes=htile_keys(tt))

        AR.free("cmp2", "ternq", "wst2a", "wst2b", "lnwf", "lnbf")
        q2Tg = AR.place("q2Tg", 40960, 4096, BF16)
        tern2g = AR.place("tern2g", 45056, 2048, BF16)
        wst3a = AR.place("wst3a", 47104, 1024)
        wst3b = AR.place("wst3b", 48128, 1024)
        T2 = 1024
        accs = []
        hbs = []
        for half in range(2):
            if half == 0:
                acc = AR.place("acc2a", 0, 8192)
                anm = "acc2a"
                hsrc = hb_
            else:
                AR.free("hb")
                acc = AR.place("acc2b", 8192, 8192)
                anm = "acc2b"
                hsrc = AR.place("hb", 24576, 16384, BF16)
            accs.append((anm, acc))
            hbs.append(hsrc)
            for grp in range(8):
                for k4 in range(4):
                    kc = grp * 4 + k4
                    wpar = kc % 2
                    wbuf = wst3a if wpar == 0 else wst3b
                    wkey = "wst3a" if wpar == 0 else "wst3b"
                    p.dma("sp" if wpar == 0 else "gp", wbuf[:],
                          l1f_w2t[kc * 128:(kc + 1) * 128, :],
                          writes=[wkey])
                    p.op("dve", lambda v, k=k4, w=wbuf, wk=wkey: v.tensor_scalar(
                        tern2g[:, k * D:(k + 1) * D], w[:],
                        qs2[:, 29:30], None, ALU.is_gt),
                        reads=[wkey, ("qs2", 29)], writes=[("tern2g", k4)])
                    p.op("dve", lambda v, k=k4, w=wbuf, wk=wkey:
                         v.scalar_tensor_tensor(
                             tern2g[:, k * D:(k + 1) * D], w[:],
                             qs2[:, 30:31], tern2g[:, k * D:(k + 1) * D],
                             ALU.is_lt, ALU.subtract),
                         reads=[wkey, ("qs2", 30)], writes=[("tern2g", k4)])
                    for t8 in range(8):
                        tt = half * 8 + t8
                        tp128(q2Tg[:, k4 * T2 + t8 * 128:
                                   k4 * T2 + (t8 + 1) * 128],
                              hsrc[:, t8 * 4096 + kc * 128:
                                   t8 * 4096 + (kc + 1) * 128],
                              [("hb", tt, kc // 8, (kc % 8) // 4)],
                              [("q2Tg", k4, t8)], ROT_C)
                for t8 in range(8):
                    for sg in range(2):
                        bank = PB[(t8 * 2 + sg) % 2]
                        bkey = ("PB", (t8 * 2 + sg) % 2)
                        for k4 in range(4):
                            p.op("pe", lambda t, k=k4, c=t8, s=sg, b=bank:
                                 t.matmul(
                                     b[:, 0:512],
                                     q2Tg[:, k * T2 + c * 128:
                                          k * T2 + (c + 1) * 128],
                                     tern2g[:, k * D + s * 512:
                                            k * D + (s + 1) * 512],
                                     start=(k == 0), stop=(k == 3)),
                                 reads=[("tern2g", k4), ("q2Tg", k4, t8)],
                                 writes=[bkey])
                        if grp == 0:
                            p.op("dve", lambda v, c=t8, s=sg, b=bank,
                                 a=acc, an=anm: v.tensor_copy(
                                     a[:, c * D + s * 512:c * D + (s + 1) * 512],
                                     b[:, 0:512]),
                                 reads=[bkey], writes=[(anm, t8, sg)])
                        else:
                            p.op("dve", lambda v, c=t8, s=sg, b=bank,
                                 a=acc, an=anm: v.tensor_tensor(
                                     a[:, c * D + s * 512:c * D + (s + 1) * 512],
                                     a[:, c * D + s * 512:c * D + (s + 1) * 512],
                                     b[:, 0:512], ALU.add),
                                 reads=[bkey, (anm, t8, sg)], writes=[(anm, t8, sg)])
            for t8 in range(8):
                p.op("dve", lambda v, c=t8, a=acc, an=anm: v.tensor_scalar_mul(
                    a[:, c * D:(c + 1) * D], a[:, c * D:(c + 1) * D],
                    qs2[:, 31:32]),
                    reads=[(anm, t8, 0), (anm, t8, 1), ("qs2", 31)],
                    writes=[(anm, t8, 0), (anm, t8, 1)])

        AR.free("hb", "q2Tg", "tern2g", "wst3a", "wst3b", "qsc")
        stream3 = AR.place("stream3", 24576, 16384)

        def accv(tt):
            return accs[tt // 8][1][:, (tt % 8) * D:(tt % 8 + 1) * D]

        def acck(tt):
            an = accs[tt // 8][0]
            return [(an, tt % 8, 0), (an, tt % 8, 1)]

        for tt in range(NT):
            p.dma("sp", stream3[:, tt * D:(tt + 1) * D],
                  spill_x2[tt * 128:(tt + 1) * 128, :],
                  writes=[("stream3", tt)])
        ln_stats(acck, accv, muB, rhoB, "muB", "rhoB")
        for tt in range(NT):
            p.op("dve", lambda v, c=tt: v.tensor_scalar(
                accv(c), accv(c),
                muB[:, c:c + 1], rhoB[:, c:c + 1], ALU.subtract, ALU.mult),
                reads=[("muB", tt), ("rhoB", tt)] + acck(tt),
                writes=acck(tt) + [("yq3", tt)])
            p.op("dve", lambda v, c=tt: v.tensor_tensor(
                stream3[:, c * D:(c + 1) * D], stream3[:, c * D:(c + 1) * D],
                accv(c), ALU.add),
                reads=[("yq3", tt)], writes=[("stream3", tt)])
        STREAM_FINAL = ("stream3", stream3)

    skey, sap = STREAM_FINAL
    final = []
    for tt in range(NT):
        final.append(p.dma("sp", y_out[tt * 128:(tt + 1) * 128, :],
                           sap[:, tt * D:(tt + 1) * D],
                           reads=[(skey, tt)]))
    if dbg is not None:
        for tt in range(NT):
            final.append(p.dma("sp", dbg[tt * 128:(tt + 1) * 128, :],
                               sap[:, tt * D:(tt + 1) * D],
                               reads=[(skey, tt)]))

    import os
    lim = int(os.environ.get("KBISECT", "0"))
    if lim:
        p.ops = p.ops[:lim]
        final = [d for d in final if d < lim]
    streams, stack = p.build(final)
    with stack:
        with nc.Block() as block:
            block.tensor(lambda e: [f(e) for f in streams["pe"]])
            block.scalar(lambda e: [f(e) for f in streams["act"]])
            block.vector(lambda e: [f(e) for f in streams["dve"]])
            block.sync(lambda e: [f(e) for f in streams["sp"]])
            block.gpsimd(lambda e: [f(e) for f in streams["gp"]])
    return nc


def host_prep(inputs):
    import ml_dtypes
    d = {k: np.asarray(v) for k, v in inputs.items()}
    cst = {}
    cst["identc"] = np.eye(128).astype(ml_dtypes.bfloat16)
    vecs = np.concatenate([
        np.broadcast_to(d["L0_attn_b_out"], (128, D)),
        np.broadcast_to(d["L0_attn_gamma"], (128, D)),
        np.broadcast_to(d["L0_attn_beta"], (128, D))], axis=1)
    cst["l0a_vecs"] = np.ascontiguousarray(vecs).astype(ml_dtypes.bfloat16)
    cst["w_qkv"] = np.ascontiguousarray(d["L0_attn_w_qkv"], dtype=np.float32)
    cst["w_out"] = np.ascontiguousarray(d["L0_attn_w_out"], dtype=np.float32)
    W = np.asarray(d["L0_ffn_W"], np.float32)
    R = np.asarray(d["L0_ffn_R"], np.float32)
    A = np.asarray(d["L0_ffn_A"], np.float32).reshape(128, D)
    cst["l0f_w"] = np.ascontiguousarray(
        np.concatenate([W.T, R.T, A.T], axis=1), np.float32)
    Bm = np.asarray(d["L0_ffn_B"], np.float32)
    cst["l0f_bt"] = np.ascontiguousarray(
        Bm.transpose(0, 2, 1).reshape(128, D), np.float32)
    cst["l0f_b"] = np.ascontiguousarray(
        np.broadcast_to(d["L0_ffn_b"], (128, D))).astype(ml_dtypes.bfloat16)
    cst["l1_maa_w1"] = np.ascontiguousarray(d["L1_attn_time_maa_w1"], np.float32)
    w2 = np.asarray(d["L1_attn_time_maa_w2"], np.float32)
    pk = np.zeros((128, D), np.float32)
    for f in range(4):
        pk[32 * f:32 * (f + 1), :] = w2[f]
    cst["l1_w2pk"] = pk
    cst["l1_w2p4"] = np.ascontiguousarray(w2[4], np.float32)
    cst["l1_dec_w1"] = np.ascontiguousarray(d["L1_attn_time_decay_w1"], np.float32)
    cst["l1_dec_w2"] = np.ascontiguousarray(d["L1_attn_time_decay_w2"], np.float32)
    mv = np.zeros((8, 128, 7), np.float32)
    for j, nm in enumerate(["time_maa_x", "time_maa_w", "time_maa_k",
                            "time_maa_v", "time_maa_r", "time_maa_g",
                            "time_decay"]):
        mv[:, :, j] = np.asarray(d["L1_attn_" + nm], np.float32).reshape(8, 128)
    cst["l1_maav"] = np.ascontiguousarray(mv.transpose(1, 0, 2).reshape(128, 56))
    for nm in ("r", "k", "v", "g", "o"):
        cst["l1_w" + nm] = np.ascontiguousarray(
            np.asarray(d["L1_attn_W_" + nm], np.float32).T)
    cst["l1_lnw"] = np.ascontiguousarray(np.broadcast_to(
        d["L1_attn_ln_w"], (128, D))).astype(ml_dtypes.bfloat16)
    cst["l1_lnb"] = np.ascontiguousarray(np.broadcast_to(
        d["L1_attn_ln_b"], (128, D))).astype(ml_dtypes.bfloat16)
    msk = np.zeros((128, 128), np.float32)
    sl = np.tril(np.ones((64, 64), np.float32), -1)
    msk[0:64, 64:128] = sl
    msk[64:128, 64:128] = sl
    cst["l1_masks"] = msk.astype(np.uint8)
    cst["l1f_w1t"] = np.ascontiguousarray(np.asarray(d["L1_ffn_w1"], np.float32).T)
    cst["l1f_w2t"] = np.ascontiguousarray(np.asarray(d["L1_ffn_w2"], np.float32).T)
    cst["l1f_lnw"] = np.ascontiguousarray(np.broadcast_to(
        d["L1_ffn_ln_g"], (128, D))).astype(ml_dtypes.bfloat16)
    cst["l1f_lnb"] = np.ascontiguousarray(np.broadcast_to(
        d["L1_ffn_ln_b"], (128, D))).astype(ml_dtypes.bfloat16)
    return cst


def kernel(**inputs):
    x = np.ascontiguousarray(np.asarray(inputs["L0_attn_x"], dtype=np.float32))
    B = x.shape[0]
    cst = host_prep(inputs)
    nc = build_nc()
    in_maps = [{"x": x[c], **cst} for c in range(B)]
    res = run_bass_kernel_spmd(nc, in_maps, core_ids=list(range(B)))
    return np.stack([r["y"] for r in res.results], axis=0)
